# version 11
# baseline (speedup 1.0000x reference)
import contextlib
import numpy as np
import concourse.bass as bass
import concourse.mybir as mybir
from concourse.bass_utils import run_bass_kernel_spmd

F32 = mybir.dt.float32
F32R = mybir.dt.float32r
BF16 = mybir.dt.bfloat16
AF = mybir.ActivationFunctionType
ALU = mybir.AluOpType
AX = mybir.AxisListType

B, S, D = 4, 4096, 1024
T = 2048
NT = 4
SEG = 512
PMEM = 16
HEADS = 16
DH = 64
NCH = 8
EPS1 = 1e-5
EPS2 = 1.1920929e-07
NEG = -1.0e30
BASE_C = 0.1

ENGS = ("pe", "act", "dve", "sp", "gp")
QS = ("qsp", "qgp")


class Bf:
    __slots__ = ("w", "r")

    def __init__(self):
        self.w = None
        self.r = []


class K:

    def __init__(self, nc, sems):
        self.nc = nc
        self.sems = sems
        self.h = {"pe": nc.tensor, "act": nc.scalar, "dve": nc.vector,
                  "sp": nc.sync, "gp": nc.gpsimd}
        self.cnt = {k: 0 for k in ENGS + QS}
        self.cnt["cc"] = 0
        self.seen = {e: {} for e in ENGS}
        self.streams = {e: [] for e in ENGS}
        self.marks = []
        self._scope = "init"

    @property
    def cur_scope(self):
        return self._scope

    @cur_scope.setter
    def cur_scope(self, v):
        if v != self._scope:
            self.marks.append((v, dict(self.cnt)))
        self._scope = v

    def _waitlist(self, e, deps):
        best = {}
        for (kk, v) in deps:
            if kk == e and e == "pe":
                continue
            if best.get(kk, 0) < v:
                best[kk] = v
        out = []
        for kk, v in best.items():
            if kk == e:
                if self.seen[e].get(e, 0) >= v:
                    continue
                out.append((e, v))
                self.seen[e][e] = v
                continue
            if self.seen[e].get(kk, 0) >= v:
                continue
            out.append((kk, v))
            self.seen[e][kk] = v
        return out

    @staticmethod
    def _deps(reads, writes):
        deps = []
        for b in reads:
            if b.w:
                deps.append(b.w)
        for b in writes:
            if b.w:
                deps.append(b.w)
            deps.extend(b.r)
        return deps

    def emit(self, e, fn, reads=(), writes=()):
        waits = self._waitlist(e, self._deps(reads, writes))
        self.cnt[e] += 1
        self.streams[e].append((waits, fn, (e, 1), self.cur_scope))
        evt = (e, self.cnt[e])
        for b in reads:
            b.r.append(evt)
        for b in writes:
            b.w = evt
            b.r = []
        return evt

    def dma(self, q, out, in_, reads=(), writes=()):
        e = "sp" if q == "qsp" else "gp"
        waits = self._waitlist(e, self._deps(reads, writes))
        self.cnt[q] += 16
        self.streams[e].append(
            (waits, lambda h, out=out, in_=in_: h.dma_start(out=out, in_=in_),
             (q, 16), self.cur_scope))
        evt = (q, self.cnt[q])
        for b in reads:
            b.r.append(evt)
        for b in writes:
            b.w = evt
            b.r = []
        return evt

    def barrier(self):
        for e in ENGS:
            waits = []
            for kk in ENGS:
                if kk != e and self.seen[e].get(kk, 0) < self.cnt[kk]:
                    waits.append((kk, self.cnt[kk]))
                    self.seen[e][kk] = self.cnt[kk]
            for q in QS:
                if self.cnt[q] and self.seen[e].get(q, 0) < self.cnt[q]:
                    waits.append((q, self.cnt[q]))
                    self.seen[e][q] = self.cnt[q]
            self.streams[e].append((waits, None, None, self.cur_scope))

    def flush(self, block):
        secs = {"pe": block.tensor, "act": block.scalar, "dve": block.vector,
                "sp": block.sync, "gp": block.gpsimd}
        for e in ENGS:
            ops = self.streams[e]
            if not ops:
                continue

            def body(h, ops=ops):
                import contextlib as _ctx
                stck = _ctx.ExitStack()
                open_scope = None
                for waits, fn, inc, scope in ops:
                    if scope != open_scope:
                        stck.close()
                        stck = _ctx.ExitStack()
                        stck.enter_context(self.nc.named_scope(scope))
                        open_scope = scope
                    for (kk, v) in waits:
                        h.wait_ge(self.sems[kk], v)
                    if fn is not None:
                        last = fn(h)
                        if inc is not None:
                            last.then_inc(self.sems[inc[0]], inc[1])
                stck.close()
            secs[e](body)
            self.streams[e] = []


def ln_rows(k, c, pbS, pbX, t, combine_g):
    rows, b_rows = c["rows"]
    r0 = rows[:, 0:512]; r1 = rows[:, 512:1024]
    r2 = rows[:, 1024:1536]; r3 = rows[:, 1536:2048]
    k.emit("dve", lambda v: v.tensor_scalar(r0, pbS[0][0:1, :], 1.0 / D, None,
                                            op0=ALU.mult),
           reads=[pbS[1]], writes=[b_rows])
    k.emit("dve", lambda v: v.tensor_scalar(r1, pbX[0][0:1, :], 1.0 / D, None,
                                            op0=ALU.mult),
           reads=[pbX[1]], writes=[b_rows])
    k.emit("dve", lambda v: v.tensor_tensor(r2, r0, r0, op=ALU.mult),
           reads=[b_rows], writes=[b_rows])
    k.emit("dve", lambda v: v.tensor_tensor(r1, r1, r2, op=ALU.subtract),
           reads=[b_rows], writes=[b_rows])
    if combine_g:
        k.emit("act", lambda a: a.activation(out=r2, in_=r1, func=AF.Sqrt,
                                             bias=k.epsB_t[:, :], scale=1.0 + EPS2),
               reads=[b_rows, k.epsB_b], writes=[b_rows])
        scale = 1.0
    else:
        k.emit("act", lambda a: a.activation(out=r2, in_=r1, func=AF.Sqrt,
                                             bias=k.epsA_t[:, :], scale=1.0),
               reads=[b_rows, k.epsA_b], writes=[b_rows])
        scale = 0.5
    k.emit("dve", lambda v: v.reciprocal(r3, r2), reads=[b_rows], writes=[b_rows])
    k.emit("dve", lambda v: v.tensor_scalar(r1, r3, scale, None, op0=ALU.mult),
           reads=[b_rows], writes=[b_rows])
    k.emit("dve", lambda v: v.tensor_tensor(r2, r0, r1, op=ALU.mult),
           reads=[b_rows], writes=[b_rows])
    k.emit("dve", lambda v: v.tensor_copy(c["rh_r"][t][0][:, :], r1),
           reads=[b_rows], writes=[c["rh_r"][t][1]])
    k.emit("dve", lambda v: v.tensor_copy(c["nu_r"][t][0][:, :], r2),
           reads=[b_rows], writes=[c["nu_r"][t][1]])


def ln_stats(k, xio, b_xio, xbf, b_xbf, sqbf, b_sq, pbS, pbX):
    k.emit("gp", lambda gp: gp.tensor_copy(xbf[:, :], xio[:, :]),
           reads=[b_xio], writes=[b_xbf])
    k.emit("act", lambda a: a.activation(out=sqbf[:, :], in_=xio[:, :], func=AF.Square),
           reads=[b_xio], writes=[b_sq])

    def mk(src, bank):
        def mm(p):
            last = None
            for ch in range(NCH):
                last = p.matmul(bank[0][0:1, :], k.onescol_t[:, :],
                                src[:, 512 * ch:512 * (ch + 1)],
                                start=(ch == 0), stop=(ch == NCH - 1))
            return last
        return mm
    k.emit("pe", mk(xbf, pbS), reads=[b_xbf, k.onescol_b], writes=[pbS[1]])
    k.emit("pe", mk(sqbf, pbX), reads=[b_sq, k.onescol_b], writes=[pbX[1]])


def ln_bcast(k, c, t, pbA, pbB):
    k.emit("pe", lambda p: p.matmul(pbA[0][:, :], k.onesr_t[:, :],
                                    c["rh_r"][t][0][:, :], start=True, stop=True),
           reads=[c["rh_r"][t][1], k.onesr_b], writes=[pbA[1]])
    k.emit("pe", lambda p: p.matmul(pbB[0][:, :], k.onesr_t[:, :],
                                    c["nu_r"][t][0][:, :], start=True, stop=True),
           reads=[c["nu_r"][t][1], k.onesr_b], writes=[pbB[1]])


def ln_apply(k, c, xio, b_xio, dst, b_dst, pbA, pbB):
    tmp, b_tmp = c["tmp"]
    for ch in range(NCH):
        sl = slice(512 * ch, 512 * (ch + 1))
        ts = tmp[:, 512 * (ch % 2):512 * (ch % 2 + 1)]
        k.emit("dve", lambda v, sl=sl, ts=ts: v.tensor_tensor(ts, xio[:, sl],
                                                              pbA[0][:, :], op=ALU.mult),
               reads=[b_xio, pbA[1]], writes=[b_tmp])
        k.emit("dve", lambda v, sl=sl, ts=ts: v.tensor_tensor(dst[:, sl], ts,
                                                              pbB[0][:, :], op=ALU.subtract),
               reads=[b_tmp, pbB[1]], writes=[b_dst])


def ffn_phase_open(nc, k, alloc, st, tag):
    c = {}

    def A(name, shape, dt):
        t = alloc(st, tag + name, shape, dt)
        b = Bf()
        c[name] = (t, b)
        return t
    for nm, shp, dt in [
        ("w1s", [128, 8 * 1024], BF16), ("w2s", [128, 8 * 512], BF16),
        ("w3s", [128, 4 * 256], BF16), ("w4s", [128, 2], BF16),
        ("b1c", [128, 8], F32), ("b2c", [128, 4], F32), ("b3c", [128, 2], F32),
        ("b4c", [1, 1], F32), ("gb1c", [128, 4], F32), ("gb2c", [128, 2], F32),
        ("gb3c", [1, 1], F32), ("arr", [1, 1], F32),
        ("tmp", [128, 2048], F32),
        ("gcacc", [128, 8], F32), ("gcs", [128, 32], F32), ("gcr", [128, 32], F32),
        ("gcsum", [128, 8], F32), ("gcbf", [128, 8], BF16),
        ("g1h", [128, 4], BF16), ("g2h", [128, 2], BF16),
        ("gfr", [1, 1], F32), ("c0r", [1, 1], F32), ("c1r", [1, 1], F32),
        ("rows", [1, 2048], F32),
    ]:
        A(nm, shp, dt)
    c["states"] = [(alloc(st, tag + f"stt{i}", [128, 4096], F32), Bf())
                   for i in range(NT)]
    c["sbf"] = [(alloc(st, tag + f"sbf{i}", [128, 4096], BF16), Bf()) for i in range(2)]
    c["h1"] = [(alloc(st, tag + f"h1_{i}", [128, 4096], BF16), Bf()) for i in range(2)]
    c["h2"] = [(alloc(st, tag + f"h2_{i}", [128, 2048], BF16), Bf()) for i in range(2)]
    c["h3"] = [(alloc(st, tag + "h3_0", [128, 1024], BF16), Bf())] * 2
    c["xio"] = [(alloc(st, tag + f"xio{i}", [128, 4096], F32), Bf()) for i in range(2)]
    c["stage"] = c["xio"][1]
    c["xbf"] = c["sbf"][0]
    c["sqbf"] = c["sbf"][1]
    xv = c["xio"][0][0][:, :].bitcast(BF16)
    c["g1s"] = (xv[:, 0:4096], c["xio"][0][1])
    c["g2s"] = (xv[:, 4096:5122], c["xio"][0][1])
    c["g3s"] = (xv[:, 5122:5124], c["xio"][0][1])
    c["rh_r"] = [(alloc(st, tag + f"rh{i}", [1, 512], F32R), Bf()) for i in range(NT)]
    c["nu_r"] = [(alloc(st, tag + f"nu{i}", [1, 512], F32R), Bf()) for i in range(NT)]
    c["crow_r"] = [(alloc(st, tag + f"cr{i}", [1, 512], F32R), Bf()) for i in range(2)]
    c["tfrow"] = [(alloc(st, tag + f"tf{i}", [1, 512], F32), Bf()) for i in range(2)]
    return c


def ffn_load(nc, k, g, c):
    k.cur_scope = "wload"
    for nm, src, ncols in [("w1s", g["w1"], 1024), ("w2s", g["w2"], 512),
                           ("w3s", g["w3"], 256), ("w4s", g["w4"], 1)]:
        t, bf = c[nm]
        nkc = src.shape[0] // 128
        for kc in range(nkc):
            k.dma("qgp", t[:, ncols * kc:ncols * (kc + 1)],
                  src[128 * kc:128 * (kc + 1), :], writes=[bf])
    for nm, src in [("b1c", g["b1"]), ("b2c", g["b2"]), ("b3c", g["b3"]),
                    ("b4c", g["b4"]), ("gb1c", g["gb1"]), ("gb2c", g["gb2"]),
                    ("gb3c", g["gb3"]), ("arr", g["ar"])]:
        t, bf = c[nm]
        k.dma("qsp", t[:, :], src, writes=[bf])
    k.emit("dve", lambda v: v.memset(c["gcacc"][0][:, :], 0.0), writes=[c["gcacc"][1]])


def ffn_load_gc(nc, k, g, c):
    k.cur_scope = "wload"
    for nm, src, ncols in [("g1s", g["gw1"], 512), ("g2s", g["gw2"], 256),
                           ("g3s", g["gw3"], 1)]:
        t, bf = c[nm]
        nkc = src.shape[0] // 128
        for kc in range(nkc):
            k.dma("qgp", t[:, ncols * kc:ncols * (kc + 1)],
                  src[128 * kc:128 * (kc + 1), :], writes=[bf])


def ffn_prepass(nc, k, g, c, src):
    k.cur_scope = "prepass"
    pbt, pb = k.pbanks, k.pbanks_bf
    gcacc, b_gcacc = c["gcacc"]
    xbf, b_xbf = c["xbf"]
    sqbf, b_sq = c["sqbf"]

    def banks(t):
        return ((pbt[5], pb[5]), (pbt[6], pb[6])) if t % 2 == 0 else \
               ((pbt[2], pb[2]), (pbt[3], pb[3]))

    def pstats(t):
        xio, b_xio = c["xio"][t % 2]
        for ch in range(NCH):
            k.dma("qsp", xio[:, 512 * ch:512 * (ch + 1)],
                  src[128 * ch:128 * (ch + 1), 512 * t:512 * (t + 1)],
                  writes=[b_xio])
        pbS, pbX = banks(t)
        ln_stats(k, xio, b_xio, xbf, b_xbf, sqbf, b_sq, pbS, pbX)

    def ptail(t):
        pbS, pbX = banks(t)
        ln_rows(k, c, pbS, pbX, t, combine_g=False)
        ln_bcast(k, c, t, pbS, pbX)
        xio, b_xio = c["xio"][t % 2]
        state_t, b_state = c["states"][t]
        ln_apply(k, c, xio, b_xio, state_t, b_state, pbS, pbX)
        gcs, b_gcs = c["gcs"]
        k.emit("dve", lambda v, state_t=state_t: v.reduce_sum(
            gcs[:, 0:8], state_t[:, :].rearrange("p (c t) -> p c t", c=8), axis=AX.X),
               reads=[b_state], writes=[b_gcs])
        k.emit("dve", lambda v: v.tensor_tensor(gcacc[:, :], gcacc[:, :], gcs[:, 0:8],
                                                op=ALU.add),
               reads=[b_gcs, b_gcacc], writes=[b_gcacc])

    pstats(0)
    pstats(1)
    ptail(0)
    pstats(2)
    ptail(1)
    pstats(3)
    ptail(2)
    ptail(3)


def ffn_cc_stage(nc, k, g, c, ccb):
    gcacc, b_gcacc = c["gcacc"]
    gcs, b_gcs = c["gcs"]
    cin, _ = ccb
    cin3 = cin.ap().rearrange("s (p c) -> s p c", p=128)
    for s in range(4):
        k.emit("dve", lambda v, s=s: v.tensor_scalar(
            gcs[:, 8 * s:8 * (s + 1)], gcacc[:, :], k.ccm_t[:, s:s + 1], None,
            op0=ALU.mult),
               reads=[b_gcacc, k.ccm_b], writes=[b_gcs])
    for s in range(4):
        k.dma("qsp", cin3[s], gcs[:, 8 * s:8 * (s + 1)], reads=[b_gcs])


CC_ENABLE = True


def cc_run(nc, k, c, ccb):
    cin, cout = ccb
    v = k.cnt["qsp"]
    if k.seen["gp"].get("qsp", 0) < v:
        nc.gpsimd.wait_ge(k.sems["qsp"], v)
        k.seen["gp"]["qsp"] = v
    if CC_ENABLE:
        nc.gpsimd.collective_compute(
            "AllReduce", ALU.add,
            replica_groups=[[0, 1, 2, 3, 4, 5, 6, 7]],
            ins=[cin.ap().opt()], outs=[cout.ap().opt()],
        ).then_inc(k.sems["cc"])
        k.cnt["cc"] += 1
        nc.gpsimd.wait_ge(k.sems["cc"], k.cnt["cc"])
        rdbuf = cout
    else:
        rdbuf = cin
    gcr, b_gcr = c["gcr"]
    cout3 = rdbuf.ap().rearrange("s (p c) -> s p c", p=128)
    for s in range(4):
        nc.gpsimd.dma_start(out=gcr[:, 8 * s:8 * (s + 1)], in_=cout3[s]).then_inc(
            k.sems["qgp"], 16)
        k.cnt["qgp"] += 16
    b_gcr.w = ("qgp", k.cnt["qgp"])
    b_gcr.r = []


def ffn_gf(nc, k, g, c):
    k.cur_scope = "gf"
    pbt, pb = k.pbanks, k.pbanks_bf
    gcr, b_gcr = c["gcr"]
    gcsum, b_gcsum = c["gcsum"]
    gcbf, b_gcbf = c["gcbf"]
    k.emit("dve", lambda v: v.memset(gcsum[:, :], 0.0), writes=[b_gcsum])
    for s in range(4):
        k.emit("dve", lambda v, s=s: v.scalar_tensor_tensor(
            gcsum[:, :], gcr[:, 8 * s:8 * (s + 1)], k.ccm_t[:, s:s + 1], gcsum[:, :],
            op0=ALU.mult, op1=ALU.add),
               reads=[b_gcr, k.ccm_b, b_gcsum], writes=[b_gcsum])
    k.emit("dve", lambda v: v.tensor_scalar(gcbf[:, :], gcsum[:, :], 1.0 / 2048.0,
                                            None, op0=ALU.mult),
           reads=[b_gcsum], writes=[b_gcbf])
    g1s, b_g1s = c["g1s"]; g2s, b_g2s = c["g2s"]; g3s, b_g3s = c["g3s"]
    g1h, b_g1h = c["g1h"]; g2h, b_g2h = c["g2h"]
    gb1c, b_gb1 = c["gb1c"]; gb2c, b_gb2 = c["gb2c"]; gb3c, b_gb3 = c["gb3c"]
    for oc in range(4):
        def mm(p, oc=oc):
            last = None
            for kc in range(8):
                last = p.matmul(pbt[6][:, 0:1],
                                g1s[:, 512 * kc + 128 * oc:512 * kc + 128 * oc + 128],
                                gcbf[:, kc:kc + 1], start=(kc == 0), stop=(kc == 7))
            return last
        k.emit("pe", mm, reads=[b_g1s, b_gcbf], writes=[pb[6]])
        k.emit("act", lambda a, oc=oc: a.activation(
            out=g1h[:, oc:oc + 1], in_=pbt[6][:, 0:1], func=AF.Gelu,
            bias=gb1c[:, oc:oc + 1]),
               reads=[pb[6], b_gb1], writes=[b_g1h])
    for oc in range(2):
        def mm(p, oc=oc):
            last = None
            for kc in range(4):
                last = p.matmul(pbt[6][:, 0:1],
                                g2s[:, 256 * kc + 128 * oc:256 * kc + 128 * oc + 128],
                                g1h[:, kc:kc + 1], start=(kc == 0), stop=(kc == 3))
            return last
        k.emit("pe", mm, reads=[b_g2s, b_g1h], writes=[pb[6]])
        k.emit("act", lambda a, oc=oc: a.activation(
            out=g2h[:, oc:oc + 1], in_=pbt[6][:, 0:1], func=AF.Gelu,
            bias=gb2c[:, oc:oc + 1]),
               reads=[pb[6], b_gb2], writes=[b_g2h])

    def mm3(p):
        last = None
        for kc in range(2):
            last = p.matmul(pbt[6][0:1, 0:1], g3s[:, kc:kc + 1], g2h[:, kc:kc + 1],
                            start=(kc == 0), stop=(kc == 1))
        return last
    k.emit("pe", mm3, reads=[b_g3s, b_g2h], writes=[pb[6]])
    gfr, b_gfr = c["gfr"]
    k.emit("act", lambda a: a.activation(out=gfr[:, :], in_=pbt[6][0:1, 0:1],
                                         func=AF.Sigmoid, bias=gb3c[:, :]),
           reads=[pb[6], b_gb3], writes=[b_gfr])
    arr, b_arr = c["arr"]
    c0r, b_c0 = c["c0r"]; c1r, b_c1 = c["c1r"]
    rows, b_rows = c["rows"]
    tt = rows[:, 1536:1537]
    k.emit("dve", lambda v: v.tensor_scalar(c1r[:, :], arr[:, :], 0.6, None,
                                            op0=ALU.mult),
           reads=[b_arr], writes=[b_c1])
    k.emit("dve", lambda v: v.tensor_scalar(tt, gfr[:, :], 1.4, -1.0,
                                            op0=ALU.mult, op1=ALU.add),
           reads=[b_gfr], writes=[b_rows])
    k.emit("dve", lambda v: v.tensor_tensor(c0r[:, :], tt, arr[:, :], op=ALU.mult),
           reads=[b_rows, b_arr], writes=[b_c0])
    k.emit("dve", lambda v: v.tensor_scalar(c0r[:, :], c0r[:, :], 1.0, BASE_C,
                                            op0=ALU.mult, op1=ALU.add),
           reads=[b_c0], writes=[b_c0])


def ffn_cast(nc, k, c, t):
    k.cur_scope = "fmain"
    sbf, b_sbf = c["sbf"][t % 2]
    state_t, b_state = c["states"][t]
    k.emit("dve", lambda v: v.tensor_scalar(sbf[:, :], state_t[:, :], 2.0, None,
                                            op0=ALU.mult),
           reads=[b_state], writes=[b_sbf])


def ffn_wchain(nc, k, c, t):
    pbt, pb = k.pbanks, k.pbanks_bf
    sbf, b_sbf = c["sbf"][t % 2]
    h1, b_h1 = c["h1"][t % 2]; h2, b_h2 = c["h2"][t % 2]; h3, b_h3 = c["h3"][t % 2]
    w1s, b_w1 = c["w1s"]; w2s, b_w2 = c["w2s"]; w3s, b_w3 = c["w3s"]; w4s, b_w4 = c["w4s"]
    b1c, bb1 = c["b1c"]; b2c, bb2 = c["b2c"]; b3c, bb3 = c["b3c"]; b4c, bb4 = c["b4c"]
    for oc in range(8):
        bi = k.rot[0]; k.rot[0] = (k.rot[0] + 1) % 5

        def mm(p, oc=oc, bi=bi):
            last = None
            for kc in range(8):
                last = p.matmul(pbt[bi][:, :],
                                w1s[:, 1024 * kc + 128 * oc:1024 * kc + 128 * oc + 128],
                                sbf[:, 512 * kc:512 * (kc + 1)],
                                start=(kc == 0), stop=(kc == 7))
            return last
        k.emit("pe", mm, reads=[b_w1, b_sbf], writes=[pb[bi]])
        k.emit("act", lambda a, oc=oc, bi=bi: a.activation(
            out=h1[:, 512 * oc:512 * (oc + 1)], in_=pbt[bi][:, :], func=AF.Gelu,
            bias=b1c[:, oc:oc + 1]),
               reads=[pb[bi], bb1], writes=[b_h1])
    for oc in range(4):
        bi = k.rot[0]; k.rot[0] = (k.rot[0] + 1) % 5

        def mm(p, oc=oc, bi=bi):
            last = None
            for kc in range(8):
                last = p.matmul(pbt[bi][:, :],
                                w2s[:, 512 * kc + 128 * oc:512 * kc + 128 * oc + 128],
                                h1[:, 512 * kc:512 * (kc + 1)],
                                start=(kc == 0), stop=(kc == 7))
            return last
        k.emit("pe", mm, reads=[b_w2, b_h1], writes=[pb[bi]])
        k.emit("act", lambda a, oc=oc, bi=bi: a.activation(
            out=h2[:, 512 * oc:512 * (oc + 1)], in_=pbt[bi][:, :], func=AF.Gelu,
            bias=b2c[:, oc:oc + 1]),
               reads=[pb[bi], bb2], writes=[b_h2])
    for oc in range(2):
        bi = k.rot[0]; k.rot[0] = (k.rot[0] + 1) % 5

        def mm(p, oc=oc, bi=bi):
            last = None
            for kc in range(4):
                last = p.matmul(pbt[bi][:, :],
                                w3s[:, 256 * kc + 128 * oc:256 * kc + 128 * oc + 128],
                                h2[:, 512 * kc:512 * (kc + 1)],
                                start=(kc == 0), stop=(kc == 3))
            return last
        k.emit("pe", mm, reads=[b_w3, b_h2], writes=[pb[bi]])
        k.emit("act", lambda a, oc=oc, bi=bi: a.activation(
            out=h3[:, 512 * oc:512 * (oc + 1)], in_=pbt[bi][:, :], func=AF.Gelu,
            bias=b3c[:, oc:oc + 1]),
               reads=[pb[bi], bb3], writes=[b_h3])

    def mm4(p):
        last = None
        for kc in range(2):
            last = p.matmul(pbt[6][0:1, :], w4s[:, kc:kc + 1],
                            h3[:, 512 * kc:512 * (kc + 1)],
                            start=(kc == 0), stop=(kc == 1))
        return last
    k.emit("pe", mm4, reads=[b_w4, b_h3], writes=[pb[6]])
    tfrow, b_tf = c["tfrow"][t % 2]
    k.emit("act", lambda a: a.activation(out=tfrow[:, :], in_=pbt[6][0:1, :],
                                         func=AF.Sigmoid, bias=b4c[:, :]),
           reads=[pb[6], bb4], writes=[b_tf])


TWIST_GROUPS = [
    [(0, 4, ALU.add), (1, 5, ALU.add), (4, 0, ALU.subtract), (5, 1, ALU.subtract)],
    [(2, 6, ALU.subtract), (3, 7, ALU.subtract), (6, 2, ALU.add), (7, 3, ALU.add)],
]


def ffn_update(nc, k, c, t, cyc):
    pbt, pb = k.pbanks, k.pbanks_bf
    state_t, b_state = c["states"][t]
    tmp, b_tmp = c["tmp"]
    tfrow, b_tf = c["tfrow"][t % 2]
    crow, b_crow = c["crow_r"][t % 2]
    c0r, b_c0 = c["c0r"]; c1r, b_c1 = c["c1r"]
    rows, b_rows = c["rows"]
    cr = rows[:, 1536:2048]
    k.emit("dve", lambda v: v.tensor_scalar(cr, tfrow[:, :], c1r[:, :], c0r[:, :],
                                            op0=ALU.mult, op1=ALU.add),
           reads=[b_tf, b_c0, b_c1], writes=[b_rows])
    k.emit("dve", lambda v: v.tensor_copy(crow[:, :], cr),
           reads=[b_rows], writes=[b_crow])
    k.emit("pe", lambda p: p.matmul(pbt[5][:, :], k.onesr_t[:, :], crow[:, :],
                                    start=True, stop=True),
           reads=[b_crow, k.onesr_b], writes=[pb[5]])
    for grp in TWIST_GROUPS:
        for i, (dst, src, op) in enumerate(grp):
            k.emit("dve", lambda v, i=i, src=src: v.tensor_tensor(
                tmp[:, 512 * i:512 * (i + 1)], state_t[:, 512 * src:512 * (src + 1)],
                pbt[5][:, :], op=ALU.mult),
                   reads=[b_state, pb[5]], writes=[b_tmp])
        for i, (dst, src, op) in enumerate(grp):
            k.emit("dve", lambda v, i=i, dst=dst, op=op: v.tensor_tensor(
                state_t[:, 512 * dst:512 * (dst + 1)],
                state_t[:, 512 * dst:512 * (dst + 1)],
                tmp[:, 512 * i:512 * (i + 1)], op=op),
                   reads=[b_tmp, b_state], writes=[b_state])


def ffn_resid(nc, k, g, c, t, src, dst):
    k.cur_scope = "fresid"
    xio, b_xio = c["xio"][t % 2]
    state_t, b_state = c["states"][t]
    for ch in range(NCH):
        k.dma("qsp", xio[:, 512 * ch:512 * (ch + 1)],
              src[128 * ch:128 * (ch + 1), 512 * t:512 * (t + 1)],
              writes=[b_xio])
    k.emit("dve", lambda v: v.tensor_tensor(
        xio[:, :], xio[:, :], state_t[:, :], op=ALU.add),
           reads=[b_state, b_xio], writes=[b_xio])
    for ch in range(NCH):
        k.dma("qgp", dst[128 * ch:128 * (ch + 1), 512 * t:512 * (t + 1)],
              xio[:, 512 * ch:512 * (ch + 1)], reads=[b_xio])


def ffn_main(nc, k, g, c, src, dst, cycles=(1, 2, 3), resid=True, gf_fn=None):
    for cyc in cycles:
        ffn_cast(nc, k, c, 0)
        ffn_cast(nc, k, c, 1)
        ffn_wchain(nc, k, c, 0)
        ffn_cast(nc, k, c, 2)
        ffn_wchain(nc, k, c, 1)
        if gf_fn is not None:
            gf_fn()
            gf_fn = None
        ffn_update(nc, k, c, 0, cyc)
        ffn_cast(nc, k, c, 3)
        ffn_wchain(nc, k, c, 2)
        ffn_update(nc, k, c, 1, cyc)
        ffn_wchain(nc, k, c, 3)
        ffn_update(nc, k, c, 2, cyc)
        ffn_update(nc, k, c, 3, cyc)
        if cyc == 3 and resid:
            for t in range(NT):
                ffn_resid(nc, k, g, c, t, src, dst)


def attn_open(nc, k, alloc, st):
    c = {}

    def A(name, shape, dt):
        t = alloc(st, "at_" + name, shape, dt)
        b = Bf()
        c[name] = (t, b)
        return t
    for nm, shp, dt in [
        ("qkvw", [128, 8 * 3072], BF16), ("wow", [128, 8 * 1024], BF16),
        ("coskT", [128, 2048], BF16), ("sskT", [128, 2048], BF16),
        ("tri1T", [128, 128], F32), ("tri2T", [128, 128], F32),
        ("identT", [128, 128], BF16), ("permT", [128, 128], BF16),
        ("pmkT", [128, 128], BF16), ("pmvT", [16, 1024], BF16),
        ("ybf", [128, 4096], BF16),
        ("vtm", [128, 4 * 1024], BF16),
        ("ofm", [128, 8 * 512], BF16),
        ("burn", [128, 512], BF16), ("burn2", [128, 16], BF16),
        ("rows", [1, 2048], F32),
    ]:
        A(nm, shp, dt)
    c["xs"] = [(alloc(st, f"at_xs{i}", [128, 4096], F32), Bf()) for i in range(2)]
    c["sqbf2"] = (alloc(st, "at_sqbf2", [128, 4096], BF16), Bf())
    c["qaug"] = alloc(st, "at_qaug", [128, 16 * 512], BF16)
    c["kaug"] = alloc(st, "at_kaug", [128, 16 * 528], BF16)
    c["qaug_d"] = [Bf() for _ in range(16)]
    c["qaug_s"] = [Bf() for _ in range(16)]
    c["kaug_d"] = [Bf() for _ in range(16)]
    c["kaug_p"] = [Bf() for _ in range(16)]
    c["attnT"] = [(alloc(st, f"at_at{i}", [128, 512], BF16), Bf()) for i in range(3)]
    c["atpm"] = [(alloc(st, f"at_atpm{i}", [16, 512], BF16), Bf()) for i in range(2)]
    c["qsb"] = [(alloc(st, f"at_qsb{i}", [128, 512], BF16), Bf()) for i in range(2)]
    c["t1f"] = [(alloc(st, f"at_t1f{i}", [128, 512], F32), Bf()) for i in range(2)]
    c["t2f"] = [(alloc(st, f"at_t2f{i}", [128, 512], F32), Bf()) for i in range(2)]
    for nm in ("mnegA", "dsegA", "dpmA", "shfA"):
        c[nm] = (alloc(st, "at_" + nm, [128, 64], F32), [Bf() for _ in range(4)])
    c["hlsA"] = (alloc(st, "at_hlsA", [128, 128], BF16), [Bf() for _ in range(4)])
    c["rh_r"] = [(alloc(st, "at_rh", [1, 512], F32R), Bf())]
    c["nu_r"] = [(alloc(st, "at_nu", [1, 512], F32R), Bf())]
    c["growR"] = (alloc(st, "at_grow", [1, 1024], F32R), Bf())
    return c


def attn_load(nc, k, g, c):
    k.cur_scope = "aload"
    qkvw, b_qkvw = c["qkvw"]
    for kc in range(8):
        k.dma("qgp", qkvw[:, 3072 * kc:3072 * (kc + 1)],
              g["wqkv"][128 * kc:128 * (kc + 1), :], writes=[b_qkvw])
    wow, b_wow = c["wow"]
    for kc in range(8):
        k.dma("qgp", wow[:, 1024 * kc:1024 * (kc + 1)],
              g["wout"][128 * kc:128 * (kc + 1), :], writes=[b_wow])
    for nm, src in [("coskT", g["cosk"]), ("sskT", g["ssk"])]:
        t, bf = c[nm]
        k.dma("qgp", t[:, :], src, writes=[bf])
    for nm, src in [("tri1T", g["tri1"]), ("tri2T", g["tri2"])]:
        t, bf = c[nm]
        k.dma("qsp", t[:, :], src, writes=[bf])
    for nm, src in [("identT", g["ident"]), ("permT", g["perm"])]:
        t, bf = c[nm]
        k.dma("qgp", t[:, :], src, writes=[bf])
    pmkT, b_pmk = c["pmkT"]
    for ch in range(8):
        k.dma("qgp", pmkT[:, 16 * ch:16 * (ch + 1)],
              g["pmk"][128 * ch:128 * (ch + 1), :], writes=[b_pmk])
    pmvT, b_pmv = c["pmvT"]
    k.dma("qgp", pmvT[:, :], g["pmv"], writes=[b_pmv])
    growR, b_grow = c["growR"]
    k.dma("qsp", growR[:, :], g["gvec"].bitcast(F32R), writes=[b_grow])
    kaug = c["kaug"]
    k.emit("dve", lambda v: v.memset(kaug[64:66, :], 1.0), writes=c["kaug_p"])
    for h in range(16):
        hc, lo = h // 2, (h % 2) * 64
        k.emit("act", lambda a, h=h, hc=hc, lo=lo: a.activation(
            out=kaug[0:64, 528 * h + 512:528 * h + 528],
            in_=pmkT[lo:lo + 64, 16 * hc:16 * (hc + 1)], func=AF.Copy),
               reads=[b_pmk], writes=[c["kaug_p"][h]])


def attn_ln(nc, k, g, c, s):
    pbt, pb = k.pbanks, k.pbanks_bf
    k.cur_scope = "aln"
    xs, b_xs = c["xs"][s % 2]
    src = g["xbuf"]
    for ch in range(NCH):
        k.dma("qsp", xs[:, 512 * ch:512 * (ch + 1)],
              src[128 * ch:128 * (ch + 1), 512 * s:512 * (s + 1)],
              writes=[b_xs])
    ybf, b_ybf = c["ybf"]
    sqbf, b_sq = c["sqbf2"]
    ln_stats(k, xs, b_xs, ybf, b_ybf, sqbf, b_sq, (pbt[5], pb[5]), (pbt[6], pb[6]))
    ln_rows(k, c, (pbt[5], pb[5]), (pbt[6], pb[6]), 0, combine_g=True)


def attn_lnapply(nc, k, g, c, s):
    pbt, pb = k.pbanks, k.pbanks_bf
    k.cur_scope = "aln"
    xs, b_xs = c["xs"][s % 2]
    ybf, b_ybf = c["ybf"]
    growR, b_grow = c["growR"]
    rh_r, b_rh = c["rh_r"][0]
    nu_r, b_nu = c["nu_r"][0]
    for ch in range(NCH):
        bA, bB = 5, 6
        k.emit("pe", lambda p, ch=ch, bA=bA: p.matmul(
            pbt[bA][:, :], growR[:, 128 * ch:128 * (ch + 1)], rh_r[:, :],
            start=True, stop=True),
               reads=[b_grow, b_rh], writes=[pb[bA]])
        k.emit("pe", lambda p, ch=ch, bB=bB: p.matmul(
            pbt[bB][:, :], growR[:, 128 * ch:128 * (ch + 1)], nu_r[:, :],
            start=True, stop=True),
               reads=[b_grow, b_nu], writes=[pb[bB]])
        sl = slice(512 * ch, 512 * (ch + 1))
        t1f, b_t1 = c["t1f"][ch % 2]
        k.emit("dve", lambda v, sl=sl, bA=bA, t1f=t1f: v.tensor_tensor(
            t1f[:, :], xs[:, sl], pbt[bA][:, :], op=ALU.mult),
               reads=[b_xs, pb[bA]], writes=[b_t1])
        k.emit("dve", lambda v, sl=sl, bB=bB, t1f=t1f: v.tensor_tensor(
            ybf[:, sl], t1f[:, :], pbt[bB][:, :], op=ALU.subtract),
               reads=[b_t1, pb[bB]], writes=[b_ybf])


def attn_qkv(nc, k, g, c, s):
    pbt, pb = k.pbanks, k.pbanks_bf
    ybf, b_ybf = c["ybf"]
    qkvw, b_qkvw = c["qkvw"]
    cosT, b_cos = c["coskT"]
    ssT, b_ss = c["sskT"]
    permT, b_perm = c["permT"]
    tsl = slice(512 * s, 512 * (s + 1))

    def qk_mm(which, qscale, base, hc):
        bi = hc % 2

        def mm(p, hc=hc, bi=bi, base=base):
            last = None
            for kc in range(8):
                last = p.matmul(pbt[bi][:, :],
                                qkvw[:, 3072 * kc + base + 128 * hc:
                                     3072 * kc + base + 128 * hc + 128],
                                ybf[:, 512 * kc:512 * (kc + 1)],
                                start=(kc == 0), stop=(kc == 7))
            return last
        k.emit("pe", mm, reads=[b_qkvw, b_ybf], writes=[pb[bi]])
        qsb, b_qsb = c["qsb"][hc % 2]
        k.emit("act", lambda a, bi=bi, qsb=qsb, qscale=qscale: a.activation(
            out=qsb[:, :], in_=pbt[bi][:, :], func=AF.Copy, scale=qscale),
               reads=[pb[bi]], writes=[b_qsb])

    def qk_tail(which, hc):
        bi = hc % 2
        qsb, b_qsb = c["qsb"][hc % 2]
        k.emit("pe", lambda p, bi=bi, qsb=qsb: p.matmul(
            pbt[2 + bi][:, :], permT[:, :], qsb[:, :], start=True, stop=True),
               reads=[b_perm, b_qsb], writes=[pb[2 + bi]])
        t1x, b_t1x = c["t1f"][hc % 2]
        t2x, b_t2x = c["t2f"][hc % 2]
        k.emit("dve", lambda v, qsb=qsb, t1x=t1x: v.tensor_tensor(
            t1x[:, :], qsb[:, :], cosT[:, tsl], op=ALU.mult),
               reads=[b_qsb, b_cos], writes=[b_t1x])
        k.emit("dve", lambda v, bi=bi, t2x=t2x: v.tensor_tensor(
            t2x[:, :], pbt[2 + bi][:, :], ssT[:, tsl], op=ALU.mult),
               reads=[pb[2 + bi], b_ss], writes=[b_t2x])
        for half in range(2):
            h = 2 * hc + half
            if which == "q":
                rdst = c["qaug"][0:64, 512 * h:512 * (h + 1)]
                wb = c["qaug_d"][h]
            else:
                rdst = c["kaug"][0:64, 528 * h:528 * h + 512]
                wb = c["kaug_d"][h]
            lo = 64 * half
            k.emit("dve", lambda v, rdst=rdst, lo=lo, t1x=t1x, t2x=t2x:
                   v.tensor_tensor(rdst, t1x[lo:lo + 64, :], t2x[lo:lo + 64, :],
                                   op=ALU.add),
                   reads=[b_t1x, b_t2x], writes=[wb])

    k.cur_scope = "aqkv"
    prev = None
    for which, qscale, base in (("q", 0.125, 0), ("k", 1.0, 1024)):
        for hc in range(8):
            qk_mm(which, qscale, base, hc)
            if prev is not None:
                qk_tail(*prev)
            prev = (which, hc)
    qk_tail(*prev)
    vtm, b_vtm = c["vtm"]
    for tc in range(4):
        for vh in range(2):
            bi = (tc * 2 + vh) % 2

            def mm(p, tc=tc, vh=vh, bi=bi):
                last = None
                for kc in range(8):
                    last = p.matmul(pbt[bi][:, :],
                                    ybf[:, 512 * kc + 128 * tc:512 * kc + 128 * tc + 128],
                                    qkvw[:, 3072 * kc + 2048 + 512 * vh:
                                         3072 * kc + 2048 + 512 * (vh + 1)],
                                    start=(kc == 0), stop=(kc == 7))
                return last
            k.emit("pe", mm, reads=[b_qkvw, b_ybf], writes=[pb[bi]])
            k.emit("act", lambda a, tc=tc, vh=vh, bi=bi: a.activation(
                out=vtm[:, 1024 * tc + 512 * vh:1024 * tc + 512 * (vh + 1)],
                in_=pbt[bi][:, :], func=AF.Copy),
                   reads=[pb[bi]], writes=[b_vtm])


def attn_heads(nc, k, g, c, s, dst):
    pbt, pb = k.pbanks, k.pbanks_bf
    xs, b_xs = c["xs"][s % 2]
    k.cur_scope = "aheads"
    attn_stats_stage(nc, k, c, 0)
    attn_stats_stage(nc, k, c, 1)
    for grp in range(4):
        attn_shift_stage(nc, k, c, grp)
        for h in range(4 * grp, 4 * grp + 4):
            attn_flip(nc, k, c, h)
        if grp + 2 < 4:
            attn_stats_stage(nc, k, c, grp + 2)
        for h in range(4 * grp, 4 * grp + 4):
            attn_pass2(nc, k, c, h)
    k.cur_scope = "awout"
    ofm, b_ofm = c["ofm"]
    wow, b_wow = c["wow"]
    for oc in range(8):
        bi = 2 + (oc % 2)

        def mm(p, oc=oc, bi=bi):
            last = None
            for kc in range(8):
                last = p.matmul(pbt[bi][:, :],
                                wow[:, 1024 * kc + 128 * oc:1024 * kc + 128 * oc + 128],
                                ofm[:, 512 * kc:512 * (kc + 1)],
                                start=(kc == 0), stop=(kc == 7))
            return last
        k.emit("pe", mm, reads=[b_wow, b_ofm], writes=[pb[bi]])
        sl = slice(512 * oc, 512 * (oc + 1))
        k.emit("dve", lambda v, sl=sl, bi=bi: v.tensor_tensor(
            xs[:, sl], xs[:, sl], pbt[bi][:, :], op=ALU.add),
               reads=[pb[bi], b_xs], writes=[b_xs])
    for ch in range(NCH):
        k.dma("qgp", dst[128 * ch:128 * (ch + 1), 512 * s:512 * (s + 1)],
              xs[:, 512 * ch:512 * (ch + 1)], reads=[b_xs])


def attn_stats_stage(nc, k, c, grp):
    pbt, pb = k.pbanks, k.pbanks_bf
    mnegA, b_mn = c["mnegA"][0], c["mnegA"][1][grp]
    dsegA, b_dsg = c["dsegA"][0], c["dsegA"][1][grp]
    dpmA, b_dpm = c["dpmA"][0], c["dpmA"][1][grp]
    burn, _ = c["burn"]
    burn2, _ = c["burn2"]
    qaug = c["qaug"]; kaug = c["kaug"]
    tri1T, b_tri1 = c["tri1T"]
    for h in range(4 * grp, 4 * grp + 4):
        b_qd = c["qaug_d"][h]; b_kd = c["kaug_d"][h]; b_kp = c["kaug_p"][h]
        for qc in range(4):
            col = 4 * h + qc
            N = 128 * (qc + 1)
            bi = 2 + (qc % 2)
            bp = 4 if qc % 2 == 0 else 0
            k.emit("pe", lambda p, h=h, qc=qc, N=N, bi=bi: p.matmul(
                pbt[bi][:, 0:N],
                qaug[0:64, 512 * h + 128 * qc:512 * h + 128 * (qc + 1)],
                kaug[0:64, 528 * h:528 * h + N], start=True, stop=True),
                   reads=[b_qd, b_kd], writes=[pb[bi]])
            k.emit("pe", lambda p, h=h, qc=qc, bp=bp: p.matmul(
                pbt[bp][:, 0:16],
                qaug[0:64, 512 * h + 128 * qc:512 * h + 128 * (qc + 1)],
                kaug[0:64, 528 * h + 512:528 * h + 528], start=True, stop=True),
                   reads=[b_qd, b_kp], writes=[pb[bp]])
            k.emit("dve", lambda v, qc=qc, bi=bi: v.tensor_tensor(
                pbt[bi][:, 128 * qc:128 * (qc + 1)], pbt[bi][:, 128 * qc:128 * (qc + 1)],
                tri1T[:, :], op=ALU.add),
                   reads=[b_tri1, pb[bi]], writes=[pb[bi]])
            k.emit("dve", lambda v, col=col, N=N, bi=bi: v.reduce_max(
                mnegA[:, col:col + 1], pbt[bi][:, 0:N], axis=AX.X, negate=True),
                   reads=[pb[bi]], writes=[b_mn])
            k.emit("act", lambda a, col=col, N=N, bi=bi: a.activation(
                out=burn[:, 0:N], in_=pbt[bi][:, 0:N], func=AF.Exp,
                bias=mnegA[:, col:col + 1], accum_out=dsegA[:, col:col + 1]),
                   reads=[pb[bi], b_mn], writes=[b_dsg])
            k.emit("act", lambda a, col=col, bp=bp: a.activation(
                out=burn2[:, :], in_=pbt[bp][:, 0:16], func=AF.Exp,
                bias=mnegA[:, col:col + 1], accum_out=dpmA[:, col:col + 1]),
                   reads=[pb[bp], b_mn], writes=[b_dpm])


def attn_shift_stage(nc, k, c, grp):
    sl = slice(16 * grp, 16 * grp + 16)
    mnegA, b_mn = c["mnegA"][0], c["mnegA"][1][grp]
    dsegA, b_dsg = c["dsegA"][0], c["dsegA"][1][grp]
    dpmA, b_dpm = c["dpmA"][0], c["dpmA"][1][grp]
    shfA, b_shf = c["shfA"][0], c["shfA"][1][grp]
    hlsA, b_hls = c["hlsA"][0], c["hlsA"][1][grp]
    k.emit("dve", lambda v: v.tensor_tensor(dsegA[:, sl], dsegA[:, sl], dpmA[:, sl],
                                            op=ALU.add),
           reads=[b_dpm, b_dsg], writes=[b_dsg])
    k.emit("act", lambda a: a.activation(out=shfA[:, sl], in_=dsegA[:, sl], func=AF.Ln),
           reads=[b_dsg], writes=[b_shf])
    k.emit("dve", lambda v: v.tensor_tensor(shfA[:, sl], mnegA[:, sl], shfA[:, sl],
                                            op=ALU.subtract),
           reads=[b_mn, b_shf], writes=[b_shf])
    hv = hlsA[:, 32 * grp:32 * grp + 32].rearrange("p (a b) -> p a b", b=2)
    sv = shfA[:, sl].unsqueeze(2)
    k.emit("dve", lambda v: v.tensor_copy(hv[:, :, 0:1], sv),
           reads=[b_shf], writes=[b_hls])
    k.emit("dve", lambda v: v.tensor_tensor(shfA[:, sl].unsqueeze(2), sv, hv[:, :, 0:1],
                                            op=ALU.subtract),
           reads=[b_shf, b_hls], writes=[b_shf])
    k.emit("dve", lambda v: v.tensor_copy(hv[:, :, 1:2], shfA[:, sl].unsqueeze(2)),
           reads=[b_shf], writes=[b_hls])


def attn_flip(nc, k, c, h):
    hlsA, b_hls = c["hlsA"][0], c["hlsA"][1][h // 4]
    identT, b_id = c["identT"]
    qaug = c["qaug"]

    def flips(p):
        last = None
        for qc in range(4):
            last = p.transpose(k.pbF_t[:, 128 * qc:128 * (qc + 1)],
                               hlsA[:, 8 * h + 2 * qc:8 * h + 2 * qc + 2],
                               identT[:, :])
        return last
    k.emit("pe", flips, reads=[b_hls, b_id], writes=[k.pbF_b])
    k.emit("act", lambda a: a.activation(out=qaug[64:66, 512 * h:512 * (h + 1)],
                                         in_=k.pbF_t[:, :], func=AF.Copy),
           reads=[k.pbF_b], writes=[c["qaug_s"][h]])


def attn_pass2(nc, k, c, h):
    pbt, pb = k.pbanks, k.pbanks_bf
    qaug = c["qaug"]; kaug = c["kaug"]
    b_qd = c["qaug_d"][h]; b_qs = c["qaug_s"][h]
    b_kd = c["kaug_d"][h]; b_kp = c["kaug_p"][h]
    tri2T, b_tri2 = c["tri2T"]
    vtm, b_vtm = c["vtm"]
    pmvT, b_pmv = c["pmvT"]
    obank = 1
    atpm, b_atpm = c["atpm"][h % 2]

    def emit_sT(jc):
        if jc == "pm":
            k.emit("pe", lambda p: p.matmul(pbt[4][0:16, :],
                                            kaug[0:66, 528 * h + 512:528 * h + 528],
                                            qaug[0:66, 512 * h:512 * (h + 1)],
                                            start=True, stop=True),
                   reads=[b_kd, b_kp, b_qd, b_qs], writes=[pb[4]])
            k.emit("act", lambda a: a.activation(out=atpm[:, :], in_=pbt[4][0:16, :],
                                                 func=AF.Exp),
                   reads=[pb[4]], writes=[b_atpm])
        else:
            N = 512 - 128 * jc
            bi = 2 + (jc % 2)
            at, b_at = c["attnT"][jc % 3]
            k.emit("pe", lambda p, jc=jc, N=N, bi=bi: p.matmul(
                pbt[bi][:, 0:N],
                kaug[0:66, 528 * h + 128 * jc:528 * h + 128 * (jc + 1)],
                qaug[0:66, 512 * h + 128 * jc:512 * h + 512],
                start=True, stop=True),
                   reads=[b_kd, b_kp, b_qd, b_qs], writes=[pb[bi]])
            k.emit("dve", lambda v, bi=bi: v.tensor_tensor(
                pbt[bi][:, 0:128], pbt[bi][:, 0:128], tri2T[:, :], op=ALU.add),
                   reads=[b_tri2, pb[bi]], writes=[pb[bi]])
            k.emit("act", lambda a, N=N, bi=bi, at=at: a.activation(
                out=at[:, 0:N], in_=pbt[bi][:, 0:N], func=AF.Exp),
                   reads=[pb[bi]], writes=[b_at])

    def emit_av(jc):
        if jc == "pm":
            k.emit("pe", lambda p: p.matmul(pbt[obank][0:64, :],
                                            pmvT[:, 64 * h:64 * (h + 1)],
                                            atpm[:, :], start=True, stop=False),
                   reads=[b_pmv, b_atpm], writes=[pb[obank]])
        else:
            N = 512 - 128 * jc
            at, b_at = c["attnT"][jc % 3]
            k.emit("pe", lambda p, jc=jc, N=N, at=at: p.matmul(
                pbt[obank][0:64, 128 * jc:512],
                vtm[:, 1024 * jc + 64 * h:1024 * jc + 64 * (h + 1)],
                at[:, 0:N], start=False, stop=(jc == 3)),
                   reads=[b_vtm, b_at], writes=[pb[obank]])

    order = ["pm", 0, 1, 2, 3]
    prev = None
    for jc in order:
        emit_sT(jc)
        if prev is not None:
            emit_av(prev)
        prev = jc
    emit_av(prev)
    ofm, b_ofm = c["ofm"]
    hc, half = h // 2, h % 2
    k.emit("act", lambda a: a.activation(
        out=ofm[64 * half:64 * (half + 1), 512 * hc:512 * (hc + 1)],
        in_=pbt[obank][0:64, :], func=AF.Copy),
           reads=[pb[obank]], writes=[b_ofm])


STAGE = "full"
DUMP = None


def build_program():
    nc = bass.Bass("TRN2", target_bir_lowering=False)
    g = {}
    g["x"] = nc.dram_tensor("x", [D, T], F32, kind="ExternalInput").ap()
    g["y"] = nc.dram_tensor("y", [D, T], F32, kind="ExternalOutput").ap()
    g["xbuf"] = nc.dram_tensor("xbuf", [D, T], F32).ap()
    for nm, shp in [
        ("wqkv", [D, 3 * D]), ("wout", [D, D]),
        ("w1", [D, D]), ("w2", [D, 512]), ("w3", [512, 256]), ("w4", [256, 1]),
        ("gw1", [D, 512]), ("gw2", [512, 256]), ("gw3", [256, 1]),
        ("b1", [128, 8]), ("b2", [128, 4]), ("b3", [128, 2]), ("b4", [1, 1]),
        ("gb1", [128, 4]), ("gb2", [128, 2]), ("gb3", [1, 1]),
        ("ar", [1, 1]), ("gvec", [1, D]),
        ("pmk", [D, PMEM]), ("pmv", [PMEM, D]),
        ("cosk", [128, T]), ("ssk", [128, T]),
        ("tri1", [128, 128]), ("tri2", [128, 128]),
        ("ident", [128, 128]), ("perm", [128, 128]),
        ("onesrow", [1, 128]), ("onescolf", [128, 1]), ("ccmask", [128, 4]),
    ]:
        g[nm] = nc.dram_tensor(nm, shp, F32, kind="ExternalInput").ap()
    ccbufs = [(nc.dram_tensor("cc1i", [4, 1024], F32), nc.dram_tensor("cc1o", [4, 1024], F32)),
              (nc.dram_tensor("cc2i", [4, 1024], F32), nc.dram_tensor("cc2o", [4, 1024], F32))]

    stack = contextlib.ExitStack()
    sems = {}
    for snm in ENGS + QS + ("cc",):
        sems[snm] = stack.enter_context(nc.semaphore("sem_" + snm))

    def alloc(st, name, shape, dt):
        return st.enter_context(nc.sbuf_tensor(name, shape, dt))

    onesr = alloc(stack, "g_onesr", [1, 128], F32R)
    onescol = alloc(stack, "g_onescol", [128, 1], BF16)
    onescolr = alloc(stack, "g_onescolr", [128, 1], F32R)
    ccm = alloc(stack, "g_ccm", [128, 4], F32)
    epsA = alloc(stack, "g_epsA", [1, 1], F32)
    epsB = alloc(stack, "g_epsB", [1, 1], F32)
    pbanks = [stack.enter_context(nc.psum_tensor(f"pb{i}", [128, 512], F32))
              for i in range(7)]
    pbF = stack.enter_context(nc.psum_tensor("pbF", [2, 512], BF16))

    with nc.Block() as block:
        k = K(nc, sems)
        k.pbanks = pbanks
        k.pbanks_bf = [Bf() for _ in range(7)]
        k.pbF_t = pbF
        k.pbF_b = Bf()
        k.onesr_t = onesr
        k.onesr_b = Bf()
        k.onescol_t = onescol
        k.onescol_b = Bf()
        k.ccm_t = ccm
        k.ccm_b = Bf()
        k.epsA_t = epsA
        k.epsA_b = Bf()
        k.epsB_t = epsB
        k.epsB_b = Bf()
        k.rot = [0]
        k.dma("qsp", onesr[:, :], g["onesrow"].bitcast(F32R), writes=[k.onesr_b])
        k.onescolr_t = onescolr
        k.onescolr_b = Bf()
        k.dma("qsp", onescolr[:, :], g["onescolf"].bitcast(F32R), writes=[k.onescolr_b])
        k.dma("qsp", ccm[:, :], g["ccmask"], writes=[k.ccm_b])
        k.emit("dve", lambda v: v.memset(onescol[:, :], 1.0), writes=[k.onescol_b])
        k.emit("dve", lambda v: v.memset(epsA[:, :], EPS1), writes=[k.epsA_b])
        k.emit("dve", lambda v: v.memset(epsB[:, :], EPS1 * EPS2), writes=[k.epsB_b])
        st1 = contextlib.ExitStack()
        c1 = ffn_phase_open(nc, k, alloc, st1, "f1_")
        ffn_load(nc, k, g, c1)
        early = False
        if STAGE == "ld":
            k.dma("qsp", g["y"][0:128, 0:8], c1["b1c"][0][:, :], reads=[c1["b1c"][1]])
            early = True
        if not early:
            ffn_prepass(nc, k, g, c1, src=g["x"])
            ffn_load_gc(nc, k, g, c1)
            if STAGE == "prepass":
                for t in range(NT):
                    st_t, b_st = c1["states"][t]
                    for ch in range(NCH):
                        k.dma("qsp",
                              g["y"][128 * ch:128 * (ch + 1), 512 * t:512 * (t + 1)],
                              st_t[:, 512 * ch:512 * (ch + 1)], reads=[b_st])
                early = True
        if early:
            k.barrier()
            k.flush(block)
        else:
            ffn_cc_stage(nc, k, g, c1, ccbufs[0])
            k.flush(block)

            @block.gpsimd
            def _(gp):
                cc_run(nc, k, c1, ccbufs[0])

            if STAGE == "cc":
                k.dma("qsp", g["y"][0:128, 0:32], c1["gcr"][0][:, :],
                      reads=[c1["gcr"][1]])
                k.barrier()
                k.flush(block)
                early = True

        if early:
            STAGE2 = "done"
        else:
            STAGE2 = STAGE
        if STAGE2 != "done":
            gf1 = lambda: ffn_gf(nc, k, g, c1)
            if STAGE == "wc":
                gf1()
                dump = "h1" if DUMP is None else DUMP
                for t in range(NT):
                    ffn_wchain(nc, k, c1, t)
                    xio, b_xio = c1["xio"][t % 2]
                    if dump in ("h1", "h2", "h3"):
                        hsrc, b_h = c1[dump][t % 2]
                        k.emit("dve", lambda v, hsrc=hsrc, xio=xio: v.tensor_copy(
                            xio[:, 0:hsrc.shape[1]], hsrc[:, :]),
                               reads=[b_h], writes=[b_xio])
                        for ch in range(NCH):
                            k.dma("qsp",
                                  g["y"][128 * ch:128 * (ch + 1),
                                         512 * t:512 * (t + 1)],
                                  xio[:, 512 * ch:512 * (ch + 1)], reads=[b_xio])
                    elif dump == "tf":
                        tfrow, b_tf = c1["tfrow"][t % 2]
                        k.dma("qsp", g["y"][0:1, 512 * t:512 * (t + 1)],
                              tfrow[:, :], reads=[b_tf])
                    elif dump == "gf":
                        k.dma("qsp", g["y"][0:1, 0:1], c1["gfr"][0][:, :],
                              reads=[c1["gfr"][1]])
                        k.dma("qsp", g["y"][0:1, 1:2], c1["c0r"][0][:, :],
                              reads=[c1["c0r"][1]])
                        k.dma("qsp", g["y"][0:1, 2:3], c1["c1r"][0][:, :],
                              reads=[c1["c1r"][1]])
                STAGE2 = "ffn1"
            elif STAGE == "gfd":
                ffn_main(nc, k, g, c1, src=g["x"], dst=g["y"], cycles=(1,),
                         resid=False, gf_fn=gf1)
                k.dma("qsp", g["y"][0:1, 0:1], c1["gfr"][0][:, :],
                      reads=[c1["gfr"][1]])
                k.dma("qsp", g["y"][0:1, 1:2], c1["c0r"][0][:, :],
                      reads=[c1["c0r"][1]])
                k.dma("qsp", g["y"][0:1, 2:3], c1["c1r"][0][:, :],
                      reads=[c1["c1r"][1]])
                k.dma("qgp", g["y"][0:1, 3:8], c1["gcbf"][0][0:1, 0:5],
                      reads=[c1["gcbf"][1]])
                STAGE2 = "ffn1"
            elif STAGE in ("c1", "c2"):
                ffn_main(nc, k, g, c1, src=g["x"], dst=g["y"],
                         cycles=(1,) if STAGE == "c1" else (1, 2), resid=False,
                         gf_fn=gf1)
                for t in range(NT):
                    st_t, b_st = c1["states"][t]
                    for ch in range(NCH):
                        k.dma("qsp",
                              g["y"][128 * ch:128 * (ch + 1), 512 * t:512 * (t + 1)],
                              st_t[:, 512 * ch:512 * (ch + 1)], reads=[b_st])
                STAGE2 = "ffn1"
            else:
                ffn_main(nc, k, g, c1, src=g["x"],
                         dst=(g["y"] if STAGE == "ffn1" else g["xbuf"]),
                         gf_fn=gf1)
            k.barrier()
        st1.close()
        if STAGE2 not in ("done", "ffn1"):
            st2 = contextlib.ExitStack()
            c2 = attn_open(nc, k, alloc, st2)
            attn_load(nc, k, g, c2)
            adst = g["y"] if STAGE == "attn" else g["xbuf"]
            attn_ln(nc, k, g, c2, 0)
            attn_lnapply(nc, k, g, c2, 0)
            for s in range(NT):
                attn_qkv(nc, k, g, c2, s)
                if s + 1 < NT:
                    attn_ln(nc, k, g, c2, s + 1)
                    attn_lnapply(nc, k, g, c2, s + 1)
                attn_heads(nc, k, g, c2, s, adst)
            k.barrier()
            st2.close()
        if STAGE == "full":
            st3 = contextlib.ExitStack()
            c3 = ffn_phase_open(nc, k, alloc, st3, "f2_")
            ffn_load(nc, k, g, c3)
            ffn_prepass(nc, k, g, c3, src=g["xbuf"])
            ffn_load_gc(nc, k, g, c3)
            ffn_cc_stage(nc, k, g, c3, ccbufs[1])
            k.flush(block)

            @block.gpsimd
            def _(gp):
                cc_run(nc, k, c3, ccbufs[1])

            ffn_main(nc, k, g, c3, src=g["xbuf"], dst=g["y"],
                     gf_fn=lambda: ffn_gf(nc, k, g, c3))
            k.barrier()
            st3.close()
        k.flush(block)

    nc._keepalive = stack
    nc._kemit = k
    return nc


def _host_constants(half):
    f32 = np.float32
    inv = (1.0 / (10000.0 ** (np.arange(0, DH, 2, dtype=f32) / f32(DH)))).astype(f32)
    pos = np.arange(T * half, T * (half + 1), dtype=f32)
    th = (pos[None, :] * inv[:, None]).astype(f32)
    cos = np.cos(th).astype(f32)
    sin = np.sin(th).astype(f32)
    cos64 = np.repeat(cos, 2, axis=0)
    ss64 = np.repeat(sin, 2, axis=0)
    ss64[0::2] *= -1.0
    cos128 = np.concatenate([cos64, cos64], axis=0)
    ss128 = np.concatenate([ss64, ss64], axis=0)
    return np.ascontiguousarray(cos128), np.ascontiguousarray(ss128)


_PROG_CACHE = {}


def kernel(**inputs):
    if "nc" not in _PROG_CACHE:
        _PROG_CACHE["nc"] = build_program()
    nc = _PROG_CACHE["nc"]

    f32 = np.float32
    xin = np.asarray(inputs["L0_attn_seq"], dtype=f32)
    wqkv = np.ascontiguousarray(np.asarray(inputs["L0_attn_w_qkv"], f32).T)
    wout = np.ascontiguousarray(np.asarray(inputs["L0_attn_w_out"], f32).T)
    gvec = np.ascontiguousarray(np.asarray(inputs["L0_attn_g"], f32).reshape(1, D))
    pm = np.asarray(inputs["L0_attn_pm"], f32)
    pmk = np.ascontiguousarray(pm[0].transpose(0, 2, 1).reshape(D, PMEM))
    pmv = np.ascontiguousarray(pm[1].transpose(1, 0, 2).reshape(PMEM, D))
    w1 = np.ascontiguousarray(np.asarray(inputs["L0_ffn_cn_w1"], f32))
    w2 = np.ascontiguousarray(np.asarray(inputs["L0_ffn_cn_w2"], f32))
    w3 = np.ascontiguousarray(np.asarray(inputs["L0_ffn_cn_w3"], f32))
    w4 = np.ascontiguousarray(np.asarray(inputs["L0_ffn_cn_w4"], f32))
    gw1 = np.ascontiguousarray(np.asarray(inputs["L0_ffn_gc_w1"], f32))
    gw2 = np.ascontiguousarray(np.asarray(inputs["L0_ffn_gc_w2"], f32))
    gw3 = np.ascontiguousarray(np.asarray(inputs["L0_ffn_gc_w3"], f32))
    b1 = np.ascontiguousarray(np.asarray(inputs["L0_ffn_cn_b1"], f32).reshape(8, 128).T)
    b2 = np.ascontiguousarray(np.asarray(inputs["L0_ffn_cn_b2"], f32).reshape(4, 128).T)
    b3 = np.ascontiguousarray(np.asarray(inputs["L0_ffn_cn_b3"], f32).reshape(2, 128).T)
    b4 = np.asarray(inputs["L0_ffn_cn_b4"], f32).reshape(1, 1)
    gb1 = np.ascontiguousarray(np.asarray(inputs["L0_ffn_gc_b1"], f32).reshape(4, 128).T)
    gb2 = np.ascontiguousarray(np.asarray(inputs["L0_ffn_gc_b2"], f32).reshape(2, 128).T)
    gb3 = np.asarray(inputs["L0_ffn_gc_b3"], f32).reshape(1, 1)
    ar = np.asarray(inputs["L0_ffn_adaptive_range"], f32).reshape(1, 1)

    tri1 = np.triu(np.full((128, 128), NEG, dtype=f32), k=1)
    tri2 = np.ascontiguousarray(tri1.T)
    ident = np.eye(128, dtype=f32)
    perm = np.zeros((128, 128), dtype=f32)
    idx = np.arange(128)
    perm[idx, idx ^ 1] = 1.0
    onesrow = np.ones((1, 128), dtype=f32)
    onescol_host = np.ones((128, 1), dtype=f32)

    in_maps = []
    for core in range(8):
        b, half = core // 2, core % 2
        cosk, ssk = _host_constants(half)
        ccmask = np.zeros((128, 4), dtype=f32)
        ccmask[:, b] = 1.0
        xcore = np.ascontiguousarray(xin[b, T * half:T * (half + 1), :].T)
        in_maps.append({
            "x": xcore, "wqkv": wqkv, "wout": wout, "gvec": gvec,
            "pmk": pmk, "pmv": pmv,
            "w1": w1, "w2": w2, "w3": w3, "w4": w4,
            "gw1": gw1, "gw2": gw2, "gw3": gw3,
            "b1": b1, "b2": b2, "b3": b3, "b4": b4,
            "gb1": gb1, "gb2": gb2, "gb3": gb3, "ar": ar,
            "cosk": cosk, "ssk": ssk,
            "tri1": tri1, "tri2": tri2, "ident": ident, "perm": perm,
            "onesrow": onesrow, "onescolf": onescol_host, "ccmask": ccmask,
        })

    res = run_bass_kernel_spmd(nc, in_maps, core_ids=list(range(8)))
    out = np.empty((B, S, D), dtype=np.float32)
    for core in range(8):
        b, half = core // 2, core % 2
        out[b, T * half:T * (half + 1), :] = res.results[core]["y"].T
    return out


# version 12
# speedup vs baseline: 1.0959x; 1.0959x over previous
import contextlib
import numpy as np
import concourse.bass as bass
import concourse.mybir as mybir
from concourse.bass_utils import run_bass_kernel_spmd

F32 = mybir.dt.float32
F32R = mybir.dt.float32r
BF16 = mybir.dt.bfloat16
AF = mybir.ActivationFunctionType
ALU = mybir.AluOpType
AX = mybir.AxisListType

B, S, D = 4, 4096, 1024
T = 2048
NT = 4
SEG = 512
PMEM = 16
HEADS = 16
DH = 64
NCH = 8
EPS1 = 1e-5
EPS2 = 1.1920929e-07
NEG = -1.0e30
BASE_C = 0.1

ENGS = ("pe", "act", "dve", "sp", "gp")
QS = ("qsp", "qgp")


class Bf:
    __slots__ = ("w", "r")

    def __init__(self):
        self.w = None
        self.r = []


class K:

    def __init__(self, nc, sems):
        self.nc = nc
        self.sems = sems
        self.h = {"pe": nc.tensor, "act": nc.scalar, "dve": nc.vector,
                  "sp": nc.sync, "gp": nc.gpsimd}
        self.cnt = {k: 0 for k in ENGS + QS}
        self.cnt["cc"] = 0
        self.seen = {e: {} for e in ENGS}
        self.streams = {e: [] for e in ENGS}
        self.marks = []
        self._scope = "init"

    @property
    def cur_scope(self):
        return self._scope

    @cur_scope.setter
    def cur_scope(self, v):
        if v != self._scope:
            self.marks.append((v, dict(self.cnt)))
        self._scope = v

    def _waitlist(self, e, deps):
        best = {}
        for (kk, v) in deps:
            if kk == e and e == "pe":
                continue
            if best.get(kk, 0) < v:
                best[kk] = v
        out = []
        for kk, v in best.items():
            if kk == e:
                if self.seen[e].get(e, 0) >= v:
                    continue
                out.append((e, v))
                self.seen[e][e] = v
                continue
            if self.seen[e].get(kk, 0) >= v:
                continue
            out.append((kk, v))
            self.seen[e][kk] = v
        return out

    @staticmethod
    def _deps(reads, writes):
        deps = []
        for b in reads:
            if b.w:
                deps.append(b.w)
        for b in writes:
            if b.w:
                deps.append(b.w)
            deps.extend(b.r)
        return deps

    def emit(self, e, fn, reads=(), writes=()):
        waits = self._waitlist(e, self._deps(reads, writes))
        self.cnt[e] += 1
        self.streams[e].append((waits, fn, (e, 1), self.cur_scope))
        evt = (e, self.cnt[e])
        for b in reads:
            b.r.append(evt)
        for b in writes:
            b.w = evt
            b.r = []
        return evt

    def dma(self, q, out, in_, reads=(), writes=()):
        e = "sp" if q == "qsp" else "gp"
        waits = self._waitlist(e, self._deps(reads, writes))
        self.cnt[q] += 16
        self.streams[e].append(
            (waits, lambda h, out=out, in_=in_: h.dma_start(out=out, in_=in_),
             (q, 16), self.cur_scope))
        evt = (q, self.cnt[q])
        for b in reads:
            b.r.append(evt)
        for b in writes:
            b.w = evt
            b.r = []
        return evt

    def barrier(self):
        for e in ENGS:
            waits = []
            for kk in ENGS:
                if kk != e and self.seen[e].get(kk, 0) < self.cnt[kk]:
                    waits.append((kk, self.cnt[kk]))
                    self.seen[e][kk] = self.cnt[kk]
            for q in QS:
                if self.cnt[q] and self.seen[e].get(q, 0) < self.cnt[q]:
                    waits.append((q, self.cnt[q]))
                    self.seen[e][q] = self.cnt[q]
            self.streams[e].append((waits, None, None, self.cur_scope))

    def flush(self, block):
        secs = {"pe": block.tensor, "act": block.scalar, "dve": block.vector,
                "sp": block.sync, "gp": block.gpsimd}
        for e in ENGS:
            ops = self.streams[e]
            if not ops:
                continue

            def body(h, ops=ops):
                import contextlib as _ctx
                stck = _ctx.ExitStack()
                open_scope = None
                for waits, fn, inc, scope in ops:
                    if scope != open_scope:
                        stck.close()
                        stck = _ctx.ExitStack()
                        stck.enter_context(self.nc.named_scope(scope))
                        open_scope = scope
                    for (kk, v) in waits:
                        h.wait_ge(self.sems[kk], v)
                    if fn is not None:
                        last = fn(h)
                        if inc is not None:
                            last.then_inc(self.sems[inc[0]], inc[1])
                stck.close()
            secs[e](body)
            self.streams[e] = []


def ln_rows(k, c, pbS, pbX, t, combine_g):
    rows, b_rows = c["rows"]
    r0 = rows[:, 0:512]; r1 = rows[:, 512:1024]
    r2 = rows[:, 1024:1536]; r3 = rows[:, 1536:2048]
    k.emit("dve", lambda v: v.tensor_scalar(r0, pbS[0][0:1, :], 1.0 / D, None,
                                            op0=ALU.mult),
           reads=[pbS[1]], writes=[b_rows])
    k.emit("dve", lambda v: v.tensor_scalar(r1, pbX[0][0:1, :], 1.0 / D, None,
                                            op0=ALU.mult),
           reads=[pbX[1]], writes=[b_rows])
    k.emit("dve", lambda v: v.tensor_tensor(r2, r0, r0, op=ALU.mult),
           reads=[b_rows], writes=[b_rows])
    k.emit("dve", lambda v: v.tensor_tensor(r1, r1, r2, op=ALU.subtract),
           reads=[b_rows], writes=[b_rows])
    if combine_g:
        k.emit("act", lambda a: a.activation(out=r2, in_=r1, func=AF.Sqrt,
                                             bias=k.epsB_t[:, :], scale=1.0 + EPS2),
               reads=[b_rows, k.epsB_b], writes=[b_rows])
        scale = 1.0
    else:
        k.emit("act", lambda a: a.activation(out=r2, in_=r1, func=AF.Sqrt,
                                             bias=k.epsA_t[:, :], scale=1.0),
               reads=[b_rows, k.epsA_b], writes=[b_rows])
        scale = 0.5
    k.emit("dve", lambda v: v.reciprocal(r3, r2), reads=[b_rows], writes=[b_rows])
    k.emit("dve", lambda v: v.tensor_scalar(r1, r3, scale, None, op0=ALU.mult),
           reads=[b_rows], writes=[b_rows])
    k.emit("dve", lambda v: v.tensor_tensor(r2, r0, r1, op=ALU.mult),
           reads=[b_rows], writes=[b_rows])
    k.emit("dve", lambda v: v.tensor_copy(c["rh_r"][t][0][:, :], r1),
           reads=[b_rows], writes=[c["rh_r"][t][1]])
    k.emit("dve", lambda v: v.tensor_copy(c["nu_r"][t][0][:, :], r2),
           reads=[b_rows], writes=[c["nu_r"][t][1]])


def ln_stats(k, xio, b_xio, xbf, b_xbf, sqbf, b_sq, pbS, pbX):
    k.emit("dve", lambda v: v.tensor_copy(xbf[:, :], xio[:, :]),
           reads=[b_xio], writes=[b_xbf])
    k.emit("act", lambda a: a.activation(out=sqbf[:, :], in_=xio[:, :], func=AF.Square),
           reads=[b_xio], writes=[b_sq])

    def mk(src):
        def mm(p):
            last = None
            for ch in range(NCH):
                last = p.matmul(pbS[0][0:1, :] if src is xbf else pbX[0][0:1, :],
                                k.onescol_t[:, :], src[:, 512 * ch:512 * (ch + 1)],
                                start=(ch == 0), stop=(ch == NCH - 1))
            return last
        return mm
    k.emit("pe", mk(xbf), reads=[b_xbf, k.onescol_b], writes=[pbS[1]])
    k.emit("pe", mk(sqbf), reads=[b_sq, k.onescol_b], writes=[pbX[1]])


def ln_bcast(k, c, t, pbA, pbB):
    k.emit("pe", lambda p: p.matmul(pbA[0][:, :], k.onesr_t[:, :],
                                    c["rh_r"][t][0][:, :], start=True, stop=True),
           reads=[c["rh_r"][t][1], k.onesr_b], writes=[pbA[1]])
    k.emit("pe", lambda p: p.matmul(pbB[0][:, :], k.onesr_t[:, :],
                                    c["nu_r"][t][0][:, :], start=True, stop=True),
           reads=[c["nu_r"][t][1], k.onesr_b], writes=[pbB[1]])


def ln_apply(k, c, xio, b_xio, dst, b_dst, pbA, pbB):
    tmp, b_tmp = c["tmp"]
    for ch in range(NCH):
        sl = slice(512 * ch, 512 * (ch + 1))
        ts = tmp[:, 512 * (ch % 2):512 * (ch % 2 + 1)]
        k.emit("dve", lambda v, sl=sl, ts=ts: v.tensor_tensor(ts, xio[:, sl],
                                                              pbA[0][:, :], op=ALU.mult),
               reads=[b_xio, pbA[1]], writes=[b_tmp])
        k.emit("dve", lambda v, sl=sl, ts=ts: v.tensor_tensor(dst[:, sl], ts,
                                                              pbB[0][:, :], op=ALU.subtract),
               reads=[b_tmp, pbB[1]], writes=[b_dst])


def ffn_phase_open(nc, k, alloc, st, tag):
    c = {}

    def A(name, shape, dt):
        t = alloc(st, tag + name, shape, dt)
        b = Bf()
        c[name] = (t, b)
        return t
    for nm, shp, dt in [
        ("w1s", [128, 8 * 1024], BF16), ("w2s", [128, 8 * 512], BF16),
        ("w3s", [128, 4 * 256], BF16), ("w4s", [128, 2], BF16),
        ("b1c", [128, 8], F32), ("b2c", [128, 4], F32), ("b3c", [128, 2], F32),
        ("b4c", [1, 1], F32), ("gb1c", [128, 4], F32), ("gb2c", [128, 2], F32),
        ("gb3c", [1, 1], F32), ("arr", [1, 1], F32),
        ("tmp", [128, 2048], F32),
        ("gcacc", [128, 8], F32), ("gcs", [128, 32], F32), ("gcr", [128, 32], F32),
        ("gcsum", [128, 8], F32), ("gcbf", [128, 8], BF16),
        ("g1h", [128, 4], BF16), ("g2h", [128, 2], BF16),
        ("gfr", [1, 1], F32), ("c0r", [1, 1], F32), ("c1r", [1, 1], F32),
        ("rows", [1, 2048], F32),
    ]:
        A(nm, shp, dt)
    c["states"] = [(alloc(st, tag + f"stt{i}", [128, 4096], F32), Bf())
                   for i in range(NT)]
    c["sbf"] = [(alloc(st, tag + f"sbf{i}", [128, 4096], BF16), Bf()) for i in range(2)]
    c["h1"] = [(alloc(st, tag + f"h1_{i}", [128, 4096], BF16), Bf()) for i in range(2)]
    c["h2"] = [(alloc(st, tag + f"h2_{i}", [128, 2048], BF16), Bf()) for i in range(2)]
    c["h3"] = [(alloc(st, tag + "h3_0", [128, 1024], BF16), Bf())] * 2
    c["xio"] = [(alloc(st, tag + f"xio{i}", [128, 4096], F32), Bf()) for i in range(2)]
    c["stage"] = c["xio"][1]
    c["xbf"] = c["sbf"][0]
    c["sqbf"] = c["sbf"][1]
    xv = c["xio"][0][0][:, :].bitcast(BF16)
    c["g1s"] = (xv[:, 0:4096], c["xio"][0][1])
    c["g2s"] = (xv[:, 4096:5122], c["xio"][0][1])
    c["g3s"] = (xv[:, 5122:5124], c["xio"][0][1])
    c["rh_r"] = [(alloc(st, tag + f"rh{i}", [1, 512], F32R), Bf()) for i in range(NT)]
    c["nu_r"] = [(alloc(st, tag + f"nu{i}", [1, 512], F32R), Bf()) for i in range(NT)]
    c["crow_r"] = [(alloc(st, tag + f"cr{i}", [1, 512], F32R), Bf()) for i in range(2)]
    c["tfrow"] = [(alloc(st, tag + f"tf{i}", [1, 512], F32), Bf()) for i in range(2)]
    return c


def ffn_load(nc, k, g, c):
    k.cur_scope = "wload"
    for nm, src, ncols in [("w1s", g["w1"], 1024), ("w2s", g["w2"], 512),
                           ("w3s", g["w3"], 256), ("w4s", g["w4"], 1)]:
        t, bf = c[nm]
        nkc = src.shape[0] // 128
        for kc in range(nkc):
            k.dma("qgp", t[:, ncols * kc:ncols * (kc + 1)],
                  src[128 * kc:128 * (kc + 1), :], writes=[bf])
    for nm, src in [("b1c", g["b1"]), ("b2c", g["b2"]), ("b3c", g["b3"]),
                    ("b4c", g["b4"]), ("gb1c", g["gb1"]), ("gb2c", g["gb2"]),
                    ("gb3c", g["gb3"]), ("arr", g["ar"])]:
        t, bf = c[nm]
        k.dma("qsp", t[:, :], src, writes=[bf])
    k.emit("dve", lambda v: v.memset(c["gcacc"][0][:, :], 0.0), writes=[c["gcacc"][1]])


def ffn_load_gc(nc, k, g, c):
    k.cur_scope = "wload"
    for nm, src, ncols in [("g1s", g["gw1"], 512), ("g2s", g["gw2"], 256),
                           ("g3s", g["gw3"], 1)]:
        t, bf = c[nm]
        nkc = src.shape[0] // 128
        for kc in range(nkc):
            k.dma("qgp", t[:, ncols * kc:ncols * (kc + 1)],
                  src[128 * kc:128 * (kc + 1), :], writes=[bf])


def ffn_prepass(nc, k, g, c, src):
    k.cur_scope = "prepass"
    pbt, pb = k.pbanks, k.pbanks_bf
    gcacc, b_gcacc = c["gcacc"]
    xbf, b_xbf = c["xbf"]
    sqbf, b_sq = c["sqbf"]

    def banks(t):
        return ((pbt[4], pb[4]), (pbt[5], pb[5])) if t % 2 == 0 else \
               ((pbt[2], pb[2]), (pbt[3], pb[3]))

    def pstats(t):
        xio, b_xio = c["xio"][t % 2]
        for ch in range(NCH):
            k.dma("qsp", xio[:, 512 * ch:512 * (ch + 1)],
                  src[128 * ch:128 * (ch + 1), 512 * t:512 * (t + 1)], writes=[b_xio])
        pbS, pbX = banks(t)
        ln_stats(k, xio, b_xio, xbf, b_xbf, sqbf, b_sq, pbS, pbX)

    def ptail(t):
        pbS, pbX = banks(t)
        ln_rows(k, c, pbS, pbX, t, combine_g=False)
        ln_bcast(k, c, t, pbS, pbX)
        xio, b_xio = c["xio"][t % 2]
        state_t, b_state = c["states"][t]
        ln_apply(k, c, xio, b_xio, state_t, b_state, pbS, pbX)
        gcs, b_gcs = c["gcs"]
        k.emit("dve", lambda v, state_t=state_t: v.reduce_sum(
            gcs[:, 0:8], state_t[:, :].rearrange("p (c t) -> p c t", c=8), axis=AX.X),
               reads=[b_state], writes=[b_gcs])
        k.emit("dve", lambda v: v.tensor_tensor(gcacc[:, :], gcacc[:, :], gcs[:, 0:8],
                                                op=ALU.add),
               reads=[b_gcs, b_gcacc], writes=[b_gcacc])

    pstats(0)
    pstats(1)
    ptail(0)
    pstats(2)
    ptail(1)
    pstats(3)
    ptail(2)
    ptail(3)


def ffn_cc_stage(nc, k, g, c, ccb):
    gcacc, b_gcacc = c["gcacc"]
    gcs, b_gcs = c["gcs"]
    cin, _ = ccb
    cin3 = cin.ap().rearrange("s (p c) -> s p c", p=128)
    for s in range(4):
        k.emit("dve", lambda v, s=s: v.tensor_scalar(
            gcs[:, 8 * s:8 * (s + 1)], gcacc[:, :], k.ccm_t[:, s:s + 1], None,
            op0=ALU.mult),
               reads=[b_gcacc, k.ccm_b], writes=[b_gcs])
    for s in range(4):
        k.dma("qsp", cin3[s], gcs[:, 8 * s:8 * (s + 1)], reads=[b_gcs])


CC_ENABLE = True


def cc_run(nc, k, c, ccb):
    cin, cout = ccb
    v = k.cnt["qsp"]
    if k.seen["gp"].get("qsp", 0) < v:
        nc.gpsimd.wait_ge(k.sems["qsp"], v)
        k.seen["gp"]["qsp"] = v
    if CC_ENABLE:
        nc.gpsimd.collective_compute(
            "AllReduce", ALU.add,
            replica_groups=[[0, 1, 2, 3, 4, 5, 6, 7]],
            ins=[cin.ap().opt()], outs=[cout.ap().opt()],
        ).then_inc(k.sems["cc"])
        k.cnt["cc"] += 1
        nc.gpsimd.wait_ge(k.sems["cc"], k.cnt["cc"])
        rdbuf = cout
    else:
        rdbuf = cin
    gcr, b_gcr = c["gcr"]
    cout3 = rdbuf.ap().rearrange("s (p c) -> s p c", p=128)
    for s in range(4):
        nc.gpsimd.dma_start(out=gcr[:, 8 * s:8 * (s + 1)], in_=cout3[s]).then_inc(
            k.sems["qgp"], 16)
        k.cnt["qgp"] += 16
    b_gcr.w = ("qgp", k.cnt["qgp"])
    b_gcr.r = []


def ffn_gf(nc, k, g, c):
    k.cur_scope = "gf"
    pbt, pb = k.pbanks, k.pbanks_bf
    gcr, b_gcr = c["gcr"]
    gcsum, b_gcsum = c["gcsum"]
    gcbf, b_gcbf = c["gcbf"]
    k.emit("dve", lambda v: v.memset(gcsum[:, :], 0.0), writes=[b_gcsum])
    for s in range(4):
        k.emit("dve", lambda v, s=s: v.scalar_tensor_tensor(
            gcsum[:, :], gcr[:, 8 * s:8 * (s + 1)], k.ccm_t[:, s:s + 1], gcsum[:, :],
            op0=ALU.mult, op1=ALU.add),
               reads=[b_gcr, k.ccm_b, b_gcsum], writes=[b_gcsum])
    k.emit("dve", lambda v: v.tensor_scalar(gcbf[:, :], gcsum[:, :], 1.0 / 2048.0,
                                            None, op0=ALU.mult),
           reads=[b_gcsum], writes=[b_gcbf])
    g1s, b_g1s = c["g1s"]; g2s, b_g2s = c["g2s"]; g3s, b_g3s = c["g3s"]
    g1h, b_g1h = c["g1h"]; g2h, b_g2h = c["g2h"]
    gb1c, b_gb1 = c["gb1c"]; gb2c, b_gb2 = c["gb2c"]; gb3c, b_gb3 = c["gb3c"]
    for oc in range(4):
        def mm(p, oc=oc):
            last = None
            for kc in range(8):
                last = p.matmul(pbt[6][:, 0:1],
                                g1s[:, 512 * kc + 128 * oc:512 * kc + 128 * oc + 128],
                                gcbf[:, kc:kc + 1], start=(kc == 0), stop=(kc == 7))
            return last
        k.emit("pe", mm, reads=[b_g1s, b_gcbf], writes=[pb[6]])
        k.emit("act", lambda a, oc=oc: a.activation(
            out=g1h[:, oc:oc + 1], in_=pbt[6][:, 0:1], func=AF.Gelu,
            bias=gb1c[:, oc:oc + 1]),
               reads=[pb[6], b_gb1], writes=[b_g1h])
    for oc in range(2):
        def mm(p, oc=oc):
            last = None
            for kc in range(4):
                last = p.matmul(pbt[6][:, 0:1],
                                g2s[:, 256 * kc + 128 * oc:256 * kc + 128 * oc + 128],
                                g1h[:, kc:kc + 1], start=(kc == 0), stop=(kc == 3))
            return last
        k.emit("pe", mm, reads=[b_g2s, b_g1h], writes=[pb[6]])
        k.emit("act", lambda a, oc=oc: a.activation(
            out=g2h[:, oc:oc + 1], in_=pbt[6][:, 0:1], func=AF.Gelu,
            bias=gb2c[:, oc:oc + 1]),
               reads=[pb[6], b_gb2], writes=[b_g2h])

    def mm3(p):
        last = None
        for kc in range(2):
            last = p.matmul(pbt[6][0:1, 0:1], g3s[:, kc:kc + 1], g2h[:, kc:kc + 1],
                            start=(kc == 0), stop=(kc == 1))
        return last
    k.emit("pe", mm3, reads=[b_g3s, b_g2h], writes=[pb[6]])
    gfr, b_gfr = c["gfr"]
    k.emit("act", lambda a: a.activation(out=gfr[:, :], in_=pbt[6][0:1, 0:1],
                                         func=AF.Sigmoid, bias=gb3c[:, :]),
           reads=[pb[6], b_gb3], writes=[b_gfr])
    arr, b_arr = c["arr"]
    c0r, b_c0 = c["c0r"]; c1r, b_c1 = c["c1r"]
    rows, b_rows = c["rows"]
    tt = rows[:, 1536:1537]
    k.emit("dve", lambda v: v.tensor_scalar(c1r[:, :], arr[:, :], 0.6, None,
                                            op0=ALU.mult),
           reads=[b_arr], writes=[b_c1])
    k.emit("dve", lambda v: v.tensor_scalar(tt, gfr[:, :], 1.4, -1.0,
                                            op0=ALU.mult, op1=ALU.add),
           reads=[b_gfr], writes=[b_rows])
    k.emit("dve", lambda v: v.tensor_tensor(c0r[:, :], tt, arr[:, :], op=ALU.mult),
           reads=[b_rows, b_arr], writes=[b_c0])
    k.emit("dve", lambda v: v.tensor_scalar(c0r[:, :], c0r[:, :], 1.0, BASE_C,
                                            op0=ALU.mult, op1=ALU.add),
           reads=[b_c0], writes=[b_c0])


def ffn_cast(nc, k, c, t):
    k.cur_scope = "fmain"
    sbf, b_sbf = c["sbf"][t % 2]
    state_t, b_state = c["states"][t]
    k.emit("dve", lambda v: v.tensor_scalar(sbf[:, :], state_t[:, :], 2.0, None,
                                            op0=ALU.mult),
           reads=[b_state], writes=[b_sbf])


def ffn_wchain(nc, k, c, t):
    pbt, pb = k.pbanks, k.pbanks_bf
    sbf, b_sbf = c["sbf"][t % 2]
    h1, b_h1 = c["h1"][t % 2]; h2, b_h2 = c["h2"][t % 2]; h3, b_h3 = c["h3"][t % 2]
    w1s, b_w1 = c["w1s"]; w2s, b_w2 = c["w2s"]; w3s, b_w3 = c["w3s"]; w4s, b_w4 = c["w4s"]
    b1c, bb1 = c["b1c"]; b2c, bb2 = c["b2c"]; b3c, bb3 = c["b3c"]; b4c, bb4 = c["b4c"]
    for oc in range(8):
        bi = k.rot[0]; k.rot[0] = (k.rot[0] + 1) % 4

        def mm(p, oc=oc, bi=bi):
            last = None
            for kc in range(8):
                last = p.matmul(pbt[bi][:, :],
                                w1s[:, 1024 * kc + 128 * oc:1024 * kc + 128 * oc + 128],
                                sbf[:, 512 * kc:512 * (kc + 1)],
                                start=(kc == 0), stop=(kc == 7))
            return last
        k.emit("pe", mm, reads=[b_w1, b_sbf], writes=[pb[bi]])
        k.emit("act", lambda a, oc=oc, bi=bi: a.activation(
            out=h1[:, 512 * oc:512 * (oc + 1)], in_=pbt[bi][:, :], func=AF.Gelu,
            bias=b1c[:, oc:oc + 1]),
               reads=[pb[bi], bb1], writes=[b_h1])
    for oc in range(4):
        bi = k.rot[0]; k.rot[0] = (k.rot[0] + 1) % 4

        def mm(p, oc=oc, bi=bi):
            last = None
            for kc in range(8):
                last = p.matmul(pbt[bi][:, :],
                                w2s[:, 512 * kc + 128 * oc:512 * kc + 128 * oc + 128],
                                h1[:, 512 * kc:512 * (kc + 1)],
                                start=(kc == 0), stop=(kc == 7))
            return last
        k.emit("pe", mm, reads=[b_w2, b_h1], writes=[pb[bi]])
        k.emit("act", lambda a, oc=oc, bi=bi: a.activation(
            out=h2[:, 512 * oc:512 * (oc + 1)], in_=pbt[bi][:, :], func=AF.Gelu,
            bias=b2c[:, oc:oc + 1]),
               reads=[pb[bi], bb2], writes=[b_h2])
    for oc in range(2):
        bi = k.rot[0]; k.rot[0] = (k.rot[0] + 1) % 4

        def mm(p, oc=oc, bi=bi):
            last = None
            for kc in range(4):
                last = p.matmul(pbt[bi][:, :],
                                w3s[:, 256 * kc + 128 * oc:256 * kc + 128 * oc + 128],
                                h2[:, 512 * kc:512 * (kc + 1)],
                                start=(kc == 0), stop=(kc == 3))
            return last
        k.emit("pe", mm, reads=[b_w3, b_h2], writes=[pb[bi]])
        k.emit("act", lambda a, oc=oc, bi=bi: a.activation(
            out=h3[:, 512 * oc:512 * (oc + 1)], in_=pbt[bi][:, :], func=AF.Gelu,
            bias=b3c[:, oc:oc + 1]),
               reads=[pb[bi], bb3], writes=[b_h3])

    def mm4(p):
        last = None
        for kc in range(2):
            last = p.matmul(pbt[6][0:1, :], w4s[:, kc:kc + 1],
                            h3[:, 512 * kc:512 * (kc + 1)],
                            start=(kc == 0), stop=(kc == 1))
        return last
    k.emit("pe", mm4, reads=[b_w4, b_h3], writes=[pb[6]])
    tfrow, b_tf = c["tfrow"][t % 2]
    k.emit("act", lambda a: a.activation(out=tfrow[:, :], in_=pbt[6][0:1, :],
                                         func=AF.Sigmoid, bias=b4c[:, :]),
           reads=[pb[6], bb4], writes=[b_tf])


TWIST_GROUPS = [
    [(0, 4, ALU.add), (1, 5, ALU.add), (4, 0, ALU.subtract), (5, 1, ALU.subtract)],
    [(2, 6, ALU.subtract), (3, 7, ALU.subtract), (6, 2, ALU.add), (7, 3, ALU.add)],
]


def ffn_update(nc, k, c, t, cyc):
    pbt, pb = k.pbanks, k.pbanks_bf
    state_t, b_state = c["states"][t]
    tmp, b_tmp = c["tmp"]
    tfrow, b_tf = c["tfrow"][t % 2]
    crow, b_crow = c["crow_r"][t % 2]
    c0r, b_c0 = c["c0r"]; c1r, b_c1 = c["c1r"]
    rows, b_rows = c["rows"]
    cr = rows[:, 1536:2048]
    k.emit("dve", lambda v: v.tensor_scalar(cr, tfrow[:, :], c1r[:, :], c0r[:, :],
                                            op0=ALU.mult, op1=ALU.add),
           reads=[b_tf, b_c0, b_c1], writes=[b_rows])
    k.emit("dve", lambda v: v.tensor_copy(crow[:, :], cr),
           reads=[b_rows], writes=[b_crow])
    k.emit("pe", lambda p: p.matmul(pbt[5][:, :], k.onesr_t[:, :], crow[:, :],
                                    start=True, stop=True),
           reads=[b_crow, k.onesr_b], writes=[pb[5]])
    for grp in TWIST_GROUPS:
        for i, (dst, src, op) in enumerate(grp):
            k.emit("dve", lambda v, i=i, src=src: v.tensor_tensor(
                tmp[:, 512 * i:512 * (i + 1)], state_t[:, 512 * src:512 * (src + 1)],
                pbt[5][:, :], op=ALU.mult),
                   reads=[b_state, pb[5]], writes=[b_tmp])
        for i, (dst, src, op) in enumerate(grp):
            k.emit("dve", lambda v, i=i, dst=dst, op=op: v.tensor_tensor(
                state_t[:, 512 * dst:512 * (dst + 1)],
                state_t[:, 512 * dst:512 * (dst + 1)],
                tmp[:, 512 * i:512 * (i + 1)], op=op),
                   reads=[b_tmp, b_state], writes=[b_state])


def ffn_resid_read(nc, k, g, c, t, src):
    xio, b_xio = c["xio"][t % 2]
    for ch in range(NCH):
        k.dma("qsp", xio[:, 512 * ch:512 * (ch + 1)],
              src[128 * ch:128 * (ch + 1), 512 * t:512 * (t + 1)],
              writes=[b_xio])


def ffn_resid_add(nc, k, g, c, t, dst):
    k.cur_scope = "fresid"
    xio, b_xio = c["xio"][t % 2]
    state_t, b_state = c["states"][t]
    k.emit("dve", lambda v: v.tensor_tensor(
        xio[:, :], xio[:, :], state_t[:, :], op=ALU.add),
           reads=[b_state, b_xio], writes=[b_xio])
    for ch in range(NCH):
        k.dma("qgp", dst[128 * ch:128 * (ch + 1), 512 * t:512 * (t + 1)],
              xio[:, 512 * ch:512 * (ch + 1)], reads=[b_xio])


def ffn_main(nc, k, g, c, src, dst, cycles=(1, 2, 3), resid=True, gf_fn=None):
    for cyc in cycles:
        last = (cyc == 3 and resid)
        if last:
            ffn_resid_read(nc, k, g, c, 0, src)
            ffn_resid_read(nc, k, g, c, 1, src)
        ffn_cast(nc, k, c, 0)
        ffn_cast(nc, k, c, 1)
        ffn_wchain(nc, k, c, 0)
        ffn_cast(nc, k, c, 2)
        ffn_wchain(nc, k, c, 1)
        if gf_fn is not None:
            gf_fn()
            gf_fn = None
        ffn_update(nc, k, c, 0, cyc)
        if last:
            ffn_resid_add(nc, k, g, c, 0, dst)
            ffn_resid_read(nc, k, g, c, 2, src)
        ffn_cast(nc, k, c, 3)
        ffn_wchain(nc, k, c, 2)
        ffn_update(nc, k, c, 1, cyc)
        if last:
            ffn_resid_add(nc, k, g, c, 1, dst)
            ffn_resid_read(nc, k, g, c, 3, src)
        ffn_wchain(nc, k, c, 3)
        ffn_update(nc, k, c, 2, cyc)
        if last:
            ffn_resid_add(nc, k, g, c, 2, dst)
        ffn_update(nc, k, c, 3, cyc)
        if last:
            ffn_resid_add(nc, k, g, c, 3, dst)


def attn_open(nc, k, alloc, st):
    c = {}

    def A(name, shape, dt):
        t = alloc(st, "at_" + name, shape, dt)
        b = Bf()
        c[name] = (t, b)
        return t
    for nm, shp, dt in [
        ("qkvw", [128, 8 * 3072], BF16), ("wow", [128, 8 * 1024], BF16),
        ("coskT", [128, 2048], F32), ("sskT", [128, 2048], F32),
        ("tri1T", [128, 128], F32), ("tri2T", [128, 128], F32),
        ("identT", [128, 128], BF16), ("permT", [128, 128], BF16),
        ("pmkT", [128, 128], BF16), ("pmvT", [16, 1024], BF16),
        ("ybf", [128, 4096], BF16),
        ("vtm", [128, 4 * 1024], BF16),
        ("ofm", [128, 8 * 512], BF16),
        ("burn", [128, 512], BF16), ("burn2", [128, 16], BF16),
        ("rows", [1, 2048], F32),
    ]:
        A(nm, shp, dt)
    c["xs"] = [(alloc(st, f"at_xs{i}", [128, 4096], F32), Bf()) for i in range(2)]
    c["sqbf2"] = c["vtm"]
    c["qaug"] = alloc(st, "at_qaug", [128, 16 * 512], BF16)
    c["kaug"] = alloc(st, "at_kaug", [128, 16 * 528], BF16)
    c["qaug_d"] = [Bf() for _ in range(16)]
    c["qaug_s"] = [Bf() for _ in range(16)]
    c["kaug_d"] = [Bf() for _ in range(16)]
    c["kaug_p"] = [Bf() for _ in range(16)]
    c["attnT"] = [(alloc(st, f"at_at{i}", [128, 512], BF16), Bf()) for i in range(3)]
    c["atpm"] = [(alloc(st, f"at_atpm{i}", [16, 512], BF16), Bf()) for i in range(2)]
    c["qsb"] = [(alloc(st, f"at_qsb{i}", [128, 512], BF16), Bf()) for i in range(2)]
    c["t1f"] = [(alloc(st, f"at_t1f{i}", [128, 512], F32), Bf()) for i in range(2)]
    c["t2f"] = [(alloc(st, f"at_t2f{i}", [128, 512], F32), Bf()) for i in range(2)]
    for nm in ("mnegA", "dsegA", "dpmA", "shfA"):
        c[nm] = (alloc(st, "at_" + nm, [128, 64], F32), [Bf() for _ in range(4)])
    c["hlsA"] = (alloc(st, "at_hlsA", [128, 128], BF16), [Bf() for _ in range(4)])
    c["rh_r"] = [(alloc(st, "at_rh", [1, 512], F32R), Bf())]
    c["nu_r"] = [(alloc(st, "at_nu", [1, 512], F32R), Bf())]
    c["growR"] = (alloc(st, "at_grow", [1, 1024], F32R), Bf())
    return c


def attn_load(nc, k, g, c):
    k.cur_scope = "aload"
    qkvw, b_qkvw = c["qkvw"]
    for kc in range(8):
        k.dma("qgp", qkvw[:, 3072 * kc:3072 * (kc + 1)],
              g["wqkv"][128 * kc:128 * (kc + 1), :], writes=[b_qkvw])
    wow, b_wow = c["wow"]
    for kc in range(8):
        k.dma("qgp", wow[:, 1024 * kc:1024 * (kc + 1)],
              g["wout"][128 * kc:128 * (kc + 1), :], writes=[b_wow])
    for nm, src in [("coskT", g["cosk"]), ("sskT", g["ssk"]),
                    ("tri1T", g["tri1"]), ("tri2T", g["tri2"])]:
        t, bf = c[nm]
        k.dma("qsp", t[:, :], src, writes=[bf])
    for nm, src in [("identT", g["ident"]), ("permT", g["perm"])]:
        t, bf = c[nm]
        k.dma("qgp", t[:, :], src, writes=[bf])
    pmkT, b_pmk = c["pmkT"]
    for ch in range(8):
        k.dma("qgp", pmkT[:, 16 * ch:16 * (ch + 1)],
              g["pmk"][128 * ch:128 * (ch + 1), :], writes=[b_pmk])
    pmvT, b_pmv = c["pmvT"]
    k.dma("qgp", pmvT[:, :], g["pmv"], writes=[b_pmv])
    growR, b_grow = c["growR"]
    k.dma("qsp", growR[:, :], g["gvec"].bitcast(F32R), writes=[b_grow])
    kaug = c["kaug"]
    k.emit("dve", lambda v: v.memset(kaug[64:66, :], 1.0), writes=c["kaug_p"])
    for h in range(16):
        hc, lo = h // 2, (h % 2) * 64
        k.emit("act", lambda a, h=h, hc=hc, lo=lo: a.activation(
            out=kaug[0:64, 528 * h + 512:528 * h + 528],
            in_=pmkT[lo:lo + 64, 16 * hc:16 * (hc + 1)], func=AF.Copy),
               reads=[b_pmk], writes=[c["kaug_p"][h]])


def attn_seg(nc, k, g, c, s, dst=None):
    if dst is None:
        dst = g["xbuf"]
    pbt, pb = k.pbanks, k.pbanks_bf
    k.cur_scope = "aln"
    xs, b_xs = c["xs"][s % 2]
    src = g["xbuf"]
    for ch in range(NCH):
        k.dma("qsp", xs[:, 512 * ch:512 * (ch + 1)],
              src[128 * ch:128 * (ch + 1), 512 * s:512 * (s + 1)], writes=[b_xs])
    ybf, b_ybf = c["ybf"]
    sqbf, b_sq = c["sqbf2"]
    ln_stats(k, xs, b_xs, ybf, b_ybf, sqbf, b_sq, (pbt[4], pb[4]), (pbt[5], pb[5]))
    ln_rows(k, c, (pbt[4], pb[4]), (pbt[5], pb[5]), 0, combine_g=True)
    growR, b_grow = c["growR"]
    rh_r, b_rh = c["rh_r"][0]
    nu_r, b_nu = c["nu_r"][0]
    for ch in range(NCH):
        bA, bB = (4, 5) if ch % 2 == 0 else (2, 3)
        k.emit("pe", lambda p, ch=ch, bA=bA: p.matmul(
            pbt[bA][:, :], growR[:, 128 * ch:128 * (ch + 1)], rh_r[:, :],
            start=True, stop=True),
               reads=[b_grow, b_rh], writes=[pb[bA]])
        k.emit("pe", lambda p, ch=ch, bB=bB: p.matmul(
            pbt[bB][:, :], growR[:, 128 * ch:128 * (ch + 1)], nu_r[:, :],
            start=True, stop=True),
               reads=[b_grow, b_nu], writes=[pb[bB]])
        sl = slice(512 * ch, 512 * (ch + 1))
        t1f, b_t1 = c["t1f"][ch % 2]
        k.emit("dve", lambda v, sl=sl, bA=bA, t1f=t1f: v.tensor_tensor(
            t1f[:, :], xs[:, sl], pbt[bA][:, :], op=ALU.mult),
               reads=[b_xs, pb[bA]], writes=[b_t1])
        k.emit("dve", lambda v, sl=sl, bB=bB, t1f=t1f: v.tensor_tensor(
            ybf[:, sl], t1f[:, :], pbt[bB][:, :], op=ALU.subtract),
               reads=[b_t1, pb[bB]], writes=[b_ybf])
    qkvw, b_qkvw = c["qkvw"]
    cosT, b_cos = c["coskT"]
    ssT, b_ss = c["sskT"]
    permT, b_perm = c["permT"]
    tsl = slice(512 * s, 512 * (s + 1))

    def qk_mm(which, qscale, base, hc):
        bi = hc % 2

        def mm(p, hc=hc, bi=bi, base=base):
            last = None
            for kc in range(8):
                last = p.matmul(pbt[bi][:, :],
                                qkvw[:, 3072 * kc + base + 128 * hc:
                                     3072 * kc + base + 128 * hc + 128],
                                ybf[:, 512 * kc:512 * (kc + 1)],
                                start=(kc == 0), stop=(kc == 7))
            return last
        k.emit("pe", mm, reads=[b_qkvw, b_ybf], writes=[pb[bi]])
        qsb, b_qsb = c["qsb"][hc % 2]
        k.emit("act", lambda a, bi=bi, qsb=qsb, qscale=qscale: a.activation(
            out=qsb[:, :], in_=pbt[bi][:, :], func=AF.Copy, scale=qscale),
               reads=[pb[bi]], writes=[b_qsb])

    def qk_tail(which, hc):
        bi = hc % 2
        qsb, b_qsb = c["qsb"][hc % 2]
        k.emit("pe", lambda p, bi=bi, qsb=qsb: p.matmul(
            pbt[2 + bi][:, :], permT[:, :], qsb[:, :], start=True, stop=True),
               reads=[b_perm, b_qsb], writes=[pb[2 + bi]])
        t1x, b_t1x = c["t1f"][hc % 2]
        t2x, b_t2x = c["t2f"][hc % 2]
        k.emit("dve", lambda v, qsb=qsb, t1x=t1x: v.tensor_tensor(
            t1x[:, :], qsb[:, :], cosT[:, tsl], op=ALU.mult),
               reads=[b_qsb, b_cos], writes=[b_t1x])
        k.emit("dve", lambda v, bi=bi, t2x=t2x: v.tensor_tensor(
            t2x[:, :], pbt[2 + bi][:, :], ssT[:, tsl], op=ALU.mult),
               reads=[pb[2 + bi], b_ss], writes=[b_t2x])
        for half in range(2):
            h = 2 * hc + half
            if which == "q":
                rdst = c["qaug"][0:64, 512 * h:512 * (h + 1)]
                wb = c["qaug_d"][h]
            else:
                rdst = c["kaug"][0:64, 528 * h:528 * h + 512]
                wb = c["kaug_d"][h]
            lo = 64 * half
            k.emit("dve", lambda v, rdst=rdst, lo=lo, t1x=t1x, t2x=t2x:
                   v.tensor_tensor(rdst, t1x[lo:lo + 64, :], t2x[lo:lo + 64, :],
                                   op=ALU.add),
                   reads=[b_t1x, b_t2x], writes=[wb])

    k.cur_scope = "aqkv"
    prev = None
    for which, qscale, base in (("q", 0.125, 0), ("k", 1.0, 1024)):
        for hc in range(8):
            qk_mm(which, qscale, base, hc)
            if prev is not None:
                qk_tail(*prev)
            prev = (which, hc)
    qk_tail(*prev)
    vtm, b_vtm = c["vtm"]
    for tc in range(4):
        for vh in range(2):
            bi = (tc * 2 + vh) % 2

            def mm(p, tc=tc, vh=vh, bi=bi):
                last = None
                for kc in range(8):
                    last = p.matmul(pbt[bi][:, :],
                                    ybf[:, 512 * kc + 128 * tc:512 * kc + 128 * tc + 128],
                                    qkvw[:, 3072 * kc + 2048 + 512 * vh:
                                         3072 * kc + 2048 + 512 * (vh + 1)],
                                    start=(kc == 0), stop=(kc == 7))
                return last
            k.emit("pe", mm, reads=[b_qkvw, b_ybf], writes=[pb[bi]])
            k.emit("act", lambda a, tc=tc, vh=vh, bi=bi: a.activation(
                out=vtm[:, 1024 * tc + 512 * vh:1024 * tc + 512 * (vh + 1)],
                in_=pbt[bi][:, :], func=AF.Copy),
                   reads=[pb[bi]], writes=[b_vtm])
    k.cur_scope = "aheads"
    attn_stats_stage(nc, k, c, 0)
    attn_stats_stage(nc, k, c, 1)
    for grp in range(4):
        attn_shift_stage(nc, k, c, grp)
        for h in range(4 * grp, 4 * grp + 4):
            attn_flip(nc, k, c, h)
        if grp + 2 < 4:
            attn_stats_stage(nc, k, c, grp + 2)
        for h in range(4 * grp, 4 * grp + 4):
            attn_pass2(nc, k, c, h)
    k.cur_scope = "awout"
    ofm, b_ofm = c["ofm"]
    wow, b_wow = c["wow"]
    for oc in range(8):
        bi = 2 + (oc % 2)

        def mm(p, oc=oc, bi=bi):
            last = None
            for kc in range(8):
                last = p.matmul(pbt[bi][:, :],
                                wow[:, 1024 * kc + 128 * oc:1024 * kc + 128 * oc + 128],
                                ofm[:, 512 * kc:512 * (kc + 1)],
                                start=(kc == 0), stop=(kc == 7))
            return last
        k.emit("pe", mm, reads=[b_wow, b_ofm], writes=[pb[bi]])
        sl = slice(512 * oc, 512 * (oc + 1))
        k.emit("dve", lambda v, sl=sl, bi=bi: v.tensor_tensor(
            xs[:, sl], xs[:, sl], pbt[bi][:, :], op=ALU.add),
               reads=[pb[bi], b_xs], writes=[b_xs])
    for ch in range(NCH):
        k.dma("qgp", dst[128 * ch:128 * (ch + 1), 512 * s:512 * (s + 1)],
              xs[:, 512 * ch:512 * (ch + 1)], reads=[b_xs])


def attn_stats_stage(nc, k, c, grp):
    pbt, pb = k.pbanks, k.pbanks_bf
    mnegA, b_mn = c["mnegA"][0], c["mnegA"][1][grp]
    dsegA, b_dsg = c["dsegA"][0], c["dsegA"][1][grp]
    dpmA, b_dpm = c["dpmA"][0], c["dpmA"][1][grp]
    burn, _ = c["burn"]
    burn2, _ = c["burn2"]
    qaug = c["qaug"]; kaug = c["kaug"]
    tri1T, b_tri1 = c["tri1T"]
    for h in range(4 * grp, 4 * grp + 4):
        b_qd = c["qaug_d"][h]; b_kd = c["kaug_d"][h]; b_kp = c["kaug_p"][h]
        for qc in range(4):
            col = 4 * h + qc
            N = 128 * (qc + 1)
            bi = 4 + (qc % 2)
            bp = 6 if qc % 2 == 0 else 0
            k.emit("pe", lambda p, h=h, qc=qc, N=N, bi=bi: p.matmul(
                pbt[bi][:, 0:N],
                qaug[0:64, 512 * h + 128 * qc:512 * h + 128 * (qc + 1)],
                kaug[0:64, 528 * h:528 * h + N], start=True, stop=True),
                   reads=[b_qd, b_kd], writes=[pb[bi]])
            k.emit("pe", lambda p, h=h, qc=qc, bp=bp: p.matmul(
                pbt[bp][:, 0:16],
                qaug[0:64, 512 * h + 128 * qc:512 * h + 128 * (qc + 1)],
                kaug[0:64, 528 * h + 512:528 * h + 528], start=True, stop=True),
                   reads=[b_qd, b_kp], writes=[pb[bp]])
            k.emit("dve", lambda v, qc=qc, bi=bi: v.tensor_tensor(
                pbt[bi][:, 128 * qc:128 * (qc + 1)], pbt[bi][:, 128 * qc:128 * (qc + 1)],
                tri1T[:, :], op=ALU.add),
                   reads=[b_tri1, pb[bi]], writes=[pb[bi]])
            k.emit("dve", lambda v, col=col, N=N, bi=bi: v.reduce_max(
                mnegA[:, col:col + 1], pbt[bi][:, 0:N], axis=AX.X, negate=True),
                   reads=[pb[bi]], writes=[b_mn])
            k.emit("act", lambda a, col=col, N=N, bi=bi: a.activation(
                out=burn[:, 0:N], in_=pbt[bi][:, 0:N], func=AF.Exp,
                bias=mnegA[:, col:col + 1], accum_out=dsegA[:, col:col + 1]),
                   reads=[pb[bi], b_mn], writes=[b_dsg])
            k.emit("act", lambda a, col=col, bp=bp: a.activation(
                out=burn2[:, :], in_=pbt[bp][:, 0:16], func=AF.Exp,
                bias=mnegA[:, col:col + 1], accum_out=dpmA[:, col:col + 1]),
                   reads=[pb[bp], b_mn], writes=[b_dpm])


def attn_shift_stage(nc, k, c, grp):
    sl = slice(16 * grp, 16 * grp + 16)
    mnegA, b_mn = c["mnegA"][0], c["mnegA"][1][grp]
    dsegA, b_dsg = c["dsegA"][0], c["dsegA"][1][grp]
    dpmA, b_dpm = c["dpmA"][0], c["dpmA"][1][grp]
    shfA, b_shf = c["shfA"][0], c["shfA"][1][grp]
    hlsA, b_hls = c["hlsA"][0], c["hlsA"][1][grp]
    k.emit("dve", lambda v: v.tensor_tensor(dsegA[:, sl], dsegA[:, sl], dpmA[:, sl],
                                            op=ALU.add),
           reads=[b_dpm, b_dsg], writes=[b_dsg])
    k.emit("act", lambda a: a.activation(out=shfA[:, sl], in_=dsegA[:, sl], func=AF.Ln),
           reads=[b_dsg], writes=[b_shf])
    k.emit("dve", lambda v: v.tensor_tensor(shfA[:, sl], mnegA[:, sl], shfA[:, sl],
                                            op=ALU.subtract),
           reads=[b_mn, b_shf], writes=[b_shf])
    hv = hlsA[:, 32 * grp:32 * grp + 32].rearrange("p (a b) -> p a b", b=2)
    sv = shfA[:, sl].unsqueeze(2)
    k.emit("dve", lambda v: v.tensor_copy(hv[:, :, 0:1], sv),
           reads=[b_shf], writes=[b_hls])
    k.emit("dve", lambda v: v.tensor_tensor(shfA[:, sl].unsqueeze(2), sv, hv[:, :, 0:1],
                                            op=ALU.subtract),
           reads=[b_shf, b_hls], writes=[b_shf])
    k.emit("dve", lambda v: v.tensor_copy(hv[:, :, 1:2], shfA[:, sl].unsqueeze(2)),
           reads=[b_shf], writes=[b_hls])


def attn_flip(nc, k, c, h):
    hlsA, b_hls = c["hlsA"][0], c["hlsA"][1][h // 4]
    identT, b_id = c["identT"]
    qaug = c["qaug"]

    def flips(p):
        last = None
        for qc in range(4):
            last = p.transpose(k.pbF_t[:, 128 * qc:128 * (qc + 1)],
                               hlsA[:, 8 * h + 2 * qc:8 * h + 2 * qc + 2],
                               identT[:, :])
        return last
    k.emit("pe", flips, reads=[b_hls, b_id], writes=[k.pbF_b])
    k.emit("act", lambda a: a.activation(out=qaug[64:66, 512 * h:512 * (h + 1)],
                                         in_=k.pbF_t[:, :], func=AF.Copy),
           reads=[k.pbF_b], writes=[c["qaug_s"][h]])


def attn_pass2(nc, k, c, h):
    pbt, pb = k.pbanks, k.pbanks_bf
    qaug = c["qaug"]; kaug = c["kaug"]
    b_qd = c["qaug_d"][h]; b_qs = c["qaug_s"][h]
    b_kd = c["kaug_d"][h]; b_kp = c["kaug_p"][h]
    tri2T, b_tri2 = c["tri2T"]
    vtm, b_vtm = c["vtm"]
    pmvT, b_pmv = c["pmvT"]
    obank = h % 2
    atpm, b_atpm = c["atpm"][h % 2]

    def emit_sT(jc):
        if jc == "pm":
            k.emit("pe", lambda p: p.matmul(pbt[4][0:16, :],
                                            kaug[0:66, 528 * h + 512:528 * h + 528],
                                            qaug[0:66, 512 * h:512 * (h + 1)],
                                            start=True, stop=True),
                   reads=[b_kd, b_kp, b_qd, b_qs], writes=[pb[4]])
            k.emit("act", lambda a: a.activation(out=atpm[:, :], in_=pbt[4][0:16, :],
                                                 func=AF.Exp),
                   reads=[pb[4]], writes=[b_atpm])
        else:
            N = 512 - 128 * jc
            bi = 2 + (jc % 2)
            at, b_at = c["attnT"][jc % 3]
            k.emit("pe", lambda p, jc=jc, N=N, bi=bi: p.matmul(
                pbt[bi][:, 0:N],
                kaug[0:66, 528 * h + 128 * jc:528 * h + 128 * (jc + 1)],
                qaug[0:66, 512 * h + 128 * jc:512 * h + 512],
                start=True, stop=True),
                   reads=[b_kd, b_kp, b_qd, b_qs], writes=[pb[bi]])
            k.emit("dve", lambda v, bi=bi: v.tensor_tensor(
                pbt[bi][:, 0:128], pbt[bi][:, 0:128], tri2T[:, :], op=ALU.add),
                   reads=[b_tri2, pb[bi]], writes=[pb[bi]])
            k.emit("act", lambda a, N=N, bi=bi, at=at: a.activation(
                out=at[:, 0:N], in_=pbt[bi][:, 0:N], func=AF.Exp),
                   reads=[pb[bi]], writes=[b_at])

    def emit_av(jc):
        if jc == "pm":
            k.emit("pe", lambda p: p.matmul(pbt[obank][0:64, :],
                                            pmvT[:, 64 * h:64 * (h + 1)],
                                            atpm[:, :], start=True, stop=False),
                   reads=[b_pmv, b_atpm], writes=[pb[obank]])
        else:
            N = 512 - 128 * jc
            at, b_at = c["attnT"][jc % 3]
            k.emit("pe", lambda p, jc=jc, N=N, at=at: p.matmul(
                pbt[obank][0:64, 128 * jc:512],
                vtm[:, 1024 * jc + 64 * h:1024 * jc + 64 * (h + 1)],
                at[:, 0:N], start=False, stop=(jc == 3)),
                   reads=[b_vtm, b_at], writes=[pb[obank]])

    order = ["pm", 0, 1, 2, 3]
    prev = None
    for jc in order:
        emit_sT(jc)
        if prev is not None:
            emit_av(prev)
        prev = jc
    emit_av(prev)
    ofm, b_ofm = c["ofm"]
    hc, half = h // 2, h % 2
    k.emit("act", lambda a: a.activation(
        out=ofm[64 * half:64 * (half + 1), 512 * hc:512 * (hc + 1)],
        in_=pbt[obank][0:64, :], func=AF.Copy),
           reads=[pb[obank]], writes=[b_ofm])


STAGE = "full"
DUMP = None


def build_program():
    nc = bass.Bass("TRN2", target_bir_lowering=False)
    g = {}
    g["x"] = nc.dram_tensor("x", [D, T], F32, kind="ExternalInput").ap()
    g["y"] = nc.dram_tensor("y", [D, T], F32, kind="ExternalOutput").ap()
    g["xbuf"] = nc.dram_tensor("xbuf", [D, T], F32).ap()
    for nm, shp in [
        ("wqkv", [D, 3 * D]), ("wout", [D, D]),
        ("w1", [D, D]), ("w2", [D, 512]), ("w3", [512, 256]), ("w4", [256, 1]),
        ("gw1", [D, 512]), ("gw2", [512, 256]), ("gw3", [256, 1]),
        ("b1", [128, 8]), ("b2", [128, 4]), ("b3", [128, 2]), ("b4", [1, 1]),
        ("gb1", [128, 4]), ("gb2", [128, 2]), ("gb3", [1, 1]),
        ("ar", [1, 1]), ("gvec", [1, D]),
        ("pmk", [D, PMEM]), ("pmv", [PMEM, D]),
        ("cosk", [128, T]), ("ssk", [128, T]),
        ("tri1", [128, 128]), ("tri2", [128, 128]),
        ("ident", [128, 128]), ("perm", [128, 128]),
        ("onesrow", [1, 128]), ("ccmask", [128, 4]),
    ]:
        g[nm] = nc.dram_tensor(nm, shp, F32, kind="ExternalInput").ap()
    ccbufs = [(nc.dram_tensor("cc1i", [4, 1024], F32), nc.dram_tensor("cc1o", [4, 1024], F32)),
              (nc.dram_tensor("cc2i", [4, 1024], F32), nc.dram_tensor("cc2o", [4, 1024], F32))]

    stack = contextlib.ExitStack()
    sems = {}
    for snm in ENGS + QS + ("cc",):
        sems[snm] = stack.enter_context(nc.semaphore("sem_" + snm))

    def alloc(st, name, shape, dt):
        return st.enter_context(nc.sbuf_tensor(name, shape, dt))

    onesr = alloc(stack, "g_onesr", [1, 128], F32R)
    onescol = alloc(stack, "g_onescol", [128, 1], BF16)
    ccm = alloc(stack, "g_ccm", [128, 4], F32)
    epsA = alloc(stack, "g_epsA", [1, 1], F32)
    epsB = alloc(stack, "g_epsB", [1, 1], F32)
    pbanks = [stack.enter_context(nc.psum_tensor(f"pb{i}", [128, 512], F32))
              for i in range(7)]
    pbF = stack.enter_context(nc.psum_tensor("pbF", [2, 512], BF16))

    with nc.Block() as block:
        k = K(nc, sems)
        k.pbanks = pbanks
        k.pbanks_bf = [Bf() for _ in range(7)]
        k.pbF_t = pbF
        k.pbF_b = Bf()
        k.onesr_t = onesr
        k.onesr_b = Bf()
        k.onescol_t = onescol
        k.onescol_b = Bf()
        k.ccm_t = ccm
        k.ccm_b = Bf()
        k.epsA_t = epsA
        k.epsA_b = Bf()
        k.epsB_t = epsB
        k.epsB_b = Bf()
        k.rot = [0]
        k.dma("qsp", onesr[:, :], g["onesrow"].bitcast(F32R), writes=[k.onesr_b])
        k.dma("qsp", ccm[:, :], g["ccmask"], writes=[k.ccm_b])
        k.emit("dve", lambda v: v.memset(onescol[:, :], 1.0), writes=[k.onescol_b])
        k.emit("dve", lambda v: v.memset(epsA[:, :], EPS1), writes=[k.epsA_b])
        k.emit("dve", lambda v: v.memset(epsB[:, :], EPS1 * EPS2), writes=[k.epsB_b])
        st1 = contextlib.ExitStack()
        c1 = ffn_phase_open(nc, k, alloc, st1, "f1_")
        ffn_load(nc, k, g, c1)
        early = False
        if STAGE == "ld":
            k.dma("qsp", g["y"][0:128, 0:8], c1["b1c"][0][:, :], reads=[c1["b1c"][1]])
            early = True
        if not early:
            ffn_prepass(nc, k, g, c1, src=g["x"])
            ffn_load_gc(nc, k, g, c1)
            if STAGE == "prepass":
                for t in range(NT):
                    st_t, b_st = c1["states"][t]
                    for ch in range(NCH):
                        k.dma("qsp",
                              g["y"][128 * ch:128 * (ch + 1), 512 * t:512 * (t + 1)],
                              st_t[:, 512 * ch:512 * (ch + 1)], reads=[b_st])
                early = True
        if early:
            k.barrier()
            k.flush(block)
        else:
            ffn_cc_stage(nc, k, g, c1, ccbufs[0])
            k.flush(block)

            @block.gpsimd
            def _(gp):
                cc_run(nc, k, c1, ccbufs[0])

            if STAGE == "cc":
                k.dma("qsp", g["y"][0:128, 0:32], c1["gcr"][0][:, :],
                      reads=[c1["gcr"][1]])
                k.barrier()
                k.flush(block)
                early = True

        if early:
            STAGE2 = "done"
        else:
            STAGE2 = STAGE
        if STAGE2 != "done":
            gf1 = lambda: ffn_gf(nc, k, g, c1)
            if STAGE == "wc":
                gf1()
                dump = "h1" if DUMP is None else DUMP
                for t in range(NT):
                    ffn_wchain(nc, k, c1, t)
                    xio, b_xio = c1["xio"][t % 2]
                    if dump in ("h1", "h2", "h3"):
                        hsrc, b_h = c1[dump][t % 2]
                        k.emit("dve", lambda v, hsrc=hsrc, xio=xio: v.tensor_copy(
                            xio[:, 0:hsrc.shape[1]], hsrc[:, :]),
                               reads=[b_h], writes=[b_xio])
                        for ch in range(NCH):
                            k.dma("qsp",
                                  g["y"][128 * ch:128 * (ch + 1),
                                         512 * t:512 * (t + 1)],
                                  xio[:, 512 * ch:512 * (ch + 1)], reads=[b_xio])
                    elif dump == "tf":
                        tfrow, b_tf = c1["tfrow"][t % 2]
                        k.dma("qsp", g["y"][0:1, 512 * t:512 * (t + 1)],
                              tfrow[:, :], reads=[b_tf])
                    elif dump == "gf":
                        k.dma("qsp", g["y"][0:1, 0:1], c1["gfr"][0][:, :],
                              reads=[c1["gfr"][1]])
                        k.dma("qsp", g["y"][0:1, 1:2], c1["c0r"][0][:, :],
                              reads=[c1["c0r"][1]])
                        k.dma("qsp", g["y"][0:1, 2:3], c1["c1r"][0][:, :],
                              reads=[c1["c1r"][1]])
                STAGE2 = "ffn1"
            elif STAGE == "gfd":
                ffn_main(nc, k, g, c1, src=g["x"], dst=g["y"], cycles=(1,),
                         resid=False, gf_fn=gf1)
                k.dma("qsp", g["y"][0:1, 0:1], c1["gfr"][0][:, :],
                      reads=[c1["gfr"][1]])
                k.dma("qsp", g["y"][0:1, 1:2], c1["c0r"][0][:, :],
                      reads=[c1["c0r"][1]])
                k.dma("qsp", g["y"][0:1, 2:3], c1["c1r"][0][:, :],
                      reads=[c1["c1r"][1]])
                k.dma("qgp", g["y"][0:1, 3:8], c1["gcbf"][0][0:1, 0:5],
                      reads=[c1["gcbf"][1]])
                STAGE2 = "ffn1"
            elif STAGE in ("c1", "c2"):
                ffn_main(nc, k, g, c1, src=g["x"], dst=g["y"],
                         cycles=(1,) if STAGE == "c1" else (1, 2), resid=False,
                         gf_fn=gf1)
                for t in range(NT):
                    st_t, b_st = c1["states"][t]
                    for ch in range(NCH):
                        k.dma("qsp",
                              g["y"][128 * ch:128 * (ch + 1), 512 * t:512 * (t + 1)],
                              st_t[:, 512 * ch:512 * (ch + 1)], reads=[b_st])
                STAGE2 = "ffn1"
            else:
                ffn_main(nc, k, g, c1, src=g["x"],
                         dst=(g["y"] if STAGE == "ffn1" else g["xbuf"]),
                         gf_fn=gf1)
            k.barrier()
        st1.close()
        if STAGE2 not in ("done", "ffn1"):
            st2 = contextlib.ExitStack()
            c2 = attn_open(nc, k, alloc, st2)
            attn_load(nc, k, g, c2)
            for s in range(NT):
                attn_seg(nc, k, g, c2, s, dst=(g["y"] if STAGE == "attn"
                                               else g["xbuf"]))
            k.barrier()
            st2.close()
        if STAGE == "full":
            st3 = contextlib.ExitStack()
            c3 = ffn_phase_open(nc, k, alloc, st3, "f2_")
            ffn_load(nc, k, g, c3)
            ffn_prepass(nc, k, g, c3, src=g["xbuf"])
            ffn_load_gc(nc, k, g, c3)
            ffn_cc_stage(nc, k, g, c3, ccbufs[1])
            k.flush(block)

            @block.gpsimd
            def _(gp):
                cc_run(nc, k, c3, ccbufs[1])

            ffn_main(nc, k, g, c3, src=g["xbuf"], dst=g["y"],
                     gf_fn=lambda: ffn_gf(nc, k, g, c3))
            k.barrier()
            st3.close()
        k.flush(block)

    nc._keepalive = stack
    nc._kemit = k
    return nc


def _host_constants(half):
    f32 = np.float32
    inv = (1.0 / (10000.0 ** (np.arange(0, DH, 2, dtype=f32) / f32(DH)))).astype(f32)
    pos = np.arange(T * half, T * (half + 1), dtype=f32)
    th = (pos[None, :] * inv[:, None]).astype(f32)
    cos = np.cos(th).astype(f32)
    sin = np.sin(th).astype(f32)
    cos64 = np.repeat(cos, 2, axis=0)
    ss64 = np.repeat(sin, 2, axis=0)
    ss64[0::2] *= -1.0
    cos128 = np.concatenate([cos64, cos64], axis=0)
    ss128 = np.concatenate([ss64, ss64], axis=0)
    return np.ascontiguousarray(cos128), np.ascontiguousarray(ss128)


_PROG_CACHE = {}


def kernel(**inputs):
    if "nc" not in _PROG_CACHE:
        _PROG_CACHE["nc"] = build_program()
    nc = _PROG_CACHE["nc"]

    f32 = np.float32
    xin = np.asarray(inputs["L0_attn_seq"], dtype=f32)
    wqkv = np.ascontiguousarray(np.asarray(inputs["L0_attn_w_qkv"], f32).T)
    wout = np.ascontiguousarray(np.asarray(inputs["L0_attn_w_out"], f32).T)
    gvec = np.ascontiguousarray(np.asarray(inputs["L0_attn_g"], f32).reshape(1, D))
    pm = np.asarray(inputs["L0_attn_pm"], f32)
    pmk = np.ascontiguousarray(pm[0].transpose(0, 2, 1).reshape(D, PMEM))
    pmv = np.ascontiguousarray(pm[1].transpose(1, 0, 2).reshape(PMEM, D))
    w1 = np.ascontiguousarray(np.asarray(inputs["L0_ffn_cn_w1"], f32))
    w2 = np.ascontiguousarray(np.asarray(inputs["L0_ffn_cn_w2"], f32))
    w3 = np.ascontiguousarray(np.asarray(inputs["L0_ffn_cn_w3"], f32))
    w4 = np.ascontiguousarray(np.asarray(inputs["L0_ffn_cn_w4"], f32))
    gw1 = np.ascontiguousarray(np.asarray(inputs["L0_ffn_gc_w1"], f32))
    gw2 = np.ascontiguousarray(np.asarray(inputs["L0_ffn_gc_w2"], f32))
    gw3 = np.ascontiguousarray(np.asarray(inputs["L0_ffn_gc_w3"], f32))
    b1 = np.ascontiguousarray(np.asarray(inputs["L0_ffn_cn_b1"], f32).reshape(8, 128).T)
    b2 = np.ascontiguousarray(np.asarray(inputs["L0_ffn_cn_b2"], f32).reshape(4, 128).T)
    b3 = np.ascontiguousarray(np.asarray(inputs["L0_ffn_cn_b3"], f32).reshape(2, 128).T)
    b4 = np.asarray(inputs["L0_ffn_cn_b4"], f32).reshape(1, 1)
    gb1 = np.ascontiguousarray(np.asarray(inputs["L0_ffn_gc_b1"], f32).reshape(4, 128).T)
    gb2 = np.ascontiguousarray(np.asarray(inputs["L0_ffn_gc_b2"], f32).reshape(2, 128).T)
    gb3 = np.asarray(inputs["L0_ffn_gc_b3"], f32).reshape(1, 1)
    ar = np.asarray(inputs["L0_ffn_adaptive_range"], f32).reshape(1, 1)

    tri1 = np.triu(np.full((128, 128), NEG, dtype=f32), k=1)
    tri2 = np.ascontiguousarray(tri1.T)
    ident = np.eye(128, dtype=f32)
    perm = np.zeros((128, 128), dtype=f32)
    idx = np.arange(128)
    perm[idx, idx ^ 1] = 1.0
    onesrow = np.ones((1, 128), dtype=f32)

    in_maps = []
    for core in range(8):
        b, half = core // 2, core % 2
        cosk, ssk = _host_constants(half)
        ccmask = np.zeros((128, 4), dtype=f32)
        ccmask[:, b] = 1.0
        xcore = np.ascontiguousarray(xin[b, T * half:T * (half + 1), :].T)
        in_maps.append({
            "x": xcore, "wqkv": wqkv, "wout": wout, "gvec": gvec,
            "pmk": pmk, "pmv": pmv,
            "w1": w1, "w2": w2, "w3": w3, "w4": w4,
            "gw1": gw1, "gw2": gw2, "gw3": gw3,
            "b1": b1, "b2": b2, "b3": b3, "b4": b4,
            "gb1": gb1, "gb2": gb2, "gb3": gb3, "ar": ar,
            "cosk": cosk, "ssk": ssk,
            "tri1": tri1, "tri2": tri2, "ident": ident, "perm": perm,
            "onesrow": onesrow, "ccmask": ccmask,
        })

    res = run_bass_kernel_spmd(nc, in_maps, core_ids=list(range(8)))
    out = np.empty((B, S, D), dtype=np.float32)
    for core in range(8):
        b, half = core // 2, core % 2
        out[b, T * half:T * (half + 1), :] = res.results[core]["y"].T
    return out


# version 14 (final)
# speedup vs baseline: 1.1039x; 1.0073x over previous
import contextlib
import numpy as np
import concourse.bass as bass
import concourse.mybir as mybir
from concourse.bass_utils import run_bass_kernel_spmd

F32 = mybir.dt.float32
F32R = mybir.dt.float32r
BF16 = mybir.dt.bfloat16
AF = mybir.ActivationFunctionType
ALU = mybir.AluOpType
AX = mybir.AxisListType

B, S, D = 4, 4096, 1024
T = 2048
NT = 4
SEG = 512
PMEM = 16
HEADS = 16
DH = 64
NCH = 8
EPS1 = 1e-5
EPS2 = 1.1920929e-07
NEG = -1.0e30
BASE_C = 0.1

ENGS = ("pe", "act", "dve", "sp", "gp")
QS = ("qsp", "qgp")


class Bf:
    __slots__ = ("w", "r")

    def __init__(self):
        self.w = None
        self.r = []


class K:

    def __init__(self, nc, sems):
        self.nc = nc
        self.sems = sems
        self.h = {"pe": nc.tensor, "act": nc.scalar, "dve": nc.vector,
                  "sp": nc.sync, "gp": nc.gpsimd}
        self.cnt = {k: 0 for k in ENGS + QS}
        self.cnt["cc"] = 0
        self.seen = {e: {} for e in ENGS}
        self.streams = {e: [] for e in ENGS}
        self.marks = []
        self._scope = "init"

    @property
    def cur_scope(self):
        return self._scope

    @cur_scope.setter
    def cur_scope(self, v):
        if v != self._scope:
            self.marks.append((v, dict(self.cnt)))
        self._scope = v

    def _waitlist(self, e, deps):
        best = {}
        for (kk, v) in deps:
            if kk == e and e == "pe":
                continue
            if best.get(kk, 0) < v:
                best[kk] = v
        out = []
        for kk, v in best.items():
            if kk == e:
                if self.seen[e].get(e, 0) >= v:
                    continue
                out.append((e, v))
                self.seen[e][e] = v
                continue
            if self.seen[e].get(kk, 0) >= v:
                continue
            out.append((kk, v))
            self.seen[e][kk] = v
        return out

    @staticmethod
    def _deps(reads, writes):
        deps = []
        for b in reads:
            if b.w:
                deps.append(b.w)
        for b in writes:
            if b.w:
                deps.append(b.w)
            deps.extend(b.r)
        return deps

    def emit(self, e, fn, reads=(), writes=()):
        waits = self._waitlist(e, self._deps(reads, writes))
        self.cnt[e] += 1
        self.streams[e].append((waits, fn, (e, 1), self.cur_scope))
        evt = (e, self.cnt[e])
        for b in reads:
            b.r.append(evt)
        for b in writes:
            b.w = evt
            b.r = []
        return evt

    def dma(self, q, out, in_, reads=(), writes=()):
        e = "sp" if q == "qsp" else "gp"
        waits = self._waitlist(e, self._deps(reads, writes))
        self.cnt[q] += 16
        self.streams[e].append(
            (waits, lambda h, out=out, in_=in_: h.dma_start(out=out, in_=in_),
             (q, 16), self.cur_scope))
        evt = (q, self.cnt[q])
        for b in reads:
            b.r.append(evt)
        for b in writes:
            b.w = evt
            b.r = []
        return evt

    def barrier(self):
        for e in ENGS:
            waits = []
            for kk in ENGS:
                if kk != e and self.seen[e].get(kk, 0) < self.cnt[kk]:
                    waits.append((kk, self.cnt[kk]))
                    self.seen[e][kk] = self.cnt[kk]
            for q in QS:
                if self.cnt[q] and self.seen[e].get(q, 0) < self.cnt[q]:
                    waits.append((q, self.cnt[q]))
                    self.seen[e][q] = self.cnt[q]
            self.streams[e].append((waits, None, None, self.cur_scope))

    def flush(self, block):
        secs = {"pe": block.tensor, "act": block.scalar, "dve": block.vector,
                "sp": block.sync, "gp": block.gpsimd}
        for e in ENGS:
            ops = self.streams[e]
            if not ops:
                continue

            def body(h, ops=ops):
                import contextlib as _ctx
                stck = _ctx.ExitStack()
                open_scope = None
                for waits, fn, inc, scope in ops:
                    if scope != open_scope:
                        stck.close()
                        stck = _ctx.ExitStack()
                        stck.enter_context(self.nc.named_scope(scope))
                        open_scope = scope
                    for (kk, v) in waits:
                        h.wait_ge(self.sems[kk], v)
                    if fn is not None:
                        last = fn(h)
                        if inc is not None:
                            last.then_inc(self.sems[inc[0]], inc[1])
                stck.close()
            secs[e](body)
            self.streams[e] = []


def ln_rows(k, c, pbS, pbX, t, combine_g):
    rows, b_rows = c["rows"]
    r0 = rows[:, 0:512]; r1 = rows[:, 512:1024]
    r2 = rows[:, 1024:1536]; r3 = rows[:, 1536:2048]
    k.emit("dve", lambda v: v.tensor_scalar(r0, pbS[0][0:1, :], 1.0 / D, None,
                                            op0=ALU.mult),
           reads=[pbS[1]], writes=[b_rows])
    k.emit("dve", lambda v: v.tensor_scalar(r1, pbX[0][0:1, :], 1.0 / D, None,
                                            op0=ALU.mult),
           reads=[pbX[1]], writes=[b_rows])
    k.emit("dve", lambda v: v.tensor_tensor(r2, r0, r0, op=ALU.mult),
           reads=[b_rows], writes=[b_rows])
    k.emit("dve", lambda v: v.tensor_tensor(r1, r1, r2, op=ALU.subtract),
           reads=[b_rows], writes=[b_rows])
    if combine_g:
        k.emit("act", lambda a: a.activation(out=r2, in_=r1, func=AF.Sqrt,
                                             bias=k.epsB_t[:, :], scale=1.0 + EPS2),
               reads=[b_rows, k.epsB_b], writes=[b_rows])
        scale = 1.0
    else:
        k.emit("act", lambda a: a.activation(out=r2, in_=r1, func=AF.Sqrt,
                                             bias=k.epsA_t[:, :], scale=1.0),
               reads=[b_rows, k.epsA_b], writes=[b_rows])
        scale = 0.5
    k.emit("dve", lambda v: v.reciprocal(r3, r2), reads=[b_rows], writes=[b_rows])
    k.emit("dve", lambda v: v.tensor_scalar(r1, r3, scale, None, op0=ALU.mult),
           reads=[b_rows], writes=[b_rows])
    k.emit("dve", lambda v: v.tensor_tensor(r2, r0, r1, op=ALU.mult),
           reads=[b_rows], writes=[b_rows])
    k.emit("dve", lambda v: v.tensor_copy(c["rh_r"][t][0][:, :], r1),
           reads=[b_rows], writes=[c["rh_r"][t][1]])
    k.emit("dve", lambda v: v.tensor_copy(c["nu_r"][t][0][:, :], r2),
           reads=[b_rows], writes=[c["nu_r"][t][1]])


def ln_stats(k, xio, b_xio, xbf, b_xbf, sqbf, b_sq, pbS, pbX, f32r=False):
    if not f32r:
        k.emit("dve", lambda v: v.tensor_copy(xbf[:, :], xio[:, :]),
               reads=[b_xio], writes=[b_xbf])
    k.emit("act", lambda a: a.activation(
        out=sqbf[:, :], in_=(xio[:, :].bitcast(F32) if f32r else xio[:, :]),
        func=AF.Square),
           reads=[b_xio], writes=[b_sq])

    def mk(src):
        def mm(p):
            last = None
            for ch in range(NCH):
                last = p.matmul(pbS[0][0:1, :] if src is xbf else pbX[0][0:1, :],
                                k.onescol_t[:, :], src[:, 512 * ch:512 * (ch + 1)],
                                start=(ch == 0), stop=(ch == NCH - 1))
            return last
        return mm
    if f32r:
        def mmx(p):
            last = None
            for ch in range(NCH):
                last = p.matmul(pbS[0][0:1, :], k.onescolr_t[:, :],
                                xio[:, 512 * ch:512 * (ch + 1)],
                                start=(ch == 0), stop=(ch == NCH - 1))
            return last
        k.emit("pe", mmx, reads=[b_xio, k.onescolr_b], writes=[pbS[1]])
    else:
        k.emit("pe", mk(xbf), reads=[b_xbf, k.onescol_b], writes=[pbS[1]])
    k.emit("pe", mk(sqbf), reads=[b_sq, k.onescol_b], writes=[pbX[1]])


def ln_bcast(k, c, t, pbA, pbB):
    k.emit("pe", lambda p: p.matmul(pbA[0][:, :], k.onesr_t[:, :],
                                    c["rh_r"][t][0][:, :], start=True, stop=True),
           reads=[c["rh_r"][t][1], k.onesr_b], writes=[pbA[1]])
    k.emit("pe", lambda p: p.matmul(pbB[0][:, :], k.onesr_t[:, :],
                                    c["nu_r"][t][0][:, :], start=True, stop=True),
           reads=[c["nu_r"][t][1], k.onesr_b], writes=[pbB[1]])


def ln_apply(k, c, xio, b_xio, dst, b_dst, pbA, pbB, f32r=False):
    tmp, b_tmp = c["tmp"]
    for ch in range(NCH):
        sl = slice(512 * ch, 512 * (ch + 1))
        xsl = xio[:, sl].bitcast(F32) if f32r else xio[:, sl]
        ts = tmp[:, 512 * (ch % 2):512 * (ch % 2 + 1)]
        k.emit("dve", lambda v, xsl=xsl, ts=ts: v.tensor_tensor(ts, xsl,
                                                                pbA[0][:, :], op=ALU.mult),
               reads=[b_xio, pbA[1]], writes=[b_tmp])
        k.emit("dve", lambda v, sl=sl, ts=ts: v.tensor_tensor(dst[:, sl], ts,
                                                              pbB[0][:, :], op=ALU.subtract),
               reads=[b_tmp, pbB[1]], writes=[b_dst])


def ffn_phase_open(nc, k, alloc, st, tag):
    c = {}

    def A(name, shape, dt):
        t = alloc(st, tag + name, shape, dt)
        b = Bf()
        c[name] = (t, b)
        return t
    for nm, shp, dt in [
        ("w1s", [128, 8 * 1024], BF16), ("w2s", [128, 8 * 512], BF16),
        ("w3s", [128, 4 * 256], BF16), ("w4s", [128, 2], BF16),
        ("b1c", [128, 8], F32), ("b2c", [128, 4], F32), ("b3c", [128, 2], F32),
        ("b4c", [1, 1], F32), ("gb1c", [128, 4], F32), ("gb2c", [128, 2], F32),
        ("gb3c", [1, 1], F32), ("arr", [1, 1], F32),
        ("tmp", [128, 2048], F32),
        ("gcacc", [128, 8], F32), ("gcs", [128, 32], F32), ("gcr", [128, 32], F32),
        ("gcsum", [128, 8], F32), ("gcbf", [128, 8], BF16),
        ("g1h", [128, 4], BF16), ("g2h", [128, 2], BF16),
        ("gfr", [1, 1], F32), ("c0r", [1, 1], F32), ("c1r", [1, 1], F32),
        ("rows", [1, 2048], F32),
    ]:
        A(nm, shp, dt)
    c["states"] = [(alloc(st, tag + f"stt{i}", [128, 4096], F32), Bf())
                   for i in range(NT)]
    c["sbf"] = [(alloc(st, tag + f"sbf{i}", [128, 4096], BF16), Bf()) for i in range(2)]
    c["h1"] = [(alloc(st, tag + f"h1_{i}", [128, 4096], BF16), Bf()) for i in range(2)]
    c["h2"] = [(alloc(st, tag + f"h2_{i}", [128, 2048], BF16), Bf()) for i in range(2)]
    c["h3"] = [(alloc(st, tag + "h3_0", [128, 1024], BF16), Bf())] * 2
    c["xio"] = [(alloc(st, tag + f"xio{i}", [128, 4096],
                   F32 if i == 0 else F32R), Bf()) for i in range(2)]
    c["stage"] = c["xio"][1]
    c["xbf"] = c["sbf"][0]
    c["sqbf"] = c["sbf"][1]
    xv = c["xio"][0][0][:, :].bitcast(BF16)
    c["g1s"] = (xv[:, 0:4096], c["xio"][0][1])
    c["g2s"] = (xv[:, 4096:5122], c["xio"][0][1])
    c["g3s"] = (xv[:, 5122:5124], c["xio"][0][1])
    c["rh_r"] = [(alloc(st, tag + f"rh{i}", [1, 512], F32R), Bf()) for i in range(NT)]
    c["nu_r"] = [(alloc(st, tag + f"nu{i}", [1, 512], F32R), Bf()) for i in range(NT)]
    c["crow_r"] = [(alloc(st, tag + f"cr{i}", [1, 512], F32R), Bf()) for i in range(2)]
    c["tfrow"] = [(alloc(st, tag + f"tf{i}", [1, 512], F32), Bf()) for i in range(2)]
    return c


def ffn_load(nc, k, g, c):
    k.cur_scope = "wload"
    for nm, src, ncols in [("w1s", g["w1"], 1024), ("w2s", g["w2"], 512),
                           ("w3s", g["w3"], 256), ("w4s", g["w4"], 1)]:
        t, bf = c[nm]
        nkc = src.shape[0] // 128
        for kc in range(nkc):
            k.dma("qgp", t[:, ncols * kc:ncols * (kc + 1)],
                  src[128 * kc:128 * (kc + 1), :], writes=[bf])
    for nm, src in [("b1c", g["b1"]), ("b2c", g["b2"]), ("b3c", g["b3"]),
                    ("b4c", g["b4"]), ("gb1c", g["gb1"]), ("gb2c", g["gb2"]),
                    ("gb3c", g["gb3"]), ("arr", g["ar"])]:
        t, bf = c[nm]
        k.dma("qsp", t[:, :], src, writes=[bf])
    k.emit("dve", lambda v: v.memset(c["gcacc"][0][:, :], 0.0), writes=[c["gcacc"][1]])


def ffn_load_gc(nc, k, g, c):
    k.cur_scope = "wload"
    for nm, src, ncols in [("g1s", g["gw1"], 512), ("g2s", g["gw2"], 256),
                           ("g3s", g["gw3"], 1)]:
        t, bf = c[nm]
        nkc = src.shape[0] // 128
        for kc in range(nkc):
            k.dma("qgp", t[:, ncols * kc:ncols * (kc + 1)],
                  src[128 * kc:128 * (kc + 1), :], writes=[bf])


def ffn_prepass(nc, k, g, c, src):
    k.cur_scope = "prepass"
    pbt, pb = k.pbanks, k.pbanks_bf
    gcacc, b_gcacc = c["gcacc"]
    xbf, b_xbf = c["xbf"]
    sqbf, b_sq = c["sqbf"]

    def banks(t):
        return ((pbt[4], pb[4]), (pbt[5], pb[5])) if t % 2 == 0 else \
               ((pbt[2], pb[2]), (pbt[3], pb[3]))

    def pstats(t):
        xio, b_xio = c["xio"][t % 2]
        odd = (t % 2 == 1)
        for ch in range(NCH):
            sb = src[128 * ch:128 * (ch + 1), 512 * t:512 * (t + 1)]
            k.dma("qsp", xio[:, 512 * ch:512 * (ch + 1)],
                  sb.bitcast(F32R) if odd else sb, writes=[b_xio])
        pbS, pbX = banks(t)
        ln_stats(k, xio, b_xio, xbf, b_xbf, sqbf, b_sq, pbS, pbX, f32r=odd)

    def ptail(t):
        pbS, pbX = banks(t)
        ln_rows(k, c, pbS, pbX, t, combine_g=False)
        ln_bcast(k, c, t, pbS, pbX)
        xio, b_xio = c["xio"][t % 2]
        state_t, b_state = c["states"][t]
        ln_apply(k, c, xio, b_xio, state_t, b_state, pbS, pbX, f32r=(t % 2 == 1))
        gcs, b_gcs = c["gcs"]
        k.emit("dve", lambda v, state_t=state_t: v.reduce_sum(
            gcs[:, 0:8], state_t[:, :].rearrange("p (c t) -> p c t", c=8), axis=AX.X),
               reads=[b_state], writes=[b_gcs])
        k.emit("dve", lambda v: v.tensor_tensor(gcacc[:, :], gcacc[:, :], gcs[:, 0:8],
                                                op=ALU.add),
               reads=[b_gcs, b_gcacc], writes=[b_gcacc])

    pstats(0)
    pstats(1)
    ptail(0)
    pstats(2)
    ptail(1)
    pstats(3)
    ptail(2)
    ptail(3)


def ffn_cc_stage(nc, k, g, c, ccb):
    gcacc, b_gcacc = c["gcacc"]
    gcs, b_gcs = c["gcs"]
    cin, _ = ccb
    cin3 = cin.ap().rearrange("s (p c) -> s p c", p=128)
    for s in range(4):
        k.emit("dve", lambda v, s=s: v.tensor_scalar(
            gcs[:, 8 * s:8 * (s + 1)], gcacc[:, :], k.ccm_t[:, s:s + 1], None,
            op0=ALU.mult),
               reads=[b_gcacc, k.ccm_b], writes=[b_gcs])
    for s in range(4):
        k.dma("qsp", cin3[s], gcs[:, 8 * s:8 * (s + 1)], reads=[b_gcs])


CC_ENABLE = True


def cc_run(nc, k, c, ccb):
    cin, cout = ccb
    v = k.cnt["qsp"]
    if k.seen["gp"].get("qsp", 0) < v:
        nc.gpsimd.wait_ge(k.sems["qsp"], v)
        k.seen["gp"]["qsp"] = v
    if CC_ENABLE:
        nc.gpsimd.collective_compute(
            "AllReduce", ALU.add,
            replica_groups=[[0, 1, 2, 3, 4, 5, 6, 7]],
            ins=[cin.ap().opt()], outs=[cout.ap().opt()],
        ).then_inc(k.sems["cc"])
        k.cnt["cc"] += 1
        nc.gpsimd.wait_ge(k.sems["cc"], k.cnt["cc"])
        rdbuf = cout
    else:
        rdbuf = cin
    gcr, b_gcr = c["gcr"]
    cout3 = rdbuf.ap().rearrange("s (p c) -> s p c", p=128)
    for s in range(4):
        nc.gpsimd.dma_start(out=gcr[:, 8 * s:8 * (s + 1)], in_=cout3[s]).then_inc(
            k.sems["qgp"], 16)
        k.cnt["qgp"] += 16
    b_gcr.w = ("qgp", k.cnt["qgp"])
    b_gcr.r = []


def ffn_gf(nc, k, g, c):
    k.cur_scope = "gf"
    pbt, pb = k.pbanks, k.pbanks_bf
    gcr, b_gcr = c["gcr"]
    gcsum, b_gcsum = c["gcsum"]
    gcbf, b_gcbf = c["gcbf"]
    k.emit("dve", lambda v: v.memset(gcsum[:, :], 0.0), writes=[b_gcsum])
    for s in range(4):
        k.emit("dve", lambda v, s=s: v.scalar_tensor_tensor(
            gcsum[:, :], gcr[:, 8 * s:8 * (s + 1)], k.ccm_t[:, s:s + 1], gcsum[:, :],
            op0=ALU.mult, op1=ALU.add),
               reads=[b_gcr, k.ccm_b, b_gcsum], writes=[b_gcsum])
    k.emit("dve", lambda v: v.tensor_scalar(gcbf[:, :], gcsum[:, :], 1.0 / 2048.0,
                                            None, op0=ALU.mult),
           reads=[b_gcsum], writes=[b_gcbf])
    g1s, b_g1s = c["g1s"]; g2s, b_g2s = c["g2s"]; g3s, b_g3s = c["g3s"]
    g1h, b_g1h = c["g1h"]; g2h, b_g2h = c["g2h"]
    gb1c, b_gb1 = c["gb1c"]; gb2c, b_gb2 = c["gb2c"]; gb3c, b_gb3 = c["gb3c"]
    for oc in range(4):
        def mm(p, oc=oc):
            last = None
            for kc in range(8):
                last = p.matmul(pbt[6][:, 0:1],
                                g1s[:, 512 * kc + 128 * oc:512 * kc + 128 * oc + 128],
                                gcbf[:, kc:kc + 1], start=(kc == 0), stop=(kc == 7))
            return last
        k.emit("pe", mm, reads=[b_g1s, b_gcbf], writes=[pb[6]])
        k.emit("act", lambda a, oc=oc: a.activation(
            out=g1h[:, oc:oc + 1], in_=pbt[6][:, 0:1], func=AF.Gelu,
            bias=gb1c[:, oc:oc + 1]),
               reads=[pb[6], b_gb1], writes=[b_g1h])
    for oc in range(2):
        def mm(p, oc=oc):
            last = None
            for kc in range(4):
                last = p.matmul(pbt[6][:, 0:1],
                                g2s[:, 256 * kc + 128 * oc:256 * kc + 128 * oc + 128],
                                g1h[:, kc:kc + 1], start=(kc == 0), stop=(kc == 3))
            return last
        k.emit("pe", mm, reads=[b_g2s, b_g1h], writes=[pb[6]])
        k.emit("act", lambda a, oc=oc: a.activation(
            out=g2h[:, oc:oc + 1], in_=pbt[6][:, 0:1], func=AF.Gelu,
            bias=gb2c[:, oc:oc + 1]),
               reads=[pb[6], b_gb2], writes=[b_g2h])

    def mm3(p):
        last = None
        for kc in range(2):
            last = p.matmul(pbt[6][0:1, 0:1], g3s[:, kc:kc + 1], g2h[:, kc:kc + 1],
                            start=(kc == 0), stop=(kc == 1))
        return last
    k.emit("pe", mm3, reads=[b_g3s, b_g2h], writes=[pb[6]])
    gfr, b_gfr = c["gfr"]
    k.emit("act", lambda a: a.activation(out=gfr[:, :], in_=pbt[6][0:1, 0:1],
                                         func=AF.Sigmoid, bias=gb3c[:, :]),
           reads=[pb[6], b_gb3], writes=[b_gfr])
    arr, b_arr = c["arr"]
    c0r, b_c0 = c["c0r"]; c1r, b_c1 = c["c1r"]
    rows, b_rows = c["rows"]
    tt = rows[:, 1536:1537]
    k.emit("dve", lambda v: v.tensor_scalar(c1r[:, :], arr[:, :], 0.6, None,
                                            op0=ALU.mult),
           reads=[b_arr], writes=[b_c1])
    k.emit("dve", lambda v: v.tensor_scalar(tt, gfr[:, :], 1.4, -1.0,
                                            op0=ALU.mult, op1=ALU.add),
           reads=[b_gfr], writes=[b_rows])
    k.emit("dve", lambda v: v.tensor_tensor(c0r[:, :], tt, arr[:, :], op=ALU.mult),
           reads=[b_rows, b_arr], writes=[b_c0])
    k.emit("dve", lambda v: v.tensor_scalar(c0r[:, :], c0r[:, :], 1.0, BASE_C,
                                            op0=ALU.mult, op1=ALU.add),
           reads=[b_c0], writes=[b_c0])


def ffn_cast(nc, k, c, t):
    k.cur_scope = "fmain"
    sbf, b_sbf = c["sbf"][t % 2]
    state_t, b_state = c["states"][t]
    k.emit("dve", lambda v: v.tensor_scalar(sbf[:, :], state_t[:, :], 2.0, None,
                                            op0=ALU.mult),
           reads=[b_state], writes=[b_sbf])


def ffn_wchain(nc, k, c, t):
    pbt, pb = k.pbanks, k.pbanks_bf
    sbf, b_sbf = c["sbf"][t % 2]
    h1, b_h1 = c["h1"][t % 2]; h2, b_h2 = c["h2"][t % 2]; h3, b_h3 = c["h3"][t % 2]
    w1s, b_w1 = c["w1s"]; w2s, b_w2 = c["w2s"]; w3s, b_w3 = c["w3s"]; w4s, b_w4 = c["w4s"]
    b1c, bb1 = c["b1c"]; b2c, bb2 = c["b2c"]; b3c, bb3 = c["b3c"]; b4c, bb4 = c["b4c"]
    for oc in range(8):
        bi = k.rot[0]; k.rot[0] = (k.rot[0] + 1) % 4

        def mm(p, oc=oc, bi=bi):
            last = None
            for kc in range(8):
                last = p.matmul(pbt[bi][:, :],
                                w1s[:, 1024 * kc + 128 * oc:1024 * kc + 128 * oc + 128],
                                sbf[:, 512 * kc:512 * (kc + 1)],
                                start=(kc == 0), stop=(kc == 7))
            return last
        k.emit("pe", mm, reads=[b_w1, b_sbf], writes=[pb[bi]])
        k.emit("act", lambda a, oc=oc, bi=bi: a.activation(
            out=h1[:, 512 * oc:512 * (oc + 1)], in_=pbt[bi][:, :], func=AF.Gelu,
            bias=b1c[:, oc:oc + 1]),
               reads=[pb[bi], bb1], writes=[b_h1])
    for oc in range(4):
        bi = k.rot[0]; k.rot[0] = (k.rot[0] + 1) % 4

        def mm(p, oc=oc, bi=bi):
            last = None
            for kc in range(8):
                last = p.matmul(pbt[bi][:, :],
                                w2s[:, 512 * kc + 128 * oc:512 * kc + 128 * oc + 128],
                                h1[:, 512 * kc:512 * (kc + 1)],
                                start=(kc == 0), stop=(kc == 7))
            return last
        k.emit("pe", mm, reads=[b_w2, b_h1], writes=[pb[bi]])
        k.emit("act", lambda a, oc=oc, bi=bi: a.activation(
            out=h2[:, 512 * oc:512 * (oc + 1)], in_=pbt[bi][:, :], func=AF.Gelu,
            bias=b2c[:, oc:oc + 1]),
               reads=[pb[bi], bb2], writes=[b_h2])
    for oc in range(2):
        bi = k.rot[0]; k.rot[0] = (k.rot[0] + 1) % 4

        def mm(p, oc=oc, bi=bi):
            last = None
            for kc in range(4):
                last = p.matmul(pbt[bi][:, :],
                                w3s[:, 256 * kc + 128 * oc:256 * kc + 128 * oc + 128],
                                h2[:, 512 * kc:512 * (kc + 1)],
                                start=(kc == 0), stop=(kc == 3))
            return last
        k.emit("pe", mm, reads=[b_w3, b_h2], writes=[pb[bi]])
        k.emit("act", lambda a, oc=oc, bi=bi: a.activation(
            out=h3[:, 512 * oc:512 * (oc + 1)], in_=pbt[bi][:, :], func=AF.Gelu,
            bias=b3c[:, oc:oc + 1]),
               reads=[pb[bi], bb3], writes=[b_h3])

    def mm4(p):
        last = None
        for kc in range(2):
            last = p.matmul(pbt[6][0:1, :], w4s[:, kc:kc + 1],
                            h3[:, 512 * kc:512 * (kc + 1)],
                            start=(kc == 0), stop=(kc == 1))
        return last
    k.emit("pe", mm4, reads=[b_w4, b_h3], writes=[pb[6]])
    tfrow, b_tf = c["tfrow"][t % 2]
    k.emit("act", lambda a: a.activation(out=tfrow[:, :], in_=pbt[6][0:1, :],
                                         func=AF.Sigmoid, bias=b4c[:, :]),
           reads=[pb[6], bb4], writes=[b_tf])


TWIST_GROUPS = [
    [(0, 4, ALU.add), (1, 5, ALU.add), (4, 0, ALU.subtract), (5, 1, ALU.subtract)],
    [(2, 6, ALU.subtract), (3, 7, ALU.subtract), (6, 2, ALU.add), (7, 3, ALU.add)],
]


def ffn_update(nc, k, c, t, cyc):
    pbt, pb = k.pbanks, k.pbanks_bf
    state_t, b_state = c["states"][t]
    tmp, b_tmp = c["tmp"]
    tfrow, b_tf = c["tfrow"][t % 2]
    crow, b_crow = c["crow_r"][t % 2]
    c0r, b_c0 = c["c0r"]; c1r, b_c1 = c["c1r"]
    rows, b_rows = c["rows"]
    cr = rows[:, 1536:2048]
    k.emit("dve", lambda v: v.tensor_scalar(cr, tfrow[:, :], c1r[:, :], c0r[:, :],
                                            op0=ALU.mult, op1=ALU.add),
           reads=[b_tf, b_c0, b_c1], writes=[b_rows])
    k.emit("dve", lambda v: v.tensor_copy(crow[:, :], cr),
           reads=[b_rows], writes=[b_crow])
    k.emit("pe", lambda p: p.matmul(pbt[5][:, :], k.onesr_t[:, :], crow[:, :],
                                    start=True, stop=True),
           reads=[b_crow, k.onesr_b], writes=[pb[5]])
    for grp in TWIST_GROUPS:
        for i, (dst, src, op) in enumerate(grp):
            k.emit("dve", lambda v, i=i, src=src: v.tensor_tensor(
                tmp[:, 512 * i:512 * (i + 1)], state_t[:, 512 * src:512 * (src + 1)],
                pbt[5][:, :], op=ALU.mult),
                   reads=[b_state, pb[5]], writes=[b_tmp])
        for i, (dst, src, op) in enumerate(grp):
            k.emit("dve", lambda v, i=i, dst=dst, op=op: v.tensor_tensor(
                state_t[:, 512 * dst:512 * (dst + 1)],
                state_t[:, 512 * dst:512 * (dst + 1)],
                tmp[:, 512 * i:512 * (i + 1)], op=op),
                   reads=[b_tmp, b_state], writes=[b_state])


def ffn_resid_read(nc, k, g, c, t, src):
    xio, b_xio = c["xio"][t % 2]
    odd = (t % 2 == 1)
    for ch in range(NCH):
        sb = src[128 * ch:128 * (ch + 1), 512 * t:512 * (t + 1)]
        k.dma("qsp", xio[:, 512 * ch:512 * (ch + 1)],
              sb.bitcast(F32R) if odd else sb, writes=[b_xio])


def ffn_resid_add(nc, k, g, c, t, dst):
    k.cur_scope = "fresid"
    xio, b_xio = c["xio"][t % 2]
    state_t, b_state = c["states"][t]
    xin = xio[:, :].bitcast(F32) if t % 2 == 1 else xio[:, :]
    k.emit("dve", lambda v: v.tensor_tensor(
        state_t[:, :], xin, state_t[:, :], op=ALU.add),
           reads=[b_state, b_xio], writes=[b_state])
    for ch in range(NCH):
        k.dma("qgp", dst[128 * ch:128 * (ch + 1), 512 * t:512 * (t + 1)],
              state_t[:, 512 * ch:512 * (ch + 1)], reads=[b_state])


def ffn_main(nc, k, g, c, src, dst, cycles=(1, 2, 3), resid=True, gf_fn=None):
    for cyc in cycles:
        last = (cyc == 3 and resid)
        if last:
            ffn_resid_read(nc, k, g, c, 0, src)
            ffn_resid_read(nc, k, g, c, 1, src)
        ffn_cast(nc, k, c, 0)
        ffn_cast(nc, k, c, 1)
        ffn_wchain(nc, k, c, 0)
        ffn_cast(nc, k, c, 2)
        ffn_wchain(nc, k, c, 1)
        if gf_fn is not None:
            gf_fn()
            gf_fn = None
        ffn_update(nc, k, c, 0, cyc)
        if last:
            ffn_resid_add(nc, k, g, c, 0, dst)
            ffn_resid_read(nc, k, g, c, 2, src)
        ffn_cast(nc, k, c, 3)
        ffn_wchain(nc, k, c, 2)
        ffn_update(nc, k, c, 1, cyc)
        if last:
            ffn_resid_add(nc, k, g, c, 1, dst)
            ffn_resid_read(nc, k, g, c, 3, src)
        ffn_wchain(nc, k, c, 3)
        ffn_update(nc, k, c, 2, cyc)
        if last:
            ffn_resid_add(nc, k, g, c, 2, dst)
        ffn_update(nc, k, c, 3, cyc)
        if last:
            ffn_resid_add(nc, k, g, c, 3, dst)


def attn_open(nc, k, alloc, st):
    c = {}

    def A(name, shape, dt):
        t = alloc(st, "at_" + name, shape, dt)
        b = Bf()
        c[name] = (t, b)
        return t
    for nm, shp, dt in [
        ("qkvw", [128, 8 * 3072], BF16), ("wow", [128, 8 * 1024], BF16),
        ("coskT", [128, 2048], F32), ("sskT", [128, 2048], F32),
        ("tri1T", [128, 128], F32), ("tri2T", [128, 128], F32),
        ("identT", [128, 128], BF16), ("permT", [128, 128], BF16),
        ("pmkT", [128, 128], BF16), ("pmvT", [16, 1024], BF16),
        ("ybf", [128, 4096], BF16),
        ("vtm", [128, 4 * 1024], BF16),
        ("ofm", [128, 8 * 512], BF16),
        ("burn", [128, 512], BF16), ("burn2", [128, 16], BF16),
        ("rows", [1, 2048], F32),
    ]:
        A(nm, shp, dt)
    c["xs"] = [(alloc(st, f"at_xs{i}", [128, 4096], F32R), Bf()) for i in range(2)]
    c["sqbf2"] = c["vtm"]
    c["qaug"] = alloc(st, "at_qaug", [128, 16 * 512], BF16)
    c["kaug"] = alloc(st, "at_kaug", [128, 16 * 528], BF16)
    c["qaug_d"] = [Bf() for _ in range(16)]
    c["qaug_s"] = [Bf() for _ in range(16)]
    c["kaug_d"] = [Bf() for _ in range(16)]
    c["kaug_p"] = [Bf() for _ in range(16)]
    c["attnT"] = [(alloc(st, f"at_at{i}", [128, 512], BF16), Bf()) for i in range(3)]
    c["atpm"] = [(alloc(st, f"at_atpm{i}", [16, 512], BF16), Bf()) for i in range(2)]
    c["qsb"] = [(alloc(st, f"at_qsb{i}", [128, 512], BF16), Bf()) for i in range(2)]
    c["t1f"] = [(alloc(st, f"at_t1f{i}", [128, 512], F32), Bf()) for i in range(2)]
    c["t2f"] = [(alloc(st, f"at_t2f{i}", [128, 512], F32), Bf()) for i in range(2)]
    for nm in ("mnegA", "dsegA", "dpmA", "shfA"):
        c[nm] = (alloc(st, "at_" + nm, [128, 64], F32), [Bf() for _ in range(4)])
    c["hlsA"] = (alloc(st, "at_hlsA", [128, 128], BF16), [Bf() for _ in range(4)])
    c["rh_r"] = [(alloc(st, "at_rh", [1, 512], F32R), Bf())]
    c["nu_r"] = [(alloc(st, "at_nu", [1, 512], F32R), Bf())]
    c["growR"] = (alloc(st, "at_grow", [1, 1024], F32R), Bf())
    return c


def attn_load(nc, k, g, c):
    k.cur_scope = "aload"
    qkvw, b_qkvw = c["qkvw"]
    for kc in range(8):
        k.dma("qgp", qkvw[:, 3072 * kc:3072 * (kc + 1)],
              g["wqkv"][128 * kc:128 * (kc + 1), :], writes=[b_qkvw])
    wow, b_wow = c["wow"]
    for kc in range(8):
        k.dma("qgp", wow[:, 1024 * kc:1024 * (kc + 1)],
              g["wout"][128 * kc:128 * (kc + 1), :], writes=[b_wow])
    for nm, src in [("coskT", g["cosk"]), ("sskT", g["ssk"]),
                    ("tri1T", g["tri1"]), ("tri2T", g["tri2"])]:
        t, bf = c[nm]
        k.dma("qsp", t[:, :], src, writes=[bf])
    for nm, src in [("identT", g["ident"]), ("permT", g["perm"])]:
        t, bf = c[nm]
        k.dma("qgp", t[:, :], src, writes=[bf])
    pmkT, b_pmk = c["pmkT"]
    for ch in range(8):
        k.dma("qgp", pmkT[:, 16 * ch:16 * (ch + 1)],
              g["pmk"][128 * ch:128 * (ch + 1), :], writes=[b_pmk])
    pmvT, b_pmv = c["pmvT"]
    k.dma("qgp", pmvT[:, :], g["pmv"], writes=[b_pmv])
    growR, b_grow = c["growR"]
    k.dma("qsp", growR[:, :], g["gvec"].bitcast(F32R), writes=[b_grow])
    kaug = c["kaug"]
    k.emit("dve", lambda v: v.memset(kaug[64:66, :], 1.0), writes=c["kaug_p"])
    for h in range(16):
        hc, lo = h // 2, (h % 2) * 64
        k.emit("act", lambda a, h=h, hc=hc, lo=lo: a.activation(
            out=kaug[0:64, 528 * h + 512:528 * h + 528],
            in_=pmkT[lo:lo + 64, 16 * hc:16 * (hc + 1)], func=AF.Copy),
               reads=[b_pmk], writes=[c["kaug_p"][h]])


def attn_seg(nc, k, g, c, s, dst=None):
    if dst is None:
        dst = g["xbuf"]
    pbt, pb = k.pbanks, k.pbanks_bf
    k.cur_scope = "aln"
    xs, b_xs = c["xs"][s % 2]
    src = g["xbuf"]
    for ch in range(NCH):
        k.dma("qsp", xs[:, 512 * ch:512 * (ch + 1)],
              src[128 * ch:128 * (ch + 1), 512 * s:512 * (s + 1)].bitcast(F32R),
              writes=[b_xs])
    ybf, b_ybf = c["ybf"]
    sqbf, b_sq = c["sqbf2"]
    k.emit("act", lambda a: a.activation(out=sqbf[:, :], in_=xs[:, :].bitcast(F32),
                                         func=AF.Square),
           reads=[b_xs], writes=[b_sq])

    def mkx(p):
        last = None
        for ch in range(NCH):
            last = p.matmul(pbt[4][0:1, :], k.onescolr_t[:, :],
                            xs[:, 512 * ch:512 * (ch + 1)],
                            start=(ch == 0), stop=(ch == NCH - 1))
        return last
    k.emit("pe", mkx, reads=[b_xs, k.onescolr_b], writes=[pb[4]])

    def mkq(p):
        last = None
        for ch in range(NCH):
            last = p.matmul(pbt[5][0:1, :], k.onescol_t[:, :],
                            sqbf[:, 512 * ch:512 * (ch + 1)],
                            start=(ch == 0), stop=(ch == NCH - 1))
        return last
    k.emit("pe", mkq, reads=[b_sq, k.onescol_b], writes=[pb[5]])
    ln_rows(k, c, (pbt[4], pb[4]), (pbt[5], pb[5]), 0, combine_g=True)
    growR, b_grow = c["growR"]
    rh_r, b_rh = c["rh_r"][0]
    nu_r, b_nu = c["nu_r"][0]
    for ch in range(NCH):
        bA, bB = (4, 5) if ch % 2 == 0 else (2, 3)
        k.emit("pe", lambda p, ch=ch, bA=bA: p.matmul(
            pbt[bA][:, :], growR[:, 128 * ch:128 * (ch + 1)], rh_r[:, :],
            start=True, stop=True),
               reads=[b_grow, b_rh], writes=[pb[bA]])
        k.emit("pe", lambda p, ch=ch, bB=bB: p.matmul(
            pbt[bB][:, :], growR[:, 128 * ch:128 * (ch + 1)], nu_r[:, :],
            start=True, stop=True),
               reads=[b_grow, b_nu], writes=[pb[bB]])
        sl = slice(512 * ch, 512 * (ch + 1))
        t1f, b_t1 = c["t1f"][ch % 2]
        k.emit("dve", lambda v, sl=sl, bA=bA, t1f=t1f: v.tensor_tensor(
            t1f[:, :], xs[:, sl].bitcast(F32), pbt[bA][:, :], op=ALU.mult),
               reads=[b_xs, pb[bA]], writes=[b_t1])
        k.emit("dve", lambda v, sl=sl, bB=bB, t1f=t1f: v.tensor_tensor(
            ybf[:, sl], t1f[:, :], pbt[bB][:, :], op=ALU.subtract),
               reads=[b_t1, pb[bB]], writes=[b_ybf])
    qkvw, b_qkvw = c["qkvw"]
    cosT, b_cos = c["coskT"]
    ssT, b_ss = c["sskT"]
    permT, b_perm = c["permT"]
    tsl = slice(512 * s, 512 * (s + 1))

    def qk_mm(which, qscale, base, hc):
        bi = hc % 2

        def mm(p, hc=hc, bi=bi, base=base):
            last = None
            for kc in range(8):
                last = p.matmul(pbt[bi][:, :],
                                qkvw[:, 3072 * kc + base + 128 * hc:
                                     3072 * kc + base + 128 * hc + 128],
                                ybf[:, 512 * kc:512 * (kc + 1)],
                                start=(kc == 0), stop=(kc == 7))
            return last
        k.emit("pe", mm, reads=[b_qkvw, b_ybf], writes=[pb[bi]])
        qsb, b_qsb = c["qsb"][hc % 2]
        k.emit("act", lambda a, bi=bi, qsb=qsb, qscale=qscale: a.activation(
            out=qsb[:, :], in_=pbt[bi][:, :], func=AF.Copy, scale=qscale),
               reads=[pb[bi]], writes=[b_qsb])

    def qk_tail(which, hc):
        bi = hc % 2
        qsb, b_qsb = c["qsb"][hc % 2]
        k.emit("pe", lambda p, bi=bi, qsb=qsb: p.matmul(
            pbt[2 + bi][:, :], permT[:, :], qsb[:, :], start=True, stop=True),
               reads=[b_perm, b_qsb], writes=[pb[2 + bi]])
        t1x, b_t1x = c["t1f"][hc % 2]
        t2x, b_t2x = c["t2f"][hc % 2]
        k.emit("dve", lambda v, qsb=qsb, t1x=t1x: v.tensor_tensor(
            t1x[:, :], qsb[:, :], cosT[:, tsl], op=ALU.mult),
               reads=[b_qsb, b_cos], writes=[b_t1x])
        k.emit("dve", lambda v, bi=bi, t2x=t2x: v.tensor_tensor(
            t2x[:, :], pbt[2 + bi][:, :], ssT[:, tsl], op=ALU.mult),
               reads=[pb[2 + bi], b_ss], writes=[b_t2x])
        for half in range(2):
            h = 2 * hc + half
            if which == "q":
                rdst = c["qaug"][0:64, 512 * h:512 * (h + 1)]
                wb = c["qaug_d"][h]
            else:
                rdst = c["kaug"][0:64, 528 * h:528 * h + 512]
                wb = c["kaug_d"][h]
            lo = 64 * half
            k.emit("dve", lambda v, rdst=rdst, lo=lo, t1x=t1x, t2x=t2x:
                   v.tensor_tensor(rdst, t1x[lo:lo + 64, :], t2x[lo:lo + 64, :],
                                   op=ALU.add),
                   reads=[b_t1x, b_t2x], writes=[wb])

    k.cur_scope = "aqkv"
    prev = None
    for which, qscale, base in (("q", 0.125, 0), ("k", 1.0, 1024)):
        for hc in range(8):
            qk_mm(which, qscale, base, hc)
            if prev is not None:
                qk_tail(*prev)
            prev = (which, hc)
    qk_tail(*prev)
    vtm, b_vtm = c["vtm"]
    for tc in range(4):
        for vh in range(2):
            bi = (tc * 2 + vh) % 2

            def mm(p, tc=tc, vh=vh, bi=bi):
                last = None
                for kc in range(8):
                    last = p.matmul(pbt[bi][:, :],
                                    ybf[:, 512 * kc + 128 * tc:512 * kc + 128 * tc + 128],
                                    qkvw[:, 3072 * kc + 2048 + 512 * vh:
                                         3072 * kc + 2048 + 512 * (vh + 1)],
                                    start=(kc == 0), stop=(kc == 7))
                return last
            k.emit("pe", mm, reads=[b_qkvw, b_ybf], writes=[pb[bi]])
            k.emit("act", lambda a, tc=tc, vh=vh, bi=bi: a.activation(
                out=vtm[:, 1024 * tc + 512 * vh:1024 * tc + 512 * (vh + 1)],
                in_=pbt[bi][:, :], func=AF.Copy),
                   reads=[pb[bi]], writes=[b_vtm])
    k.cur_scope = "aheads"
    attn_stats_stage(nc, k, c, 0)
    attn_stats_stage(nc, k, c, 1)
    for grp in range(4):
        attn_shift_stage(nc, k, c, grp)
        for h in range(4 * grp, 4 * grp + 4):
            attn_flip(nc, k, c, h)
        if grp + 2 < 4:
            attn_stats_stage(nc, k, c, grp + 2)
        for h in range(4 * grp, 4 * grp + 4):
            attn_pass2(nc, k, c, h)
    k.cur_scope = "awout"
    ofm, b_ofm = c["ofm"]
    wow, b_wow = c["wow"]
    for oc in range(8):
        bi = 2 + (oc % 2)

        def mm(p, oc=oc, bi=bi):
            last = None
            for kc in range(8):
                last = p.matmul(pbt[bi][:, :],
                                wow[:, 1024 * kc + 128 * oc:1024 * kc + 128 * oc + 128],
                                ofm[:, 512 * kc:512 * (kc + 1)],
                                start=(kc == 0), stop=(kc == 7))
            return last
        k.emit("pe", mm, reads=[b_wow, b_ofm], writes=[pb[bi]])
        sl = slice(512 * oc, 512 * (oc + 1))
        t1f, b_t1 = c["t1f"][oc % 2]
        k.emit("dve", lambda v, sl=sl, bi=bi, t1f=t1f: v.tensor_tensor(
            t1f[:, :], xs[:, sl].bitcast(F32), pbt[bi][:, :], op=ALU.add),
               reads=[pb[bi], b_xs], writes=[b_t1])
        k.dma("qgp", dst[128 * oc:128 * (oc + 1), 512 * s:512 * (s + 1)],
              t1f[:, :], reads=[b_t1])


def attn_stats_stage(nc, k, c, grp):
    pbt, pb = k.pbanks, k.pbanks_bf
    mnegA, b_mn = c["mnegA"][0], c["mnegA"][1][grp]
    dsegA, b_dsg = c["dsegA"][0], c["dsegA"][1][grp]
    dpmA, b_dpm = c["dpmA"][0], c["dpmA"][1][grp]
    burn, _ = c["burn"]
    burn2, _ = c["burn2"]
    qaug = c["qaug"]; kaug = c["kaug"]
    tri1T, b_tri1 = c["tri1T"]
    for h in range(4 * grp, 4 * grp + 4):
        b_qd = c["qaug_d"][h]; b_kd = c["kaug_d"][h]; b_kp = c["kaug_p"][h]
        for qc in range(4):
            col = 4 * h + qc
            N = 128 * (qc + 1)
            bi = 4 + (qc % 2)
            bp = 6 if qc % 2 == 0 else 0
            k.emit("pe", lambda p, h=h, qc=qc, N=N, bi=bi: p.matmul(
                pbt[bi][:, 0:N],
                qaug[0:64, 512 * h + 128 * qc:512 * h + 128 * (qc + 1)],
                kaug[0:64, 528 * h:528 * h + N], start=True, stop=True),
                   reads=[b_qd, b_kd], writes=[pb[bi]])
            k.emit("pe", lambda p, h=h, qc=qc, bp=bp: p.matmul(
                pbt[bp][:, 0:16],
                qaug[0:64, 512 * h + 128 * qc:512 * h + 128 * (qc + 1)],
                kaug[0:64, 528 * h + 512:528 * h + 528], start=True, stop=True),
                   reads=[b_qd, b_kp], writes=[pb[bp]])
            k.emit("dve", lambda v, qc=qc, bi=bi: v.tensor_tensor(
                pbt[bi][:, 128 * qc:128 * (qc + 1)], pbt[bi][:, 128 * qc:128 * (qc + 1)],
                tri1T[:, :], op=ALU.add),
                   reads=[b_tri1, pb[bi]], writes=[pb[bi]])
            k.emit("dve", lambda v, col=col, N=N, bi=bi: v.reduce_max(
                mnegA[:, col:col + 1], pbt[bi][:, 0:N], axis=AX.X, negate=True),
                   reads=[pb[bi]], writes=[b_mn])
            k.emit("act", lambda a, col=col, N=N, bi=bi: a.activation(
                out=burn[:, 0:N], in_=pbt[bi][:, 0:N], func=AF.Exp,
                bias=mnegA[:, col:col + 1], accum_out=dsegA[:, col:col + 1]),
                   reads=[pb[bi], b_mn], writes=[b_dsg])
            k.emit("act", lambda a, col=col, bp=bp: a.activation(
                out=burn2[:, :], in_=pbt[bp][:, 0:16], func=AF.Exp,
                bias=mnegA[:, col:col + 1], accum_out=dpmA[:, col:col + 1]),
                   reads=[pb[bp], b_mn], writes=[b_dpm])


def attn_shift_stage(nc, k, c, grp):
    sl = slice(16 * grp, 16 * grp + 16)
    mnegA, b_mn = c["mnegA"][0], c["mnegA"][1][grp]
    dsegA, b_dsg = c["dsegA"][0], c["dsegA"][1][grp]
    dpmA, b_dpm = c["dpmA"][0], c["dpmA"][1][grp]
    shfA, b_shf = c["shfA"][0], c["shfA"][1][grp]
    hlsA, b_hls = c["hlsA"][0], c["hlsA"][1][grp]
    k.emit("dve", lambda v: v.tensor_tensor(dsegA[:, sl], dsegA[:, sl], dpmA[:, sl],
                                            op=ALU.add),
           reads=[b_dpm, b_dsg], writes=[b_dsg])
    k.emit("act", lambda a: a.activation(out=shfA[:, sl], in_=dsegA[:, sl], func=AF.Ln),
           reads=[b_dsg], writes=[b_shf])
    k.emit("dve", lambda v: v.tensor_tensor(shfA[:, sl], mnegA[:, sl], shfA[:, sl],
                                            op=ALU.subtract),
           reads=[b_mn, b_shf], writes=[b_shf])
    hv = hlsA[:, 32 * grp:32 * grp + 32].rearrange("p (a b) -> p a b", b=2)
    sv = shfA[:, sl].unsqueeze(2)
    k.emit("dve", lambda v: v.tensor_copy(hv[:, :, 0:1], sv),
           reads=[b_shf], writes=[b_hls])
    k.emit("dve", lambda v: v.tensor_tensor(shfA[:, sl].unsqueeze(2), sv, hv[:, :, 0:1],
                                            op=ALU.subtract),
           reads=[b_shf, b_hls], writes=[b_shf])
    k.emit("dve", lambda v: v.tensor_copy(hv[:, :, 1:2], shfA[:, sl].unsqueeze(2)),
           reads=[b_shf], writes=[b_hls])


def attn_flip(nc, k, c, h):
    hlsA, b_hls = c["hlsA"][0], c["hlsA"][1][h // 4]
    identT, b_id = c["identT"]
    qaug = c["qaug"]

    def flips(p):
        last = None
        for qc in range(4):
            last = p.transpose(k.pbF_t[:, 128 * qc:128 * (qc + 1)],
                               hlsA[:, 8 * h + 2 * qc:8 * h + 2 * qc + 2],
                               identT[:, :])
        return last
    k.emit("pe", flips, reads=[b_hls, b_id], writes=[k.pbF_b])
    k.emit("act", lambda a: a.activation(out=qaug[64:66, 512 * h:512 * (h + 1)],
                                         in_=k.pbF_t[:, :], func=AF.Copy),
           reads=[k.pbF_b], writes=[c["qaug_s"][h]])


def attn_pass2(nc, k, c, h):
    pbt, pb = k.pbanks, k.pbanks_bf
    qaug = c["qaug"]; kaug = c["kaug"]
    b_qd = c["qaug_d"][h]; b_qs = c["qaug_s"][h]
    b_kd = c["kaug_d"][h]; b_kp = c["kaug_p"][h]
    tri2T, b_tri2 = c["tri2T"]
    vtm, b_vtm = c["vtm"]
    pmvT, b_pmv = c["pmvT"]
    obank = h % 2
    atpm, b_atpm = c["atpm"][h % 2]

    def emit_sT(jc):
        if jc == "pm":
            k.emit("pe", lambda p: p.matmul(pbt[4][0:16, :],
                                            kaug[0:66, 528 * h + 512:528 * h + 528],
                                            qaug[0:66, 512 * h:512 * (h + 1)],
                                            start=True, stop=True),
                   reads=[b_kd, b_kp, b_qd, b_qs], writes=[pb[4]])
            k.emit("act", lambda a: a.activation(out=atpm[:, :], in_=pbt[4][0:16, :],
                                                 func=AF.Exp),
                   reads=[pb[4]], writes=[b_atpm])
        else:
            N = 512 - 128 * jc
            bi = 2 + (jc % 2)
            at, b_at = c["attnT"][jc % 3]
            k.emit("pe", lambda p, jc=jc, N=N, bi=bi: p.matmul(
                pbt[bi][:, 0:N],
                kaug[0:66, 528 * h + 128 * jc:528 * h + 128 * (jc + 1)],
                qaug[0:66, 512 * h + 128 * jc:512 * h + 512],
                start=True, stop=True),
                   reads=[b_kd, b_kp, b_qd, b_qs], writes=[pb[bi]])
            k.emit("dve", lambda v, bi=bi: v.tensor_tensor(
                pbt[bi][:, 0:128], pbt[bi][:, 0:128], tri2T[:, :], op=ALU.add),
                   reads=[b_tri2, pb[bi]], writes=[pb[bi]])
            k.emit("act", lambda a, N=N, bi=bi, at=at: a.activation(
                out=at[:, 0:N], in_=pbt[bi][:, 0:N], func=AF.Exp),
                   reads=[pb[bi]], writes=[b_at])

    def emit_av(jc):
        if jc == "pm":
            k.emit("pe", lambda p: p.matmul(pbt[obank][0:64, :],
                                            pmvT[:, 64 * h:64 * (h + 1)],
                                            atpm[:, :], start=True, stop=False),
                   reads=[b_pmv, b_atpm], writes=[pb[obank]])
        else:
            N = 512 - 128 * jc
            at, b_at = c["attnT"][jc % 3]
            k.emit("pe", lambda p, jc=jc, N=N, at=at: p.matmul(
                pbt[obank][0:64, 128 * jc:512],
                vtm[:, 1024 * jc + 64 * h:1024 * jc + 64 * (h + 1)],
                at[:, 0:N], start=False, stop=(jc == 3)),
                   reads=[b_vtm, b_at], writes=[pb[obank]])

    order = ["pm", 0, 1, 2, 3]
    prev = None
    for jc in order:
        emit_sT(jc)
        if prev is not None:
            emit_av(prev)
        prev = jc
    emit_av(prev)
    ofm, b_ofm = c["ofm"]
    hc, half = h // 2, h % 2
    k.emit("act", lambda a: a.activation(
        out=ofm[64 * half:64 * (half + 1), 512 * hc:512 * (hc + 1)],
        in_=pbt[obank][0:64, :], func=AF.Copy),
           reads=[pb[obank]], writes=[b_ofm])


STAGE = "full"
DUMP = None


def build_program():
    nc = bass.Bass("TRN2", target_bir_lowering=False)
    g = {}
    g["x"] = nc.dram_tensor("x", [D, T], F32, kind="ExternalInput").ap()
    g["y"] = nc.dram_tensor("y", [D, T], F32, kind="ExternalOutput").ap()
    g["xbuf"] = nc.dram_tensor("xbuf", [D, T], F32).ap()
    for nm, shp in [
        ("wqkv", [D, 3 * D]), ("wout", [D, D]),
        ("w1", [D, D]), ("w2", [D, 512]), ("w3", [512, 256]), ("w4", [256, 1]),
        ("gw1", [D, 512]), ("gw2", [512, 256]), ("gw3", [256, 1]),
        ("b1", [128, 8]), ("b2", [128, 4]), ("b3", [128, 2]), ("b4", [1, 1]),
        ("gb1", [128, 4]), ("gb2", [128, 2]), ("gb3", [1, 1]),
        ("ar", [1, 1]), ("gvec", [1, D]),
        ("pmk", [D, PMEM]), ("pmv", [PMEM, D]),
        ("cosk", [128, T]), ("ssk", [128, T]),
        ("tri1", [128, 128]), ("tri2", [128, 128]),
        ("ident", [128, 128]), ("perm", [128, 128]),
        ("onesrow", [1, 128]), ("ccmask", [128, 4]),
    ]:
        g[nm] = nc.dram_tensor(nm, shp, F32, kind="ExternalInput").ap()
    ccbufs = [(nc.dram_tensor("cc1i", [4, 1024], F32), nc.dram_tensor("cc1o", [4, 1024], F32)),
              (nc.dram_tensor("cc2i", [4, 1024], F32), nc.dram_tensor("cc2o", [4, 1024], F32))]

    stack = contextlib.ExitStack()
    sems = {}
    for snm in ENGS + QS + ("cc",):
        sems[snm] = stack.enter_context(nc.semaphore("sem_" + snm))

    def alloc(st, name, shape, dt):
        return st.enter_context(nc.sbuf_tensor(name, shape, dt))

    onesr = alloc(stack, "g_onesr", [1, 128], F32R)
    onescol = alloc(stack, "g_onescol", [128, 1], BF16)
    onescolr = alloc(stack, "g_onescolr", [128, 1], F32R)
    ccm = alloc(stack, "g_ccm", [128, 4], F32)
    epsA = alloc(stack, "g_epsA", [1, 1], F32)
    epsB = alloc(stack, "g_epsB", [1, 1], F32)
    pbanks = [stack.enter_context(nc.psum_tensor(f"pb{i}", [128, 512], F32))
              for i in range(7)]
    pbF = stack.enter_context(nc.psum_tensor("pbF", [2, 512], BF16))

    with nc.Block() as block:
        k = K(nc, sems)
        k.pbanks = pbanks
        k.pbanks_bf = [Bf() for _ in range(7)]
        k.pbF_t = pbF
        k.pbF_b = Bf()
        k.onesr_t = onesr
        k.onesr_b = Bf()
        k.onescol_t = onescol
        k.onescol_b = Bf()
        k.onescolr_t = onescolr
        k.onescolr_b = Bf()
        k.ccm_t = ccm
        k.ccm_b = Bf()
        k.epsA_t = epsA
        k.epsA_b = Bf()
        k.epsB_t = epsB
        k.epsB_b = Bf()
        k.rot = [0]
        k.dma("qsp", onesr[:, :], g["onesrow"].bitcast(F32R), writes=[k.onesr_b])
        k.dma("qsp", ccm[:, :], g["ccmask"], writes=[k.ccm_b])
        k.emit("dve", lambda v: v.memset(onescol[:, :], 1.0), writes=[k.onescol_b])
        k.emit("dve", lambda v: v.tensor_copy(onescolr[:, :], onescol[:, :]),
               reads=[k.onescol_b], writes=[k.onescolr_b])
        k.emit("dve", lambda v: v.memset(epsA[:, :], EPS1), writes=[k.epsA_b])
        k.emit("dve", lambda v: v.memset(epsB[:, :], EPS1 * EPS2), writes=[k.epsB_b])
        st1 = contextlib.ExitStack()
        c1 = ffn_phase_open(nc, k, alloc, st1, "f1_")
        ffn_load(nc, k, g, c1)
        early = False
        if STAGE == "ld":
            k.dma("qsp", g["y"][0:128, 0:8], c1["b1c"][0][:, :], reads=[c1["b1c"][1]])
            early = True
        if not early:
            ffn_prepass(nc, k, g, c1, src=g["x"])
            ffn_load_gc(nc, k, g, c1)
            if STAGE == "prepass":
                for t in range(NT):
                    st_t, b_st = c1["states"][t]
                    for ch in range(NCH):
                        k.dma("qsp",
                              g["y"][128 * ch:128 * (ch + 1), 512 * t:512 * (t + 1)],
                              st_t[:, 512 * ch:512 * (ch + 1)], reads=[b_st])
                early = True
        if early:
            k.barrier()
            k.flush(block)
        else:
            ffn_cc_stage(nc, k, g, c1, ccbufs[0])
            k.flush(block)

            @block.gpsimd
            def _(gp):
                cc_run(nc, k, c1, ccbufs[0])

            if STAGE == "cc":
                k.dma("qsp", g["y"][0:128, 0:32], c1["gcr"][0][:, :],
                      reads=[c1["gcr"][1]])
                k.barrier()
                k.flush(block)
                early = True

        if early:
            STAGE2 = "done"
        else:
            STAGE2 = STAGE
        if STAGE2 != "done":
            gf1 = lambda: ffn_gf(nc, k, g, c1)
            if STAGE == "wc":
                gf1()
                dump = "h1" if DUMP is None else DUMP
                for t in range(NT):
                    ffn_wchain(nc, k, c1, t)
                    xio, b_xio = c1["xio"][t % 2]
                    if dump in ("h1", "h2", "h3"):
                        hsrc, b_h = c1[dump][t % 2]
                        k.emit("dve", lambda v, hsrc=hsrc, xio=xio: v.tensor_copy(
                            xio[:, 0:hsrc.shape[1]], hsrc[:, :]),
                               reads=[b_h], writes=[b_xio])
                        for ch in range(NCH):
                            k.dma("qsp",
                                  g["y"][128 * ch:128 * (ch + 1),
                                         512 * t:512 * (t + 1)],
                                  xio[:, 512 * ch:512 * (ch + 1)], reads=[b_xio])
                    elif dump == "tf":
                        tfrow, b_tf = c1["tfrow"][t % 2]
                        k.dma("qsp", g["y"][0:1, 512 * t:512 * (t + 1)],
                              tfrow[:, :], reads=[b_tf])
                    elif dump == "gf":
                        k.dma("qsp", g["y"][0:1, 0:1], c1["gfr"][0][:, :],
                              reads=[c1["gfr"][1]])
                        k.dma("qsp", g["y"][0:1, 1:2], c1["c0r"][0][:, :],
                              reads=[c1["c0r"][1]])
                        k.dma("qsp", g["y"][0:1, 2:3], c1["c1r"][0][:, :],
                              reads=[c1["c1r"][1]])
                STAGE2 = "ffn1"
            elif STAGE == "gfd":
                ffn_main(nc, k, g, c1, src=g["x"], dst=g["y"], cycles=(1,),
                         resid=False, gf_fn=gf1)
                k.dma("qsp", g["y"][0:1, 0:1], c1["gfr"][0][:, :],
                      reads=[c1["gfr"][1]])
                k.dma("qsp", g["y"][0:1, 1:2], c1["c0r"][0][:, :],
                      reads=[c1["c0r"][1]])
                k.dma("qsp", g["y"][0:1, 2:3], c1["c1r"][0][:, :],
                      reads=[c1["c1r"][1]])
                k.dma("qgp", g["y"][0:1, 3:8], c1["gcbf"][0][0:1, 0:5],
                      reads=[c1["gcbf"][1]])
                STAGE2 = "ffn1"
            elif STAGE in ("c1", "c2"):
                ffn_main(nc, k, g, c1, src=g["x"], dst=g["y"],
                         cycles=(1,) if STAGE == "c1" else (1, 2), resid=False,
                         gf_fn=gf1)
                for t in range(NT):
                    st_t, b_st = c1["states"][t]
                    for ch in range(NCH):
                        k.dma("qsp",
                              g["y"][128 * ch:128 * (ch + 1), 512 * t:512 * (t + 1)],
                              st_t[:, 512 * ch:512 * (ch + 1)], reads=[b_st])
                STAGE2 = "ffn1"
            else:
                ffn_main(nc, k, g, c1, src=g["x"],
                         dst=(g["y"] if STAGE == "ffn1" else g["xbuf"]),
                         gf_fn=gf1)
            k.barrier()
        st1.close()
        if STAGE2 not in ("done", "ffn1"):
            st2 = contextlib.ExitStack()
            c2 = attn_open(nc, k, alloc, st2)
            attn_load(nc, k, g, c2)
            for s in range(NT):
                attn_seg(nc, k, g, c2, s, dst=(g["y"] if STAGE == "attn"
                                               else g["xbuf"]))
            k.barrier()
            st2.close()
        if STAGE == "full":
            st3 = contextlib.ExitStack()
            c3 = ffn_phase_open(nc, k, alloc, st3, "f2_")
            ffn_load(nc, k, g, c3)
            ffn_prepass(nc, k, g, c3, src=g["xbuf"])
            ffn_load_gc(nc, k, g, c3)
            ffn_cc_stage(nc, k, g, c3, ccbufs[1])
            k.flush(block)

            @block.gpsimd
            def _(gp):
                cc_run(nc, k, c3, ccbufs[1])

            ffn_main(nc, k, g, c3, src=g["xbuf"], dst=g["y"],
                     gf_fn=lambda: ffn_gf(nc, k, g, c3))
            k.barrier()
            st3.close()
        k.flush(block)

    nc._keepalive = stack
    nc._kemit = k
    return nc


def _host_constants(half):
    f32 = np.float32
    inv = (1.0 / (10000.0 ** (np.arange(0, DH, 2, dtype=f32) / f32(DH)))).astype(f32)
    pos = np.arange(T * half, T * (half + 1), dtype=f32)
    th = (pos[None, :] * inv[:, None]).astype(f32)
    cos = np.cos(th).astype(f32)
    sin = np.sin(th).astype(f32)
    cos64 = np.repeat(cos, 2, axis=0)
    ss64 = np.repeat(sin, 2, axis=0)
    ss64[0::2] *= -1.0
    cos128 = np.concatenate([cos64, cos64], axis=0)
    ss128 = np.concatenate([ss64, ss64], axis=0)
    return np.ascontiguousarray(cos128), np.ascontiguousarray(ss128)


_PROG_CACHE = {}


def kernel(**inputs):
    if "nc" not in _PROG_CACHE:
        _PROG_CACHE["nc"] = build_program()
    nc = _PROG_CACHE["nc"]

    f32 = np.float32
    xin = np.asarray(inputs["L0_attn_seq"], dtype=f32)
    wqkv = np.ascontiguousarray(np.asarray(inputs["L0_attn_w_qkv"], f32).T)
    wout = np.ascontiguousarray(np.asarray(inputs["L0_attn_w_out"], f32).T)
    gvec = np.ascontiguousarray(np.asarray(inputs["L0_attn_g"], f32).reshape(1, D))
    pm = np.asarray(inputs["L0_attn_pm"], f32)
    pmk = np.ascontiguousarray(pm[0].transpose(0, 2, 1).reshape(D, PMEM))
    pmv = np.ascontiguousarray(pm[1].transpose(1, 0, 2).reshape(PMEM, D))
    w1 = np.ascontiguousarray(np.asarray(inputs["L0_ffn_cn_w1"], f32))
    w2 = np.ascontiguousarray(np.asarray(inputs["L0_ffn_cn_w2"], f32))
    w3 = np.ascontiguousarray(np.asarray(inputs["L0_ffn_cn_w3"], f32))
    w4 = np.ascontiguousarray(np.asarray(inputs["L0_ffn_cn_w4"], f32))
    gw1 = np.ascontiguousarray(np.asarray(inputs["L0_ffn_gc_w1"], f32))
    gw2 = np.ascontiguousarray(np.asarray(inputs["L0_ffn_gc_w2"], f32))
    gw3 = np.ascontiguousarray(np.asarray(inputs["L0_ffn_gc_w3"], f32))
    b1 = np.ascontiguousarray(np.asarray(inputs["L0_ffn_cn_b1"], f32).reshape(8, 128).T)
    b2 = np.ascontiguousarray(np.asarray(inputs["L0_ffn_cn_b2"], f32).reshape(4, 128).T)
    b3 = np.ascontiguousarray(np.asarray(inputs["L0_ffn_cn_b3"], f32).reshape(2, 128).T)
    b4 = np.asarray(inputs["L0_ffn_cn_b4"], f32).reshape(1, 1)
    gb1 = np.ascontiguousarray(np.asarray(inputs["L0_ffn_gc_b1"], f32).reshape(4, 128).T)
    gb2 = np.ascontiguousarray(np.asarray(inputs["L0_ffn_gc_b2"], f32).reshape(2, 128).T)
    gb3 = np.asarray(inputs["L0_ffn_gc_b3"], f32).reshape(1, 1)
    ar = np.asarray(inputs["L0_ffn_adaptive_range"], f32).reshape(1, 1)

    tri1 = np.triu(np.full((128, 128), NEG, dtype=f32), k=1)
    tri2 = np.ascontiguousarray(tri1.T)
    ident = np.eye(128, dtype=f32)
    perm = np.zeros((128, 128), dtype=f32)
    idx = np.arange(128)
    perm[idx, idx ^ 1] = 1.0
    onesrow = np.ones((1, 128), dtype=f32)

    in_maps = []
    for core in range(8):
        b, half = core // 2, core % 2
        cosk, ssk = _host_constants(half)
        ccmask = np.zeros((128, 4), dtype=f32)
        ccmask[:, b] = 1.0
        xcore = np.ascontiguousarray(xin[b, T * half:T * (half + 1), :].T)
        in_maps.append({
            "x": xcore, "wqkv": wqkv, "wout": wout, "gvec": gvec,
            "pmk": pmk, "pmv": pmv,
            "w1": w1, "w2": w2, "w3": w3, "w4": w4,
            "gw1": gw1, "gw2": gw2, "gw3": gw3,
            "b1": b1, "b2": b2, "b3": b3, "b4": b4,
            "gb1": gb1, "gb2": gb2, "gb3": gb3, "ar": ar,
            "cosk": cosk, "ssk": ssk,
            "tri1": tri1, "tri2": tri2, "ident": ident, "perm": perm,
            "onesrow": onesrow, "ccmask": ccmask,
        })

    res = run_bass_kernel_spmd(nc, in_maps, core_ids=list(range(8)))
    out = np.empty((B, S, D), dtype=np.float32)
    for core in range(8):
        b, half = core // 2, core % 2
        out[b, T * half:T * (half + 1), :] = res.results[core]["y"].T
    return out
